# version 13
# baseline (speedup 1.0000x reference)
import math
from contextlib import ExitStack

import numpy as np
import ml_dtypes

import concourse.bass as bass
import concourse.mybir as mybir
from concourse.bass_utils import run_bass_kernel_spmd

F32 = mybir.dt.float32
BF16 = mybir.dt.bfloat16
I32 = mybir.dt.int32
AF = mybir.ActivationFunctionType
ALU = mybir.AluOpType

ENGINES = ("tensor", "vector", "scalar", "gpsimd", "sync")
DMA_SEM = {"sync": "dS", "gpsimd": "dG", "scalar": "dA"}
ENG_SEM = {"tensor": "tS", "vector": "vS", "scalar": "aS", "gpsimd": "gS",
           "sync": "sS"}


class Emitter:
    def __init__(self, nc):
        self.nc = nc
        self.q = {e: [] for e in ENGINES}
        self.sems = {}
        self.counts = {}
        self.last_wait = {}

    def sem(self, name):
        if name not in self.sems:
            self.sems[name] = self.nc.alloc_semaphore(name)
            self.counts[name] = 0
        return self.sems[name]

    def wait(self, eng, ms):
        if ms is None:
            return
        semname, val = ms
        if val <= 0:
            return
        key = (eng, semname)
        if self.last_wait.get(key, 0) >= val:
            return
        self.last_wait[key] = val
        s = self.sem(semname)
        self.q[eng].append(lambda E, s=s, v=val: E.wait_ge(s, v))

    def emit(self, eng, fn, inc):
        semname, k = inc
        s = self.sem(semname)
        self.counts[semname] += k

        def wrapped(E, fn=fn, s=s, k=k):
            fn(E).then_inc(s, k)

        self.q[eng].append(wrapped)
        return (semname, self.counts[semname])

    def run(self, block):
        for e in ENGINES:
            fns = self.q[e]

            def body(E, fns=fns):
                for f in fns:
                    f(E)

            getattr(block, e)(body)


N_DMA_SEMS = 8


class SB:

    def __init__(self, em):
        self.em = em
        self.state = {}
        self.rr = {}

    def _st(self, r):
        if r not in self.state:
            self.state[r] = {"w": None, "r": []}
        return self.state[r]

    def _wait(self, eng, ms):
        if ms is None:
            return
        if eng == "tensor" and ms[0] == "tS":
            return
        self.em.wait(eng, ms)

    def op(self, eng, fn, reads=(), writes=(), dma=False):
        for r in reads:
            self._wait(eng, self._st(r)["w"])
        for r in writes:
            st = self._st(r)
            self._wait(eng, st["w"])
            for ms in st["r"]:
                self._wait(eng, ms)
        if dma:
            q = DMA_SEM[eng]
            k = self.rr.get(q, 0)
            self.rr[q] = (k + 1) % N_DMA_SEMS
            sem = f"{q}{k}"
            prev = self.em.counts.get(sem, 0)
            if prev:
                self.em.wait(eng, (sem, prev))
            ms = self.em.emit(eng, fn, inc=(sem, 16))
        else:
            ms = self.em.emit(eng, fn, inc=(ENG_SEM[eng], 1))
        for r in writes:
            st = self._st(r)
            st["w"] = ms
            st["r"] = []
        for r in reads:
            self._st(r)["r"].append(ms)
        return ms


class Cfg:
    def __init__(self, dm=2048, s_full=8192, s_own=2048, inter=5504):
        self.dm = dm
        self.nt = dm // 128
        self.hp = self.nt // 2
        self.s_full = s_full
        self.s_own = s_own
        self.inter = inter
        self.it = inter // 128
        self.cw = 512 if s_own % 512 == 0 else 128
        self.n_ch = s_full // self.cw
        self.n_oc = s_own // self.cw
        self.n_kc = s_full // 128
        self.scale = 1.0 / math.sqrt(128.0)
        self.eps = 1e-5
        self.ngu = self.it * self.nt
        self.nd = self.nt * self.it
        self.dec_slab = 8


def pages(name, lo, hi, grain=2048):
    return [f"{name}_p{p}" for p in range(lo // grain,
                                          (hi + grain - 1) // grain)]


def build_program(c, lut_vals):
    nc = bass.Bass("TRN2", target_bir_lowering=False)
    cw = c.cw
    nt, hp, it = c.nt, c.hp, c.it

    xT = nc.dram_tensor("xT", [nt, 128, c.s_full], F32,
                        kind="ExternalInput").ap()
    xownT = nc.dram_tensor("xownT", [nt, 128, c.s_own], F32,
                           kind="ExternalInput").ap()
    wqT = nc.dram_tensor("wqT", [c.dm, c.dm], F32, kind="ExternalInput").ap()
    wkT = nc.dram_tensor("wkT", [c.dm, c.dm], F32, kind="ExternalInput").ap()
    wvT = nc.dram_tensor("wvT", [c.dm, c.dm], F32, kind="ExternalInput").ap()
    woT = nc.dram_tensor("woT", [c.dm, c.dm], F32, kind="ExternalInput").ap()
    gate_in = nc.dram_tensor("gate", [1, nt], F32, kind="ExternalInput").ap()
    cosq = nc.dram_tensor("cosq", [hp, 128, c.s_own], F32,
                          kind="ExternalInput").ap()
    sinq = nc.dram_tensor("sinq", [hp, 128, c.s_own], F32,
                          kind="ExternalInput").ap()
    cosk = nc.dram_tensor("cosk", [hp, 128, c.s_full], F32,
                          kind="ExternalInput").ap()
    sink = nc.dram_tensor("sink", [hp, 128, c.s_full], F32,
                          kind="ExternalInput").ap()
    gidx = nc.dram_tensor("gidx", [128, c.ngu, 128], I32,
                          kind="ExternalInput").ap()
    uidx = nc.dram_tensor("uidx", [128, c.ngu, 128], I32,
                          kind="ExternalInput").ap()
    didx = nc.dram_tensor("didx", [128, c.nd, 128], I32,
                          kind="ExternalInput").ap()
    out = nc.dram_tensor("outT", [nt, 128, c.s_own], F32,
                         kind="ExternalOutput").ap()

    wq_bf = nc.dram_tensor("wq_bf", [128, nt, c.dm], BF16).ap()
    wk_bf = nc.dram_tensor("wk_bf", [128, nt, c.dm], BF16).ap()
    wo_bf = nc.dram_tensor("wo_bf", [128, nt, c.dm], BF16).ap()
    wv_bf = nc.dram_tensor("wv_bf", [128, nt, c.dm], BF16).ap()
    xn_hb = nc.dram_tensor("xn_hb", [nt, 128, c.s_full], BF16).ap()
    kT_h = nc.dram_tensor("kT_h", [nt, 128, c.s_full], BF16).ap()
    qT_h = nc.dram_tensor("qT_h", [nt, 128, c.s_own], BF16).ap()
    v_hb = nc.dram_tensor("v_hb", [c.s_full, c.dm], BF16).ap()
    a_hb = nc.dram_tensor("a_hb", [nt, 128, c.s_own], BF16).ap()
    r2_hb = nc.dram_tensor("r2_hb", [nt, 128, c.s_own], F32).ap()
    f_hb = nc.dram_tensor("f_hb", [nt, 128, c.s_own], BF16).ap()
    wg_hb = nc.dram_tensor("wg_hb", [128, c.ngu, 128], BF16).ap()
    wu_hb = nc.dram_tensor("wu_hb", [128, c.ngu, 128], BF16).ap()
    wd_hb = nc.dram_tensor("wd_hb", [128, c.nd, 128], BF16).ap()

    ident_d = nc.inline_tensor(np.eye(128, dtype=ml_dtypes.bfloat16),
                               "ident").ap()

    stack = ExitStack()

    def sbuf(name, shape, dt):
        return stack.enter_context(nc.sbuf_tensor(name, shape, dt))

    with stack:
        big0_w = max(2 * nt * cw + (cw // 128) * c.dm,
                     3 * c.s_full,
                     hp * c.n_oc * cw,
                     it * cw + 2 * cw)
        big0 = sbuf("big0", [128, big0_w], BF16)
        big1 = sbuf("big1", [128, max(nt * c.s_own, nt * c.dm)], BF16)
        wp = sbuf("wp", [128, 8 * cw], BF16)
        mid_w = max(4 * cw, 2 * c.s_own + 3 * 2 * cw)
        mid = sbuf("mid", [128, mid_w], BF16)
        stat_ld = sbuf("stat_ld", [128, cw], F32)
        stat_sq = sbuf("stat_sq", [128, cw], BF16)
        tf1 = sbuf("tf1", [128, cw], F32)
        tf2 = sbuf("tf2", [128, cw], F32)
        tf3 = sbuf("tf3", [128, cw], F32)
        tf4 = sbuf("tf4", [128, cw], F32)
        wbuf0 = sbuf("wbuf0", [128, c.dm], BF16)
        wbuf1 = sbuf("wbuf1", [128, c.dm], BF16)
        memory = sbuf("memory", [128, nt * 129], F32)
        memb = sbuf("memb", [128, 132], BF16)
        dec_bs = [sbuf("dec_b0", [128, 128 * c.dec_slab], BF16),
                  sbuf("dec_b1", [128, 128 * c.dec_slab], BF16)]
        dec_ts = [sbuf("dec_t0", [128, 128 * c.dec_slab], BF16),
                  sbuf("dec_t1", [128, 128 * c.dec_slab], BF16)]
        dec_as = [sbuf("dec_a0", [128, 128 * c.dec_slab], BF16),
                  sbuf("dec_a1", [128, 128 * c.dec_slab], BF16)]
        R0 = sbuf("R0", [1, c.s_own], F32)
        R1 = sbuf("R1", [1, c.s_own], F32)
        R2 = sbuf("R2", [1, c.s_own], F32)
        R0b = sbuf("R0b", [1, cw], F32)
        R1b = sbuf("R1b", [1, cw], F32)
        R2b = sbuf("R2b", [1, cw], F32)
        tf5 = sbuf("tf5", [128, cw], F32)
        mq2 = sbuf("mq2", [1, nt], F32)
        mk2 = sbuf("mk2", [1, nt], F32)
        g_row = sbuf("g_row", [1, nt], F32)
        og_row = sbuf("og_row", [1, nt], F32)
        gcol = sbuf("gcol", [1, 128], F32)
        ogcol = sbuf("ogcol", [1, 128], F32)
        ones_row = sbuf("ones_row", [1, 128], F32)
        ones_col = sbuf("ones_col", [128, 1], BF16)
        biasC = sbuf("biasC", [128, 1], F32)
        sc1 = sbuf("sc1", [1, 8], F32)
        ident_s = sbuf("ident_s", [128, 128], BF16)

        ps = stack.enter_context(nc.psum_tensor("ps", [128, 4096], F32))
        psb = ps[:, 3584:3584 + cw // 2].bitcast(BF16)

        def Q4(i, w=None):
            w = cw if w is None else w
            return ps[:, i * 512:i * 512 + w]

        def QPAIR(par):
            base = par * 1024
            return ps[:, base:base + 1024].rearrange(
                "p (b x) -> p b x", x=512)[:, :, 0:cw]

        DEN = ps[0:1, 3584:3584 + cw]
        PVPS = ps[:, 3072:3072 + cw]
        MEMPS = ps[:, 3072:3201]

        em = Emitter(nc)
        sb = SB(em)
        OP = sb.op

        def dma(eng, dst, src, reads=(), writes=()):
            return OP(eng, lambda E, d=dst, s=src: E.dma_start(out=d, in_=s),
                      reads=reads, writes=writes, dma=True)

        QA = wp[:, 0:cw]
        QB = wp[:, cw:2 * cw]
        QO = wp[:, 2 * cw:3 * cw]
        QO2 = wp[:, 3 * cw:4 * cw]
        K3A = wp[:, 4 * cw:5 * cw]
        K2A = wp[:, 5 * cw:6 * cw]
        K2B = wp[:, 6 * cw:7 * cw]
        K3B = wp[:, 7 * cw:8 * cw]
        sqb = tf2.bitcast(BF16)[:, 0:cw]
        tf1b = tf1.bitcast(BF16)[:, 0:cw]
        ldb_v = stat_ld.bitcast(BF16)[:, 0:cw]
        midf = mid.bitcast(F32)
        cos_s = [midf[:, 1024 * p:1024 * p + cw] for p in range(3)]
        sin_s = [midf[:, 1024 * p + cw:1024 * p + 2 * cw] for p in range(3)]
        m_cos_s = [pages("mid", 2048 * p, 2048 * p + 1024, 1024)
                   for p in range(3)]
        m_sin_s = [pages("mid", 2048 * p + 1024, 2048 * p + 2048, 1024)
                   for p in range(3)]
        cosb, sinb = cos_s[0], sin_s[0]
        m_cosb, m_sinb = m_cos_s[0], m_sin_s[0]
        SQK = mid[:, 6144:6144 + cw]
        m_SQK = pages("mid", 6144, 6144 + cw, 1024)

        OP("vector", lambda V: V.memset(ones_col[:, :], 1.0), writes=["ones"])
        OP("vector", lambda V: V.memset(ones_row[:, :], 1.0), writes=["onesr"])
        OP("vector", lambda V: V.memset(memory[:, :], 0.0), writes=["memory"])
        OP("vector", lambda V: V.memset(mq2[:, :], 0.0), writes=["mq2"])
        OP("vector", lambda V: V.memset(mk2[:, :], 0.0), writes=["mk2"])
        dma("sync", ident_s[:, :], ident_d[:, :], writes=["ident"])
        dma("sync", g_row[:, :], gate_in[:, :], writes=["grow"])
        OP("scalar", lambda A: A.activation(og_row[:, :], g_row[:, :], AF.Exp,
                                            scale=-1.0),
           reads=["grow"], writes=["ogrow"])
        OP("vector", lambda V: V.tensor_scalar_add(og_row[:, :], og_row[:, :],
                                                   1.0),
           reads=["ogrow"], writes=["ogrow"])
        OP("vector", lambda V: V.reciprocal(g_row[:, :], og_row[:, :]),
           reads=["ogrow"], writes=["grow"])
        OP("vector", lambda V: V.tensor_scalar(og_row[:, :], g_row[:, :],
                                               -1.0, 1.0, ALU.mult, ALU.add),
           reads=["grow"], writes=["ogrow"])
        for (wsrc, wdst, res) in ((wqT, wq_bf, "wqb"), (wkT, wk_bf, "wkb"),
                                  (woT, wo_bf, "wob")):
            s4 = wsrc.rearrange("(hc p) (m f) -> p m hc f", p=128, f=128)
            d4 = wdst.rearrange("p m (hc f) -> p m hc f", f=128)
            for m in range(nt):
                dma("gpsimd", d4[:, m], s4[:, m], writes=[res])
        sv = wvT.rearrange("(hc p) f -> p hc f", p=128)
        for hc in range(nt):
            dma("gpsimd", wv_bf[:, hc], sv[:, hc], writes=["wvb"])

        dec_jobs = []
        for src, dst, ntile in ((gidx, wg_hb, c.ngu), (uidx, wu_hb, c.ngu),
                                (didx, wd_hb, c.nd)):
            for a0 in range(0, ntile, c.dec_slab):
                w = min(c.dec_slab, ntile - a0) * 128
                dec_jobs.append((src[:, a0:a0 + w // 128, :],
                                 dst[:, a0:a0 + w // 128, :], w))

        def _dec_load(n):
            src, dst, w = dec_jobs[n]
            par = n % 2
            src2 = src.rearrange("p t f -> p (t f)")
            dma("gpsimd", dec_bs[par][:, 0:w], src2,
                writes=[f"dec_b{par}"])

        def _dec_jg(n, k):
            src, dst, w = dec_jobs[n]
            par = n % 2
            add_eng = "gpsimd" if n % 4 == 0 else "vector"
            da, dares = dec_as[par], f"dec_a{par}"
            db, dbres = dec_bs[par], f"dec_b{par}"
            for j in range(4 * k, 4 * k + 4):
                if j == 0:
                    OP("vector", lambda V, w=w, da=da, db=db: V.tensor_scalar(
                        da[:, 0:w], db[:, 0:w], 0.0, float(lut_vals[0]),
                        ALU.is_equal, ALU.mult),
                       reads=[dbres], writes=[dares])
                else:
                    tb, tres = dec_ts[j % 2], f"dec_t{j % 2}"
                    OP("vector", lambda V, w=w, j=j, tb=tb, db=db:
                       V.tensor_scalar(
                        tb[:, 0:w], db[:, 0:w], float(j),
                        float(lut_vals[j]), ALU.is_equal, ALU.mult),
                       reads=[dbres], writes=[tres])
                    OP(add_eng, lambda E, w=w, da=da, tb=tb: E.tensor_add(
                        da[:, 0:w], da[:, 0:w], tb[:, 0:w]),
                       reads=[tres, dares], writes=[dares])

        def _dec_store(n):
            src, dst, w = dec_jobs[n]
            par = n % 2
            dst2 = dst.rearrange("p t f -> p (t f)")
            dma("gpsimd", dst2, dec_as[par][:, 0:w],
                reads=[f"dec_a{par}"], writes=["dec_o"])

        dec_seq = []
        N_dec = len(dec_jobs)
        dec_seq.append(lambda: _dec_load(0))
        for n in range(N_dec):
            if n + 1 < N_dec:
                dec_seq.append(lambda n=n: _dec_load(n + 1))
            for k in range(4):
                dec_seq.append(lambda n=n, k=k: _dec_jg(n, k))
            dec_seq.append(lambda n=n: _dec_store(n))
        dec_pos = [0]

        def dec_steps(n):
            while n > 0 and dec_pos[0] < len(dec_seq):
                dec_seq[dec_pos[0]]()
                dec_pos[0] += 1
                n -= 1

        def dec_flush():
            dec_steps(len(dec_seq))

        RS = (((R0, "R0"), (R1, "R1"), (R2, "R2")),
              ((R0b, "R0b"), (R1b, "R1b"), (R2b, "R2b")))

        def ln_finalize(width, par=0, off=0):
            (Rx0, r0), (Rx1, r1), (Rx2, r2) = RS[par]
            sl = slice(off, off + width)
            inv_n = 1.0 / c.dm
            OP("vector", lambda V: V.tensor_scalar_mul(
                Rx0[0:1, sl], Rx0[0:1, sl], inv_n),
               reads=[r0], writes=[r0])
            OP("vector", lambda V: V.tensor_scalar_mul(
                Rx1[0:1, sl], Rx1[0:1, sl], inv_n),
               reads=[r1], writes=[r1])
            OP("vector", lambda V: V.tensor_mul(
                Rx2[0:1, sl], Rx0[0:1, sl], Rx0[0:1, sl]),
               reads=[r0], writes=[r2])
            OP("vector", lambda V: V.tensor_sub(
                Rx1[0:1, sl], Rx1[0:1, sl], Rx2[0:1, sl]),
               reads=[r1, r2], writes=[r1])
            OP("vector", lambda V: V.tensor_scalar_add(
                Rx1[0:1, sl], Rx1[0:1, sl], float(c.eps)),
               reads=[r1], writes=[r1])
            OP("scalar", lambda A: A.activation(
                Rx1[0:1, sl], Rx1[0:1, sl], AF.Ln),
               reads=[r1], writes=[r1])
            OP("scalar", lambda A: A.activation(
                Rx1[0:1, sl], Rx1[0:1, sl], AF.Exp, scale=-0.5),
               reads=[r1], writes=[r1])
            OP("vector", lambda V: V.tensor_mul(
                Rx2[0:1, sl], Rx0[0:1, sl], Rx1[0:1, sl]),
               reads=[r0, r1], writes=[r2])

        def bcast_rows(width, off=0, par=0, banks=(4, 5)):
            (Rx0, r0), (Rx1, r1), (Rx2, r2) = RS[par]
            b0, b1 = banks
            OP("tensor", lambda T: T.matmul(
                Q4(b0, width), ones_row[:, :], Rx1[0:1, off:off + width],
                start=True, stop=True),
               reads=["onesr", r1], writes=[f"PSq{b0}"])
            OP("tensor", lambda T: T.matmul(
                Q4(b1, width), ones_row[:, :], Rx2[0:1, off:off + width],
                start=True, stop=True),
               reads=["onesr", r2], writes=[f"PSq{b1}"])

        def stats_tiles(load_fn, keep, t0, t1):
            ld_bufs = ((stat_ld[:, 0:cw], "stld"), (tf1[:, 0:cw], "tf1"))
            sq_bufs = ((stat_sq[:, 0:cw], "stsq"), (sqb, "tf2"))
            for t in range(t0, t1):
                ld, ldres = ld_bufs[t % 2]
                sq, sqres = sq_bufs[t % 2]
                kd, kres = keep(t)
                dma("sync", ld, load_fn(t), writes=[ldres])
                OP("scalar", lambda A, ld=ld, kd=kd: A.activation(
                    kd, ld, AF.Copy), reads=[ldres], writes=kres)
                OP("scalar", lambda A, ld=ld, sq=sq: A.activation(
                    sq, ld, AF.Square), reads=[ldres], writes=[sqres])
                OP("tensor", lambda T, kd=kd, t=t: T.matmul(
                    Q4(2)[0:1, :], ones_col[:, :], kd,
                    start=(t == 0), stop=(t == nt - 1)),
                   reads=["ones"] + kres, writes=["PSq2"])
                OP("tensor", lambda T, sq=sq, t=t: T.matmul(
                    Q4(3)[0:1, :], ones_col[:, :], sq,
                    start=(t == 0), stop=(t == nt - 1)),
                   reads=["ones", sqres], writes=["PSq3"])

        def stats_rows(par=0, off=0):
            (Rx0, r0), (Rx1, r1), _ = RS[par]
            OP("vector", lambda V: V.tensor_copy(Rx0[0:1, off:off + cw],
                                                 Q4(2)[0:1, :]),
               reads=["PSq2"], writes=[r0])
            OP("vector", lambda V: V.tensor_copy(Rx1[0:1, off:off + cw],
                                                 Q4(3)[0:1, :]),
               reads=["PSq3"], writes=[r1])

        def chunk_stats(load_fn, keep, par=0):
            stats_tiles(load_fn, keep, 0, nt)
            stats_rows(par)

        def hb_stats(src_hb, src_f32, extra_reads=()):
            for s in range(c.n_oc):
                sl = slice(s * cw, (s + 1) * cw)
                for t in range(nt):
                    if src_f32:
                        ld, ldres = ((stat_ld[:, 0:cw], "stld") if t % 2 == 0
                                     else (tf1[:, 0:cw], "tf1"))
                        dma("sync", ld, src_hb[t][:, sl],
                            reads=list(extra_reads), writes=[ldres])
                        mi, mires = ((stat_sq[:, 0:cw], "stsq") if t % 2 == 0
                                     else (sqb, "tf2"))
                        OP("scalar", lambda A, ld=ld, mi=mi: A.activation(
                            mi, ld, AF.Copy), reads=[ldres], writes=[mires])
                        OP("tensor", lambda T, mi=mi, t=t: T.matmul(
                            Q4(4)[0:1, :], ones_col[:, :], mi,
                            start=(t == 0), stop=(t == nt - 1)),
                           reads=["ones", mires], writes=["PSq4"])
                        OP("scalar", lambda A, ld=ld, mi=mi: A.activation(
                            mi, ld, AF.Square), reads=[ldres, mires],
                           writes=[mires])
                        OP("tensor", lambda T, mi=mi, t=t: T.matmul(
                            Q4(5)[0:1, :], ones_col[:, :], mi,
                            start=(t == 0), stop=(t == nt - 1)),
                           reads=["ones", mires], writes=["PSq5"])
                    else:
                        ld, ldres = ((ldb_v, "stld") if t % 2 == 0
                                     else (tf1b, "tf1"))
                        sq, sqres = ((stat_sq[:, 0:cw], "stsq") if t % 2 == 0
                                     else (sqb, "tf2"))
                        dma("sync", ld, src_hb[t][:, sl],
                            reads=list(extra_reads), writes=[ldres])
                        OP("scalar", lambda A, ld=ld, sq=sq: A.activation(
                            sq, ld, AF.Square), reads=[ldres],
                           writes=[sqres])
                        OP("tensor", lambda T, ld=ld, t=t: T.matmul(
                            Q4(4)[0:1, :], ones_col[:, :], ld,
                            start=(t == 0), stop=(t == nt - 1)),
                           reads=["ones", ldres], writes=["PSq4"])
                        OP("tensor", lambda T, sq=sq, t=t: T.matmul(
                            Q4(5)[0:1, :], ones_col[:, :], sq,
                            start=(t == 0), stop=(t == nt - 1)),
                           reads=["ones", sqres], writes=["PSq5"])
                OP("vector", lambda V, sl=sl: V.tensor_copy(
                    R0[0:1, sl], Q4(4)[0:1, :]), reads=["PSq4"],
                   writes=["R0"])
                OP("vector", lambda V, sl=sl: V.tensor_copy(
                    R1[0:1, sl], Q4(5)[0:1, :]), reads=["PSq5"],
                   writes=["R1"])

        def xn_own(t, s):
            lo = t * c.s_own + s * cw
            return big1[:, lo:lo + cw], pages("big1", lo, lo + cw)

        for s in range(c.n_oc):
            sl = slice(s * cw, (s + 1) * cw)
            chunk_stats(lambda t, sl=sl: xownT[t][:, sl],
                        lambda t, s=s: xn_own(t, s), par=s % 2)
            ln_finalize(cw, par=s % 2)
            bcast_rows(cw, par=s % 2)
            for t in range(nt):
                xn, xnres = xn_own(t, s)
                OP("vector", lambda V, xn=xn: V.tensor_mul(
                    tf3[:, 0:cw], xn, Q4(4)),
                   reads=xnres + ["PSq4"], writes=["tf3"])
                OP("vector", lambda V, xn=xn: V.tensor_sub(
                    xn, tf3[:, 0:cw], Q4(5)),
                   reads=["tf3", "PSq5"], writes=xnres)
            dec_steps(4)

        def wload(wbf, m, buf, res, src_res):
            dma("sync", buf[:, :], wbf[:, m, :], reads=[src_res],
                writes=[res])

        def rope_pair(getA, resA, outA, outAres, outB, outBres,
                      getB=None, resB=None, cs=None):
            getB = QB if getB is None else getB
            resB = ["QB"] if resB is None else resB
            cosv, sinv, m_cv, m_sv = ((cosb, sinb, m_cosb, m_sinb)
                                      if cs is None else cs)
            OP("vector", lambda V, aA=getA, cosv=cosv: V.tensor_mul(
                tf1[:, 0:cw], aA, cosv),
               reads=list(resA) + m_cv, writes=["tf1"])
            OP("vector", lambda V, aB=getB, sinv=sinv: V.tensor_mul(
                tf2[:, 0:cw], aB, sinv),
               reads=list(resB) + m_sv, writes=["tf2"])
            OP("vector", lambda V, o=outA: V.tensor_sub(o, tf1[:, 0:cw],
                                                        tf2[:, 0:cw]),
               reads=["tf1", "tf2"], writes=outAres)
            OP("vector", lambda V, aB=getB, cosv=cosv: V.tensor_mul(
                tf1[:, 0:cw], aB, cosv),
               reads=list(resB) + m_cv, writes=["tf1"])
            OP("vector", lambda V, aA=getA, sinv=sinv: V.tensor_mul(
                tf2[:, 0:cw], aA, sinv),
               reads=list(resA) + m_sv, writes=["tf2"])
            OP("vector", lambda V, o=outB: V.tensor_add(o, tf1[:, 0:cw],
                                                        tf2[:, 0:cw]),
               reads=["tf1", "tf2"], writes=outBres)

        def row_norm_max(src, sres, dstrow, dres, mh):
            OP("scalar", lambda A, src=src: A.activation(
                stat_sq[:, 0:cw], src, AF.Square),
               reads=sres, writes=["stsq"])
            OP("tensor", lambda T: T.matmul(
                Q4(2)[0:1, :], ones_col[:, :], stat_sq[:, 0:cw],
                start=True, stop=True),
               reads=["ones", "stsq"], writes=["PSq2"])
            OP("scalar", lambda A: A.activation(R2[0:1, 0:cw],
                                                Q4(2)[0:1, :], AF.Copy),
               reads=["PSq2"], writes=["R2"])
            OP("vector", lambda V: V.reduce_max(
                sc1[0:1, 0:1], R2[0:1, 0:cw], axis=mybir.AxisListType.X),
               reads=["R2"], writes=["sc1"])
            OP("vector", lambda V, mh=mh, dr=dstrow: V.tensor_max(
                dr[0:1, mh:mh + 1], dr[0:1, mh:mh + 1], sc1[0:1, 0:1]),
               reads=["sc1", dres], writes=[dres])

        wload(wq_bf, 0, wbuf0, "wb0", "wqb")
        for tp in range(hp):
            wload(wq_bf, tp + hp, wbuf1, "wb1", "wqb")
            for s in range(c.n_oc):
                for hc in range(nt):
                    rhs, rres = xn_own(hc, s)
                    OP("tensor", lambda T, hc=hc, rhs=rhs: T.matmul(
                        Q4(0), wbuf0[:, hc * 128:(hc + 1) * 128], rhs,
                        start=(hc == 0), stop=(hc == nt - 1)),
                       reads=["wb0"] + rres, writes=["PSq0"])
                lo = (tp * c.n_oc + s) * cw
                st_ = big0[:, lo:lo + cw]
                stres = pages("big0", lo, lo + cw)
                OP("scalar", lambda A, st_=st_: A.activation(st_, Q4(0),
                                                             AF.Copy),
                   reads=["PSq0"], writes=stres)
            if tp + 1 < hp:
                wload(wq_bf, tp + 1, wbuf0, "wb0", "wqb")
            for s in range(c.n_oc):
                sl = slice(s * cw, (s + 1) * cw)
                for hc in range(nt):
                    rhs, rres = xn_own(hc, s)
                    OP("tensor", lambda T, hc=hc, rhs=rhs: T.matmul(
                        Q4(1), wbuf1[:, hc * 128:(hc + 1) * 128], rhs,
                        start=(hc == 0), stop=(hc == nt - 1)),
                       reads=["wb1"] + rres, writes=["PSq1"])
                OP("scalar", lambda A: A.activation(QB, Q4(1), AF.Copy),
                   reads=["PSq1"], writes=["QB"])
                dma("sync", cosb, cosq[tp][:, sl], writes=m_cosb)
                dma("sync", sinb, sinq[tp][:, sl], writes=m_sinb)
                lo = (tp * c.n_oc + s) * cw
                st_ = big0[:, lo:lo + cw]
                stres = pages("big0", lo, lo + cw)
                rope_pair(st_, stres, QO, ["QO"], QO2, ["QO2"])
                dma("sync", qT_h[tp][:, sl], QO, reads=["QO"], writes=["qTh"])
                dma("sync", qT_h[tp + hp][:, sl], QO2, reads=["QO2"],
                    writes=["qTh"])
                row_norm_max(QO, ["QO"], mq2, "mq2", tp)
                row_norm_max(QO2, ["QO2"], mq2, "mq2", tp + hp)
            dec_steps(2)

        def xn_area(t, apar):
            lo = (apar * nt + t) * cw
            return big0[:, lo:lo + cw], pages("big0", lo, lo + cw)

        wv_b1 = big1[:, 0:nt * c.dm]
        m_wv = pages("big1", 0, nt * c.dm)
        dma("sync", wv_b1.rearrange("p (hc f) -> p hc f", f=c.dm), wv_bf,
            reads=["wvb"], writes=m_wv)
        n_sub = cw // 128
        vch_lo = 2 * nt * cw
        VB = (0, 1, 6, 7)

        def vch_sl(sub, d0, d1):
            lo = vch_lo + sub * c.dm + d0
            return (big0[:, lo:lo + (d1 - d0)],
                    pages("big0", lo, lo + (d1 - d0)))

        def c12_stats_part(ch, t0, t1):
            sl_f = slice(ch * cw, (ch + 1) * cw)
            apar = ch % 2
            stats_tiles(lambda t, sl_f=sl_f: xT[t][:, sl_f],
                        lambda t, apar=apar: xn_area(t, apar), t0, t1)

        def c12_norm(ch):
            sl_f = slice(ch * cw, (ch + 1) * cw)
            apar = ch % 2
            stats_rows(par=apar)
            ln_finalize(cw, par=apar)
            bcast_rows(cw, par=apar)
            for t in range(nt):
                xn, xnres = xn_area(t, apar)
                OP("vector", lambda V, xn=xn: V.tensor_mul(
                    tf3[:, 0:cw], xn, Q4(4)),
                   reads=xnres + ["PSq4"], writes=["tf3"])
                OP("vector", lambda V, xn=xn: V.tensor_sub(
                    xn, tf3[:, 0:cw], Q4(5)),
                   reads=["tf3", "PSq5"], writes=xnres)
                dma("sync", xn_hb[t][:, sl_f], xn, reads=xnres,
                    writes=["xnhb"])

        def c12_vmms(ch, dq):
            apar = ch % 2
            for sub in range(n_sub):
                bank = VB[sub]
                for hc in range(nt):
                    rhs = wv_b1[:, hc * c.dm + dq * 512:
                                hc * c.dm + (dq + 1) * 512]
                    xs, xsres = xn_area(hc, apar)
                    lhs = xs[:, sub * 128:(sub + 1) * 128]
                    OP("tensor", lambda T, hc=hc, bank=bank, rhs=rhs,
                       lhs=lhs: T.matmul(
                        Q4(bank, 512), lhs, rhs,
                        start=(hc == 0), stop=(hc == nt - 1)),
                       reads=xsres + m_wv, writes=[f"PSq{bank}"])
            for sub in range(n_sub):
                bank = VB[sub]
                vd, vdres = vch_sl(sub, dq * 512, (dq + 1) * 512)
                OP("scalar", lambda A, vd=vd, bank=bank: A.activation(
                    vd, Q4(bank, 512), AF.Copy),
                   reads=[f"PSq{bank}"], writes=vdres)

        c12_stats_part(0, 0, nt)
        c12_norm(0)
        for ch in range(c.n_ch):
            for dq in range(c.dm // 512):
                if ch + 1 < c.n_ch:
                    tpg = nt // (c.dm // 512)
                    c12_stats_part(ch + 1, dq * tpg, (dq + 1) * tpg)
                c12_vmms(ch, dq)
                dec_steps(3)
            for sub in range(n_sub):
                vrow, vrres = vch_sl(sub, 0, c.dm)
                dma("sync", v_hb[ch * cw + sub * 128:
                                 ch * cw + (sub + 1) * 128, :],
                    vrow, reads=vrres, writes=["vhb"])
            if ch + 1 < c.n_ch:
                c12_norm(ch + 1)
            dec_steps(4)

        wk_b1 = big1[:, 0:nt * c.dm]
        m_wk = pages("big1", 0, nt * c.dm)
        dma("sync", wk_b1.rearrange("p (m f) -> p m f", f=c.dm), wk_bf,
            reads=["wkb"] + m_wv, writes=m_wk)

        kslots = [((QA, "QA"), (QB, "QB")), ((K2A, "K2A"), (K2B, "K2B")),
                  ((K3A, "K3A"), (K3B, "K3B"))]

        def c3_xn_loads(ch):
            apar = ch % 2
            sl_f = slice(ch * cw, (ch + 1) * cw)
            for t in range(nt):
                xs, xsres = xn_area(t, apar)
                dma("sync", xs, xn_hb[t][:, sl_f], reads=["xnhb"],
                    writes=xsres)

        def c3_vch_loads(ch):
            for sub in range(n_sub):
                vrow, vrres = vch_sl(sub, 0, c.dm)
                dma("sync", vrow,
                    v_hb[ch * cw + sub * 128:ch * cw + (sub + 1) * 128, :],
                    reads=["vhb"], writes=vrres)

        def c3_front_pre(u):
            ch, tp = divmod(u, hp)
            par = u % 3
            sl_f = slice(ch * cw, (ch + 1) * cw)
            if tp == min(4, hp - 1) and ch + 1 < c.n_ch:
                c3_xn_loads(ch + 1)
            dma("sync", cos_s[par], cosk[tp][:, sl_f], writes=m_cos_s[par])
            dma("sync", sin_s[par], sink[tp][:, sl_f], writes=m_sin_s[par])

        def c3_kmm(u, half):
            ch, tp = divmod(u, hp)
            apar = ch % 2
            bank = half
            m = tp + half * hp
            wslab = wk_b1[:, m * c.dm:(m + 1) * c.dm]
            for hc in range(nt):
                xs, xsres = xn_area(hc, apar)
                OP("tensor", lambda T, hc=hc, wslab=wslab, bank=bank,
                   xnv=xs: T.matmul(
                    Q4(bank), wslab[:, hc * 128:(hc + 1) * 128],
                    xnv, start=(hc == 0), stop=(hc == nt - 1)),
                   reads=m_wk + xsres, writes=[f"PSq{bank}"])

        def c3_evac(u):
            par = u % 3
            (kA, kAres), (kB, kBres) = kslots[par]
            for (bank, dst, dres) in ((0, kA, kAres), (1, kB, kBres)):
                OP("scalar", lambda A, dst=dst, bank=bank: A.activation(
                    dst, Q4(bank), AF.Copy),
                   reads=[f"PSq{bank}"], writes=[dres])

        SQK2 = ldb_v
        m_SQK2 = ["stld"]

        def c3_back1(u):
            ch, tp = divmod(u, hp)
            par = u % 3
            sl_f = slice(ch * cw, (ch + 1) * cw)
            (kA, kAres), (kB, kBres) = kslots[par]
            rope_pair(kA, [kAres], QO, ["QO"], QO2, ["QO2"],
                      getB=kB, resB=[kBres],
                      cs=(cos_s[par], sin_s[par], m_cos_s[par],
                          m_sin_s[par]))
            for (kout, kres, mh) in ((QO, "QO", tp),
                                     (QO2, "QO2", tp + hp)):
                dma("sync", kT_h[mh][:, sl_f], kout, reads=[kres],
                    writes=["kTh"])
            for sub in range(n_sub):
                OP("tensor", lambda T, sub=sub:
                   T.transpose(psb[:, sub * 128:(sub + 1) * 128],
                               QO[:, sub * 128:(sub + 1) * 128],
                               ident_s[:, :]),
                   reads=["QO", "ident"], writes=["PSq7"])
            OP("scalar", lambda A: A.activation(SQK, psb[:, 0:cw],
                                                AF.Copy),
               reads=["PSq7"], writes=m_SQK)
            for sub in range(n_sub):
                OP("tensor", lambda T, sub=sub:
                   T.transpose(psb[:, sub * 128:(sub + 1) * 128],
                               QO2[:, sub * 128:(sub + 1) * 128],
                               ident_s[:, :]),
                   reads=["QO2", "ident"], writes=["PSq7"])
            OP("scalar", lambda A: A.activation(SQK2, psb[:, 0:cw],
                                                AF.Copy),
               reads=["PSq7"], writes=m_SQK2)
            OP("vector", lambda V: V.tensor_scalar_min(
                tf3[:, 0:cw], SQK, 0.0), reads=m_SQK, writes=["tf3"])
            OP("vector", lambda V: V.tensor_scalar_min(
                tf5[:, 0:cw], SQK2, 0.0), reads=m_SQK2, writes=["tf5"])
            OP("scalar", lambda A: A.activation(
                tf4[:, 0:cw], tf3[:, 0:cw], AF.Exp),
               reads=["tf3"], writes=["tf4"])
            OP("scalar", lambda A: A.activation(
                tf2[:, 0:cw], tf5[:, 0:cw], AF.Exp),
               reads=["tf5"], writes=["tf2"])
            OP("vector", lambda V: V.tensor_scalar_add(
                tf3[:, 0:cw], SQK, 1.0), reads=m_SQK, writes=["tf3"])
            OP("vector", lambda V: V.tensor_max(
                SQK, tf3[:, 0:cw], tf4[:, 0:cw]),
               reads=["tf3", "tf4"], writes=m_SQK)
            OP("vector", lambda V: V.tensor_scalar_add(
                tf5[:, 0:cw], SQK2, 1.0), reads=m_SQK2, writes=["tf5"])
            OP("vector", lambda V: V.tensor_max(
                SQK2, tf5[:, 0:cw], tf2[:, 0:cw]),
               reads=["tf5", "tf2"], writes=m_SQK2)

        def c3_back2(u):
            ch, tp = divmod(u, hp)
            for (kf, kfm, mh) in ((SQK, m_SQK, tp), (SQK2, m_SQK2, tp + hp)):
                for sub in range(n_sub):
                    lhs = kf[:, sub * 128:(sub + 1) * 128]
                    rhs, rres = vch_sl(sub, mh * 128, (mh + 1) * 128)
                    OP("tensor", lambda T, lhs=lhs, rhs=rhs, sub=sub:
                       T.matmul(MEMPS[:, 0:128], lhs, rhs,
                                start=(sub == 0), stop=False),
                       reads=kfm + rres, writes=["PSq6"])
                    OP("tensor", lambda T, lhs=lhs, sub=sub: T.matmul(
                        MEMPS[:, 128:129], lhs, ones_col[:, :],
                        start=False, stop=(sub == n_sub - 1)),
                       reads=kfm + ["ones"], writes=["PSq6"])
                msl = slice(mh * 129, (mh + 1) * 129)
                OP("vector", lambda V, msl=msl: V.tensor_add(
                    memory[:, msl], memory[:, msl], MEMPS[:, 0:129]),
                   reads=["PSq6"], writes=["memory"])
            row_norm_max(QO, ["QO"], mk2, "mk2", tp)
            row_norm_max(QO2, ["QO2"], mk2, "mk2", tp + hp)
            if tp == hp - 1 and ch + 1 < c.n_ch:
                c3_vch_loads(ch + 1)
            dec_steps(1)

        c3_xn_loads(0)
        c3_vch_loads(0)
        n_u = c.n_ch * hp
        for u in range(n_u + 2):
            if 1 <= u <= n_u:
                c3_evac(u - 1)
            if u < n_u:
                c3_front_pre(u)
                c3_kmm(u, 0)
            if u >= 2:
                c3_back1(u - 2)
            if u < n_u:
                c3_kmm(u, 1)
            if u >= 2:
                c3_back2(u - 2)

        OP("vector", lambda V: V.reduce_max(sc1[0:1, 0:1], mq2[0:1, :],
                                            axis=mybir.AxisListType.X),
           reads=["mq2"], writes=["sc1"])
        OP("vector", lambda V: V.reduce_max(sc1[0:1, 1:2], mk2[0:1, :],
                                            axis=mybir.AxisListType.X),
           reads=["mk2"], writes=["sc1"])
        OP("vector", lambda V: V.tensor_mul(sc1[0:1, 2:3], sc1[0:1, 0:1],
                                            sc1[0:1, 1:2]),
           reads=["sc1"], writes=["sc1"])
        OP("scalar", lambda A: A.activation(sc1[0:1, 3:4], sc1[0:1, 2:3],
                                            AF.Ln), reads=["sc1"],
           writes=["sc1"])
        OP("scalar", lambda A: A.activation(sc1[0:1, 3:4], sc1[0:1, 3:4],
                                            AF.Exp, scale=0.5),
           reads=["sc1"], writes=["sc1"])
        OP("vector", lambda V: V.tensor_scalar_mul(sc1[0:1, 3:4],
                                                   sc1[0:1, 3:4],
                                                   -float(c.scale)),
           reads=["sc1"], writes=["sc1"])
        OP("tensor", lambda T: T.matmul(Q4(4, 1)[:, 0:1], ones_row[:, :],
                                        sc1[0:1, 3:4], start=True, stop=True),
           reads=["onesr", "sc1"], writes=["PSq4"])
        OP("vector", lambda V: V.tensor_copy(biasC[:, :], Q4(4, 1)[:, 0:1]),
           reads=["PSq4"], writes=["biasC"])

        qbuf = mid[:, 0:c.s_own]
        qf = mid[:, c.s_own:2 * c.s_own]
        pbs = [mid[:, 2 * c.s_own + i * 2 * cw:
                   2 * c.s_own + (i + 1) * 2 * cw] for i in range(3)]
        m_qbuf = pages("mid", 0, c.s_own, 1024)
        m_qf = pages("mid", c.s_own, 2 * c.s_own, 1024)
        m_pbs = [pages("mid", 2 * c.s_own + i * 2 * cw,
                       2 * c.s_own + (i + 1) * 2 * cw, 1024)
                 for i in range(3)]

        kbuf = big0[:, 0:c.s_full]
        vb0 = big0[:, c.s_full:2 * c.s_full]
        vb1 = big0[:, 2 * c.s_full:3 * c.s_full]
        m_kbuf = pages("big0", 0, c.s_full)
        m_vb = [pages("big0", c.s_full, 2 * c.s_full),
                pages("big0", 2 * c.s_full, 3 * c.s_full)]

        v_re = v_hb.rearrange("(cb p) d -> p cb d", p=128)
        n_cb = c.s_full // 128

        def load_vhead(h, vb, vres):
            npiece = 8 if n_cb % 8 == 0 else 1
            step = n_cb // npiece
            for pc in range(npiece):
                src = v_re[:, pc * step:(pc + 1) * step,
                           h * 128:(h + 1) * 128]
                dst = vb[:, pc * step * 128:(pc + 1) * step * 128]
                dst3 = dst.rearrange("p (cb d) -> p cb d", d=128)
                dma("sync", dst3, src, reads=["vhb"], writes=vres)

        load_vhead(0, vb0, m_vb[0])
        for h in range(nt):
            vcur, m_vcur = (vb0, m_vb[0]) if h % 2 == 0 else (vb1, m_vb[1])
            if h + 1 < nt:
                vnxt, m_vnxt = (vb1, m_vb[1]) if h % 2 == 0 else \
                    (vb0, m_vb[0])
                load_vhead(h + 1, vnxt, m_vnxt)
            for pc in range(c.s_full // 2048):
                dma("sync", kbuf[:, pc * 2048:(pc + 1) * 2048],
                    kT_h[h][:, pc * 2048:(pc + 1) * 2048], reads=["kTh"],
                    writes=pages("big0", pc * 2048, (pc + 1) * 2048))
            for pc in range(c.s_own // 1024):
                dma("sync", qbuf[:, pc * 1024:(pc + 1) * 1024],
                    qT_h[h][:, pc * 1024:(pc + 1) * 1024], reads=["qTh"],
                    writes=pages("mid", pc * 1024, (pc + 1) * 1024, 1024))
            for s in range(c.n_oc):
                qsl = qbuf[:, s * cw:(s + 1) * cw]
                m_qs = pages("mid", s * cw, (s + 1) * cw, 1024)
                OP("vector", lambda V, qsl=qsl: V.tensor_scalar_min(
                    tf1[:, 0:cw], qsl, 0.0), reads=m_qs, writes=["tf1"])
                OP("scalar", lambda A: A.activation(tf2[:, 0:cw],
                                                    tf1[:, 0:cw], AF.Exp),
                   reads=["tf1"], writes=["tf2"])
                OP("vector", lambda V, qsl=qsl: V.tensor_scalar_add(
                    tf1[:, 0:cw], qsl, 1.0), reads=m_qs, writes=["tf1"])
                OP("vector", lambda V, s=s: V.tensor_max(
                    qf[:, s * cw:(s + 1) * cw], tf1[:, 0:cw], tf2[:, 0:cw]),
                   reads=["tf1", "tf2"], writes=m_qf)
            OP("vector", lambda V, h=h: V.tensor_copy(
                memb[:, 0:129], memory[:, h * 129:(h + 1) * 129]),
               reads=["memory"], writes=["memb"])
            OP("tensor", lambda T, h=h: T.matmul(
                Q4(4, 128)[0:1, :], g_row[0:1, h:h + 1], ones_row[:, :],
                start=True, stop=True),
               reads=["grow", "onesr"], writes=["PSq4"])
            OP("vector", lambda V: V.tensor_copy(gcol[0:1, :],
                                                 Q4(4, 128)[0:1, :]),
               reads=["PSq4"], writes=["gcol"])
            OP("tensor", lambda T, h=h: T.matmul(
                Q4(4, 128)[0:1, :], og_row[0:1, h:h + 1], ones_row[:, :],
                start=True, stop=True),
               reads=["ogrow", "onesr"], writes=["PSq4"])
            OP("vector", lambda V: V.tensor_copy(ogcol[0:1, :],
                                                 Q4(4, 128)[0:1, :]),
               reads=["PSq4"], writes=["ogcol"])
            for qb in range(c.n_oc):
                qsl = slice(qb * cw, (qb + 1) * cw)
                OP("tensor", lambda T, qsl=qsl: T.matmul(
                    PVPS, memb[:, 0:128], qf[:, qsl], start=True, stop=True),
                   reads=["memb"] + m_qf, writes=["PSq6"])
                OP("scalar", lambda A: A.activation(tf3[:, 0:cw], PVPS,
                                                    AF.Copy),
                   reads=["PSq6"], writes=["tf3"])
                OP("tensor", lambda T, qsl=qsl: T.matmul(
                    DEN, memb[:, 128:129], qf[:, qsl], start=True, stop=True),
                   reads=["memb"] + m_qf, writes=["PSq7"])
                OP("vector", lambda V, qsl=qsl: V.tensor_copy(
                    R1[0:1, qsl], DEN), reads=["PSq7"], writes=["R1"])
                OP("vector", lambda V, qsl=qsl: V.reciprocal(
                    R1[0:1, qsl], R1[0:1, qsl]), reads=["R1"], writes=["R1"])
                ng = c.n_kc // 2

                def emit_sc(g, qsl=qsl, qb=qb):
                    par = g % 3
                    m_q = pages("mid", qb * cw, (qb + 1) * cw, 1024)
                    for half in range(2):
                        kc = 2 * g + half
                        bank = 2 * par + half
                        m_k = pages("big0", kc * 128, (kc + 1) * 128)
                        OP("tensor", lambda T, kc=kc, bank=bank, qsl=qsl:
                           T.matmul(Q4(bank),
                                    kbuf[:, kc * 128:(kc + 1) * 128],
                                    qbuf[:, qsl], start=True, stop=True),
                           reads=m_k + m_q, writes=[f"PSq{bank}"])

                def emit_tail(g, vcur=vcur, m_vcur=m_vcur):
                    par = g % 3
                    pbx, m_pbx = pbs[par], m_pbs[par]
                    pbx3 = pbx.rearrange("p (b x) -> p b x", x=cw)
                    OP("scalar", lambda A, par=par, pbx3=pbx3: A.activation(
                        pbx3, QPAIR(par), AF.Exp, bias=biasC[:, 0:1],
                        scale=c.scale),
                       reads=[f"PSq{2 * par}", f"PSq{2 * par + 1}", "biasC"],
                       writes=m_pbx)
                    for half in range(2):
                        kc = 2 * g + half
                        OP("tensor", lambda T, kc=kc, half=half, pbx=pbx,
                           vcur=vcur: T.matmul(
                            PVPS, vcur[:, kc * 128:(kc + 1) * 128],
                            pbx[:, half * cw:(half + 1) * cw],
                            start=(kc == 0), stop=(kc == c.n_kc - 1)),
                           reads=m_vcur + m_pbx, writes=["PSq6"])
                        OP("tensor", lambda T, kc=kc, half=half, pbx=pbx:
                           T.matmul(DEN, ones_col[:, :],
                                    pbx[:, half * cw:(half + 1) * cw],
                                    start=(kc == 0),
                                    stop=(kc == c.n_kc - 1)),
                           reads=["ones"] + m_pbx, writes=["PSq7"])

                for g in range(ng + 1):
                    if g < ng:
                        emit_sc(g)
                        if g in (2, 10, 18):
                            dec_steps(4)
                        elif g == 25:
                            dec_steps(2)
                    if g >= 1:
                        emit_tail(g - 1)
                OP("vector", lambda V, qsl=qsl: V.tensor_copy(
                    R0[0:1, qsl], DEN), reads=["PSq7"], writes=["R0"])
                OP("scalar", lambda A: A.activation(
                    tf4[:, 0:cw], PVPS, AF.Copy),
                   reads=["PSq6"], writes=["tf4"])
                OP("vector", lambda V, qsl=qsl: V.reciprocal(
                    R0[0:1, qsl], R0[0:1, qsl]), reads=["R0"], writes=["R0"])
                OP("tensor", lambda T, qsl=qsl: T.matmul(
                    PVPS, gcol[0:1, :], R1[0:1, qsl], start=True, stop=True),
                   reads=["gcol", "R1"], writes=["PSq6"])
                OP("tensor", lambda T, qsl=qsl: T.matmul(
                    Q4(7), ogcol[0:1, :], R0[0:1, qsl],
                    start=True, stop=True),
                   reads=["ogcol", "R0"], writes=["PSq7"])
                lo = h * c.s_own + qb * cw
                cmb = big1[:, lo:lo + cw]
                cres = pages("big1", lo, lo + cw)
                OP("vector", lambda V: V.tensor_mul(
                    tf3[:, 0:cw], tf3[:, 0:cw], PVPS),
                   reads=["tf3", "PSq6"], writes=["tf3"])
                OP("vector", lambda V: V.tensor_mul(
                    tf4[:, 0:cw], tf4[:, 0:cw], Q4(7)),
                   reads=["tf4", "PSq7"], writes=["tf4"])
                OP("vector", lambda V, cmb=cmb: V.tensor_add(
                    cmb, tf4[:, 0:cw], tf3[:, 0:cw]),
                   reads=["tf3", "tf4"], writes=cres)
            dec_steps(4)

        def park_a(s, ob):
            lo = (s % 2) * nt * cw + ob * cw
            return big0[:, lo:lo + cw], pages("big0", lo, lo + cw)

        def park_r2(t):
            lo = 2 * nt * cw + t * cw
            return big0[:, lo:lo + cw], pages("big0", lo, lo + cw)

        def p4_front(s):
            for ob in range(nt):
                g = s * nt + ob
                wb, wres = (wbuf0, "wb0") if g % 2 == 0 else (wbuf1, "wb1")
                if g == 0:
                    wload(wo_bf, 0, wbuf0, "wb0", "wob")
                if g + 1 < c.n_oc * nt:
                    nwb, nwres = ((wbuf1, "wb1") if g % 2 == 0 else
                                  (wbuf0, "wb0"))
                    wload(wo_bf, (g + 1) % nt, nwb, nwres, "wob")
                bank = ob % 2
                for ic in range(nt):
                    lo = ic * c.s_own + s * cw
                    cap = big1[:, lo:lo + cw]
                    cres = pages("big1", lo, lo + cw)
                    OP("tensor", lambda T, ic=ic, bank=bank, wb=wb, cap=cap:
                       T.matmul(Q4(bank), wb[:, ic * 128:(ic + 1) * 128],
                                cap, start=(ic == 0), stop=(ic == nt - 1)),
                       reads=[wres] + cres, writes=[f"PSq{bank}"])
                pk, pkres = park_a(s, ob)
                OP("scalar", lambda A, pk=pk, bank=bank: A.activation(
                    pk, Q4(bank), AF.Copy),
                   reads=[f"PSq{bank}"], writes=pkres)
                sq, sqres = ((stat_sq[:, 0:cw], "stsq") if ob % 2 == 0
                             else (sqb, "tf2"))
                OP("scalar", lambda A, pk=pk, sq=sq: A.activation(
                    sq, pk, AF.Square), reads=pkres, writes=[sqres])
                OP("tensor", lambda T, pk=pk, ob=ob: T.matmul(
                    Q4(2)[0:1, :], ones_col[:, :], pk,
                    start=(ob == 0), stop=(ob == nt - 1)),
                   reads=["ones"] + pkres, writes=["PSq2"])
                OP("tensor", lambda T, sq=sq, ob=ob: T.matmul(
                    Q4(3)[0:1, :], ones_col[:, :], sq,
                    start=(ob == 0), stop=(ob == nt - 1)),
                   reads=["ones", sqres], writes=["PSq3"])
                dec_steps(1)

        def p4_mid1(s):
            stats_rows(par=0, off=s * cw)
            ln_finalize(cw, par=0, off=s * cw)
            bcast_rows(cw, off=s * cw, banks=(6, 7))

        def p4_back1(s):
            sl = slice(s * cw, (s + 1) * cw)
            for t in range(nt):
                stg, stres, rsd, rres = (
                    (tf3, "tf3", tf4, "tf4") if t % 2 == 0
                    else (stat_ld, "stld", tf5, "tf5"))
                pk, pkres = park_a(s, t)
                dma("sync", rsd[:, 0:cw], xownT[t][:, sl], writes=[rres])
                OP("vector", lambda V, pk=pk, stg=stg: V.tensor_mul(
                    stg[:, 0:cw], pk, Q4(6)),
                   reads=pkres + ["PSq6"], writes=[stres])
                OP("vector", lambda V, stg=stg: V.tensor_sub(
                    stg[:, 0:cw], stg[:, 0:cw], Q4(7)),
                   reads=[stres, "PSq7"], writes=[stres])
                OP("vector", lambda V, stg=stg, rsd=rsd: V.tensor_add(
                    stg[:, 0:cw], stg[:, 0:cw], rsd[:, 0:cw]),
                   reads=[stres, rres], writes=[stres])
                dma("sync", r2_hb[t][:, sl], stg[:, 0:cw], reads=[stres],
                    writes=["r2hb"])
                pr, prres = park_r2(t)
                OP("scalar", lambda A, stg=stg, pr=pr: A.activation(
                    pr, stg[:, 0:cw], AF.Copy),
                   reads=[stres], writes=prres)
                sq, sqres = ((stat_sq[:, 0:cw], "stsq") if t % 2 == 0
                             else (sqb, "tf2"))
                OP("scalar", lambda A, stg=stg, sq=sq: A.activation(
                    sq, stg[:, 0:cw], AF.Square),
                   reads=[stres], writes=[sqres])
                OP("tensor", lambda T, pr=pr, t=t: T.matmul(
                    Q4(4)[0:1, :], ones_col[:, :], pr,
                    start=(t == 0), stop=(t == nt - 1)),
                   reads=["ones"] + prres, writes=["PSq4"])
                OP("tensor", lambda T, sq=sq, t=t: T.matmul(
                    Q4(5)[0:1, :], ones_col[:, :], sq[:, 0:cw],
                    start=(t == 0), stop=(t == nt - 1)),
                   reads=["ones", sqres], writes=["PSq5"])
                dec_steps(1)
            OP("vector", lambda V, s=s: V.tensor_copy(
                R0[0:1, s * cw:(s + 1) * cw], Q4(4)[0:1, :]),
               reads=["PSq4"], writes=["R0"])
            OP("vector", lambda V, s=s: V.tensor_copy(
                R1[0:1, s * cw:(s + 1) * cw], Q4(5)[0:1, :]),
               reads=["PSq5"], writes=["R1"])

        def p4_mid2(s):
            ln_finalize(cw, par=0, off=s * cw)
            bcast_rows(cw, off=s * cw, banks=(6, 7))

        def p4_back2(s):
            for t in range(nt):
                stg, stres = ((tf3, "tf3") if t % 2 == 0 else (tf2, "tf2"))
                pr, prres = park_r2(t)
                OP("vector", lambda V, pr=pr, stg=stg: V.tensor_mul(
                    stg[:, 0:cw], pr, Q4(6)),
                   reads=prres + ["PSq6"], writes=[stres])
                lo = t * c.s_own + s * cw
                xn2 = big1[:, lo:lo + cw]
                x2res = pages("big1", lo, lo + cw)
                OP("vector", lambda V, xn2=xn2, stg=stg: V.tensor_sub(
                    xn2, stg[:, 0:cw], Q4(7)),
                   reads=[stres, "PSq7"], writes=x2res)
            dec_steps(2)

        p4_front(0)
        for s in range(c.n_oc):
            if s >= 1:
                p4_mid2(s - 1)
                p4_back2(s - 1)
            p4_mid1(s)
            if s + 1 < c.n_oc:
                p4_front(s + 1)
            p4_back1(s)
        p4_mid2(c.n_oc - 1)
        p4_back2(c.n_oc - 1)

        def xn2_t(t, s):
            lo = t * c.s_own + s * cw
            return big1[:, lo:lo + cw], pages("big1", lo, lo + cw)

        hbufs = [(big0[:, ic * cw:(ic + 1) * cw],
                  pages("big0", ic * cw, (ic + 1) * cw)) for ic in range(it)]
        fstg_lo = it * cw
        fstg = big0[:, fstg_lo:fstg_lo + cw]
        m_fstg = pages("big0", fstg_lo, fstg_lo + cw)
        sgb = big0[:, fstg_lo + cw:fstg_lo + 2 * cw]
        m_sgb = pages("big0", fstg_lo + cw, fstg_lo + 2 * cw)

        for sbi in range(c.n_oc):
            sl = slice(sbi * cw, (sbi + 1) * cw)
            for ic in range(it):
                wg_s = wg_hb[:, ic * nt:(ic + 1) * nt, :].rearrange(
                    "p t f -> p (t f)")
                wu_s = wu_hb[:, ic * nt:(ic + 1) * nt, :].rearrange(
                    "p t f -> p (t f)")
                dma("sync", wbuf0[:, :], wg_s, reads=["dec_o"],
                    writes=["wb0"])
                dma("sync", wbuf1[:, :], wu_s, reads=["dec_o"],
                    writes=["wb1"])
                for hc in range(nt):
                    x2, x2res = xn2_t(hc, sbi)
                    OP("tensor", lambda T, hc=hc, x2=x2: T.matmul(
                        Q4(0), wbuf0[:, hc * 128:(hc + 1) * 128], x2,
                        start=(hc == 0), stop=(hc == nt - 1)),
                       reads=["wb0"] + x2res, writes=["PSq0"])
                for hc in range(nt):
                    x2, x2res = xn2_t(hc, sbi)
                    OP("tensor", lambda T, hc=hc, x2=x2: T.matmul(
                        Q4(1), wbuf1[:, hc * 128:(hc + 1) * 128], x2,
                        start=(hc == 0), stop=(hc == nt - 1)),
                       reads=["wb1"] + x2res, writes=["PSq1"])
                OP("scalar", lambda A: A.activation(sgb, Q4(0), AF.Silu),
                   reads=["PSq0"], writes=m_sgb)
                hb, hres = hbufs[ic]
                OP("vector", lambda V, hb=hb: V.tensor_mul(hb, sgb, Q4(1)),
                   reads=m_sgb + ["PSq1"], writes=hres)
                if sbi == 0:
                    dec_steps(4)
            if sbi == 0:
                dec_flush()
            for ob in range(nt):
                bank = 2 + ob % 2
                for icg in range(0, it, 16):
                    ilen = min(16, it - icg)
                    wd_s = wd_hb[:, ob * it + icg:ob * it + icg + ilen, :]\
                        .rearrange("p t f -> p (t f)")
                    wb, wres = ((wbuf0, "wb0") if (icg // 16) % 2 == 0 else
                                (wbuf1, "wb1"))
                    dma("sync", wb[:, 0:ilen * 128], wd_s, reads=["dec_o"],
                        writes=[wres])
                    for i2 in range(ilen):
                        ic = icg + i2
                        hb, hres = hbufs[ic]
                        OP("tensor", lambda T, i2=i2, ic=ic, hb=hb, wb=wb,
                           bank=bank: T.matmul(
                            Q4(bank), wb[:, i2 * 128:(i2 + 1) * 128], hb,
                            start=(ic == 0), stop=(ic == it - 1)),
                           reads=[wres] + hres, writes=[f"PSq{bank}"])
                OP("scalar", lambda A, bank=bank: A.activation(
                    fstg, Q4(bank), AF.Copy),
                   reads=[f"PSq{bank}"], writes=m_fstg)
                dma("sync", f_hb[ob][:, sl], fstg, reads=m_fstg,
                    writes=["fhb"])
                sq, sqres = ((stat_sq[:, 0:cw], "stsq") if ob % 2 == 0
                             else (sqb, "tf2"))
                OP("scalar", lambda A, sq=sq: A.activation(
                    sq, fstg, AF.Square), reads=m_fstg, writes=[sqres])
                OP("tensor", lambda T, ob=ob: T.matmul(
                    Q4(4)[0:1, :], ones_col[:, :], fstg,
                    start=(ob == 0), stop=(ob == nt - 1)),
                   reads=["ones"] + m_fstg, writes=["PSq4"])
                OP("tensor", lambda T, sq=sq, ob=ob: T.matmul(
                    Q4(5)[0:1, :], ones_col[:, :], sq,
                    start=(ob == 0), stop=(ob == nt - 1)),
                   reads=["ones", sqres], writes=["PSq5"])
            OP("vector", lambda V, sbi=sbi: V.tensor_copy(
                R0[0:1, sbi * cw:(sbi + 1) * cw], Q4(4)[0:1, :]),
               reads=["PSq4"], writes=["R0"])
            OP("vector", lambda V, sbi=sbi: V.tensor_copy(
                R1[0:1, sbi * cw:(sbi + 1) * cw], Q4(5)[0:1, :]),
               reads=["PSq5"], writes=["R1"])

        ln_finalize(c.s_own)
        for s in range(c.n_oc):
            sl = slice(s * cw, (s + 1) * cw)
            bcast_rows(cw, off=s * cw)
            for t in range(nt):
                ld, ldres, stg, stres, rsd, rres = (
                    (ldb_v, "stld", tf3, "tf3", tf4, "tf4") if t % 2 == 0
                    else (tf1b, "tf1", tf2, "tf2", tf5, "tf5"))
                dma("sync", ld, f_hb[t][:, sl], reads=["fhb"],
                    writes=[ldres])
                OP("vector", lambda V, ld=ld, stg=stg: V.tensor_mul(
                    stg[:, 0:cw], ld, Q4(4)),
                   reads=[ldres, "PSq4"], writes=[stres])
                OP("vector", lambda V, stg=stg: V.tensor_sub(
                    stg[:, 0:cw], stg[:, 0:cw], Q4(5)),
                   reads=[stres, "PSq5"], writes=[stres])
                dma("sync", rsd[:, 0:cw], r2_hb[t][:, sl],
                    reads=["r2hb"], writes=[rres])
                OP("vector", lambda V, stg=stg, rsd=rsd: V.tensor_add(
                    stg[:, 0:cw], stg[:, 0:cw], rsd[:, 0:cw]),
                   reads=[stres, rres], writes=[stres])
                dma("sync", out[t][:, sl], stg[:, 0:cw], reads=[stres],
                    writes=["outd"])

        for eng in ENGINES:
            for q in ("dS", "dG", "dA"):
                for k in range(N_DMA_SEMS):
                    semname = f"{q}{k}"
                    if semname in em.counts:
                        em.wait(eng, (semname, em.counts[semname]))

        with nc.Block() as block:
            em.run(block)

    return nc


def _stage_inputs(c, inputs):
    x = np.asarray(inputs["L0_attn_hidden_states"])
    B = x.shape[0]
    lut = np.asarray(inputs["L0_ffn_lut"]).astype(np.float32)

    wqT = np.ascontiguousarray(np.asarray(inputs["L0_attn_Wq"]).T)
    wkT = np.ascontiguousarray(np.asarray(inputs["L0_attn_Wk"]).T)
    wvT = np.ascontiguousarray(np.asarray(inputs["L0_attn_Wv"]).T)
    woT = np.ascontiguousarray(np.asarray(inputs["L0_attn_Wo"]).T)
    gate = np.ascontiguousarray(
        np.asarray(inputs["L0_attn_gate"]).reshape(1, c.nt).astype(np.float32))

    inv = (1.0 / (10000.0 ** (np.arange(0, c.dm, 2, dtype=np.float64)
                              / c.dm))).astype(np.float64)
    t = np.arange(c.s_full, dtype=np.float64)
    fr = np.outer(inv, t)
    cosk = np.ascontiguousarray(
        np.cos(fr).astype(np.float32).reshape(c.hp, 128, c.s_full))
    sink = np.ascontiguousarray(
        np.sin(fr).astype(np.float32).reshape(c.hp, 128, c.s_full))

    gi = np.asarray(inputs["L0_ffn_gate_idx"]).astype(np.int32)
    ui = np.asarray(inputs["L0_ffn_up_idx"]).astype(np.int32)
    di = np.asarray(inputs["L0_ffn_down_idx"]).astype(np.int32)
    G4 = gi.reshape(c.it, 128, c.nt, 128)
    gidx = np.ascontiguousarray(
        G4.transpose(3, 0, 2, 1).reshape(128, c.ngu, 128))
    U4 = ui.reshape(c.it, 128, c.nt, 128)
    uidx = np.ascontiguousarray(
        U4.transpose(3, 0, 2, 1).reshape(128, c.ngu, 128))
    D4 = di.reshape(c.nt, 128, c.it, 128)
    didx = np.ascontiguousarray(
        D4.transpose(3, 0, 2, 1).reshape(128, c.nd, 128))

    n_cores = 8
    q_per_b = n_cores // B
    in_maps = []
    for core in range(n_cores):
        b = core // q_per_b
        q = core % q_per_b
        xTb = np.ascontiguousarray(x[b].T).reshape(c.nt, 128, c.s_full)
        xown = np.ascontiguousarray(
            x[b, q * c.s_own:(q + 1) * c.s_own, :].T).reshape(
            c.nt, 128, c.s_own)
        cq = np.ascontiguousarray(
            cosk[:, :, q * c.s_own:(q + 1) * c.s_own])
        sq = np.ascontiguousarray(
            sink[:, :, q * c.s_own:(q + 1) * c.s_own])
        in_maps.append({
            "xT": xTb, "xownT": xown, "wqT": wqT, "wkT": wkT, "wvT": wvT,
            "woT": woT, "gate": gate, "cosq": cq, "sinq": sq,
            "cosk": cosk, "sink": sink, "gidx": gidx, "uidx": uidx,
            "didx": didx,
        })
    return in_maps, lut


def kernel(**inputs):
    c = Cfg()
    in_maps, lut = _stage_inputs(c, inputs)
    nc = build_program(c, [float(v) for v in lut])
    res = run_bass_kernel_spmd(nc, in_maps, core_ids=list(range(8)))
    x = np.asarray(inputs["L0_attn_hidden_states"])
    B, S, H = x.shape
    out = np.empty((B, S, H), dtype=np.float32)
    q_per_b = 8 // B
    for core in range(8):
        b = core // q_per_b
        q = core % q_per_b
        o = res.results[core]["outT"].reshape(c.dm, c.s_own)
        out[b, q * c.s_own:(q + 1) * c.s_own, :] = o.T
    return out


# version 14
# speedup vs baseline: 1.0297x; 1.0297x over previous
import math
from contextlib import ExitStack

import numpy as np
import ml_dtypes

import concourse.bass as bass
import concourse.mybir as mybir
from concourse.bass_utils import run_bass_kernel_spmd

F32 = mybir.dt.float32
BF16 = mybir.dt.bfloat16
I32 = mybir.dt.int32
AF = mybir.ActivationFunctionType
ALU = mybir.AluOpType

ENGINES = ("tensor", "vector", "scalar", "gpsimd", "sync")
DMA_SEM = {"sync": "dS", "gpsimd": "dG", "scalar": "dA"}
ENG_SEM = {"tensor": "tS", "vector": "vS", "scalar": "aS", "gpsimd": "gS",
           "sync": "sS"}


class Emitter:
    def __init__(self, nc):
        self.nc = nc
        self.q = {e: [] for e in ENGINES}
        self.sems = {}
        self.counts = {}
        self.last_wait = {}

    def sem(self, name):
        if name not in self.sems:
            self.sems[name] = self.nc.alloc_semaphore(name)
            self.counts[name] = 0
        return self.sems[name]

    def wait(self, eng, ms):
        if ms is None:
            return
        semname, val = ms
        if val <= 0:
            return
        key = (eng, semname)
        if self.last_wait.get(key, 0) >= val:
            return
        self.last_wait[key] = val
        s = self.sem(semname)
        self.q[eng].append(lambda E, s=s, v=val: E.wait_ge(s, v))

    def emit(self, eng, fn, inc):
        semname, k = inc
        s = self.sem(semname)
        self.counts[semname] += k

        def wrapped(E, fn=fn, s=s, k=k):
            fn(E).then_inc(s, k)

        self.q[eng].append(wrapped)
        return (semname, self.counts[semname])

    def run(self, block):
        for e in ENGINES:
            fns = self.q[e]

            def body(E, fns=fns):
                for f in fns:
                    f(E)

            getattr(block, e)(body)


N_DMA_SEMS = 8


class SB:

    def __init__(self, em):
        self.em = em
        self.state = {}
        self.rr = {}

    def _st(self, r):
        if r not in self.state:
            self.state[r] = {"w": None, "r": []}
        return self.state[r]

    def _wait(self, eng, ms):
        if ms is None:
            return
        if eng == "tensor" and ms[0] == "tS":
            return
        self.em.wait(eng, ms)

    def op(self, eng, fn, reads=(), writes=(), dma=False):
        for r in reads:
            self._wait(eng, self._st(r)["w"])
        for r in writes:
            st = self._st(r)
            self._wait(eng, st["w"])
            for ms in st["r"]:
                self._wait(eng, ms)
        if dma:
            q = DMA_SEM[eng]
            k = self.rr.get(q, 0)
            self.rr[q] = (k + 1) % N_DMA_SEMS
            sem = f"{q}{k}"
            prev = self.em.counts.get(sem, 0)
            if prev:
                self.em.wait(eng, (sem, prev))
            ms = self.em.emit(eng, fn, inc=(sem, 16))
        else:
            ms = self.em.emit(eng, fn, inc=(ENG_SEM[eng], 1))
        for r in writes:
            st = self._st(r)
            st["w"] = ms
            st["r"] = []
        for r in reads:
            self._st(r)["r"].append(ms)
        return ms


class Cfg:
    def __init__(self, dm=2048, s_full=8192, s_own=2048, inter=5504):
        self.dm = dm
        self.nt = dm // 128
        self.hp = self.nt // 2
        self.s_full = s_full
        self.s_own = s_own
        self.inter = inter
        self.it = inter // 128
        self.cw = 512 if s_own % 512 == 0 else 128
        self.n_ch = s_full // self.cw
        self.n_oc = s_own // self.cw
        self.n_kc = s_full // 128
        self.scale = 1.0 / math.sqrt(128.0)
        self.eps = 1e-5
        self.ngu = self.it * self.nt
        self.nd = self.nt * self.it
        self.dec_slab = 8


def pages(name, lo, hi, grain=2048):
    return [f"{name}_p{p}" for p in range(lo // grain,
                                          (hi + grain - 1) // grain)]


def build_program(c, lut_vals):
    nc = bass.Bass("TRN2", target_bir_lowering=False)
    cw = c.cw
    nt, hp, it = c.nt, c.hp, c.it

    xT = nc.dram_tensor("xT", [nt, 128, c.s_full], F32,
                        kind="ExternalInput").ap()
    xownT = nc.dram_tensor("xownT", [nt, 128, c.s_own], F32,
                           kind="ExternalInput").ap()
    wqT = nc.dram_tensor("wqT", [c.dm, c.dm], F32, kind="ExternalInput").ap()
    wkT = nc.dram_tensor("wkT", [c.dm, c.dm], F32, kind="ExternalInput").ap()
    wvT = nc.dram_tensor("wvT", [c.dm, c.dm], F32, kind="ExternalInput").ap()
    woT = nc.dram_tensor("woT", [c.dm, c.dm], F32, kind="ExternalInput").ap()
    gate_in = nc.dram_tensor("gate", [1, nt], F32, kind="ExternalInput").ap()
    cosq = nc.dram_tensor("cosq", [hp, 128, c.s_own], F32,
                          kind="ExternalInput").ap()
    sinq = nc.dram_tensor("sinq", [hp, 128, c.s_own], F32,
                          kind="ExternalInput").ap()
    cosk = nc.dram_tensor("cosk", [hp, 128, c.s_full], F32,
                          kind="ExternalInput").ap()
    sink = nc.dram_tensor("sink", [hp, 128, c.s_full], F32,
                          kind="ExternalInput").ap()
    gidx = nc.dram_tensor("gidx", [128, c.ngu, 128], I32,
                          kind="ExternalInput").ap()
    uidx = nc.dram_tensor("uidx", [128, c.ngu, 128], I32,
                          kind="ExternalInput").ap()
    didx = nc.dram_tensor("didx", [128, c.nd, 128], I32,
                          kind="ExternalInput").ap()
    out = nc.dram_tensor("outT", [nt, 128, c.s_own], F32,
                         kind="ExternalOutput").ap()

    wq_bf = nc.dram_tensor("wq_bf", [128, nt, c.dm], BF16).ap()
    wk_bf = nc.dram_tensor("wk_bf", [128, nt, c.dm], BF16).ap()
    wo_bf = nc.dram_tensor("wo_bf", [128, nt, c.dm], BF16).ap()
    wv_bf = nc.dram_tensor("wv_bf", [128, nt, c.dm], BF16).ap()
    xn_hb = nc.dram_tensor("xn_hb", [nt, 128, c.s_full], BF16).ap()
    kT_h = nc.dram_tensor("kT_h", [nt, 128, c.s_full], BF16).ap()
    qT_h = nc.dram_tensor("qT_h", [nt, 128, c.s_own], BF16).ap()
    v_hb = nc.dram_tensor("v_hb", [c.s_full, c.dm], BF16).ap()
    a_hb = nc.dram_tensor("a_hb", [nt, 128, c.s_own], BF16).ap()
    r2_hb = nc.dram_tensor("r2_hb", [nt, 128, c.s_own], F32).ap()
    f_hb = nc.dram_tensor("f_hb", [nt, 128, c.s_own], BF16).ap()
    wg_hb = nc.dram_tensor("wg_hb", [128, c.ngu, 128], BF16).ap()
    wu_hb = nc.dram_tensor("wu_hb", [128, c.ngu, 128], BF16).ap()
    wd_hb = nc.dram_tensor("wd_hb", [128, c.nd, 128], BF16).ap()

    ident_d = nc.inline_tensor(np.eye(128, dtype=ml_dtypes.bfloat16),
                               "ident").ap()

    stack = ExitStack()

    def sbuf(name, shape, dt):
        return stack.enter_context(nc.sbuf_tensor(name, shape, dt))

    with stack:
        big0_w = max(2 * nt * cw + (cw // 128) * c.dm,
                     3 * c.s_full,
                     hp * c.n_oc * cw,
                     it * cw + 2 * cw)
        big0 = sbuf("big0", [128, big0_w], BF16)
        big1 = sbuf("big1", [128, max(nt * c.s_own, nt * c.dm)], BF16)
        wp = sbuf("wp", [128, 8 * cw], BF16)
        mid_w = max(4 * cw, 2 * c.s_own + 3 * 2 * cw)
        mid = sbuf("mid", [128, mid_w], BF16)
        stat_ld = sbuf("stat_ld", [128, cw], F32)
        stat_sq = sbuf("stat_sq", [128, cw], BF16)
        tf1 = sbuf("tf1", [128, cw], F32)
        tf2 = sbuf("tf2", [128, cw], F32)
        tf3 = sbuf("tf3", [128, cw], F32)
        tf4 = sbuf("tf4", [128, cw], F32)
        wbuf0 = sbuf("wbuf0", [128, c.dm], BF16)
        wbuf1 = sbuf("wbuf1", [128, c.dm], BF16)
        memory = sbuf("memory", [128, nt * 129], F32)
        memb = sbuf("memb", [128, 132], BF16)
        dec_bs = [sbuf("dec_b0", [128, 128 * c.dec_slab], BF16),
                  sbuf("dec_b1", [128, 128 * c.dec_slab], BF16)]
        dec_ts = [sbuf("dec_t0", [128, 128 * c.dec_slab], BF16),
                  sbuf("dec_t1", [128, 128 * c.dec_slab], BF16)]
        dec_as = [sbuf("dec_a0", [128, 128 * c.dec_slab], BF16),
                  sbuf("dec_a1", [128, 128 * c.dec_slab], BF16)]
        R0 = sbuf("R0", [1, c.s_own], F32)
        R1 = sbuf("R1", [1, c.s_own], F32)
        R2 = sbuf("R2", [1, c.s_own], F32)
        R0b = sbuf("R0b", [1, cw], F32)
        R1b = sbuf("R1b", [1, cw], F32)
        R2b = sbuf("R2b", [1, cw], F32)
        tf5 = sbuf("tf5", [128, cw], F32)
        mq2 = sbuf("mq2", [1, nt], F32)
        mk2 = sbuf("mk2", [1, nt], F32)
        g_row = sbuf("g_row", [1, nt], F32)
        og_row = sbuf("og_row", [1, nt], F32)
        gcol = sbuf("gcol", [1, 128], F32)
        ogcol = sbuf("ogcol", [1, 128], F32)
        ones_row = sbuf("ones_row", [1, 128], F32)
        ones_col = sbuf("ones_col", [128, 1], BF16)
        biasC = sbuf("biasC", [128, 1], F32)
        sc1 = sbuf("sc1", [1, 8], F32)
        ident_s = sbuf("ident_s", [128, 128], BF16)

        ps = stack.enter_context(nc.psum_tensor("ps", [128, 4096], F32))
        psb = ps[:, 3584:3584 + cw // 2].bitcast(BF16)

        def Q4(i, w=None):
            w = cw if w is None else w
            return ps[:, i * 512:i * 512 + w]

        def QPAIR(par):
            base = par * 1024
            return ps[:, base:base + 1024].rearrange(
                "p (b x) -> p b x", x=512)[:, :, 0:cw]

        DEN = ps[0:1, 3584:3584 + cw]
        PVPS = ps[:, 3072:3072 + cw]
        MEMPS = ps[:, 3072:3201]

        em = Emitter(nc)
        sb = SB(em)
        OP = sb.op

        def dma(eng, dst, src, reads=(), writes=()):
            return OP(eng, lambda E, d=dst, s=src: E.dma_start(out=d, in_=s),
                      reads=reads, writes=writes, dma=True)

        QA = wp[:, 0:cw]
        QB = wp[:, cw:2 * cw]
        QO = wp[:, 2 * cw:3 * cw]
        QO2 = wp[:, 3 * cw:4 * cw]
        K3A = wp[:, 4 * cw:5 * cw]
        K2A = wp[:, 5 * cw:6 * cw]
        K2B = wp[:, 6 * cw:7 * cw]
        K3B = wp[:, 7 * cw:8 * cw]
        sqb = tf2.bitcast(BF16)[:, 0:cw]
        tf1b = tf1.bitcast(BF16)[:, 0:cw]
        ldb_v = stat_ld.bitcast(BF16)[:, 0:cw]
        midf = mid.bitcast(F32)
        cos_s = [midf[:, 1024 * p:1024 * p + cw] for p in range(3)]
        sin_s = [midf[:, 1024 * p + cw:1024 * p + 2 * cw] for p in range(3)]
        m_cos_s = [pages("mid", 2048 * p, 2048 * p + 1024, 1024)
                   for p in range(3)]
        m_sin_s = [pages("mid", 2048 * p + 1024, 2048 * p + 2048, 1024)
                   for p in range(3)]
        cosb, sinb = cos_s[0], sin_s[0]
        m_cosb, m_sinb = m_cos_s[0], m_sin_s[0]
        SQK = mid[:, 6144:6144 + cw]
        m_SQK = pages("mid", 6144, 6144 + cw, 1024)

        OP("vector", lambda V: V.memset(ones_col[:, :], 1.0), writes=["ones"])
        OP("vector", lambda V: V.memset(ones_row[:, :], 1.0), writes=["onesr"])
        OP("vector", lambda V: V.memset(memory[:, :], 0.0), writes=["memory"])
        OP("vector", lambda V: V.memset(mq2[:, :], 0.0), writes=["mq2"])
        OP("vector", lambda V: V.memset(mk2[:, :], 0.0), writes=["mk2"])
        dma("sync", ident_s[:, :], ident_d[:, :], writes=["ident"])
        dma("sync", g_row[:, :], gate_in[:, :], writes=["grow"])
        OP("scalar", lambda A: A.activation(og_row[:, :], g_row[:, :], AF.Exp,
                                            scale=-1.0),
           reads=["grow"], writes=["ogrow"])
        OP("vector", lambda V: V.tensor_scalar_add(og_row[:, :], og_row[:, :],
                                                   1.0),
           reads=["ogrow"], writes=["ogrow"])
        OP("vector", lambda V: V.reciprocal(g_row[:, :], og_row[:, :]),
           reads=["ogrow"], writes=["grow"])
        OP("vector", lambda V: V.tensor_scalar(og_row[:, :], g_row[:, :],
                                               -1.0, 1.0, ALU.mult, ALU.add),
           reads=["grow"], writes=["ogrow"])
        for (wsrc, wdst, res) in ((wqT, wq_bf, "wqb"), (wkT, wk_bf, "wkb"),
                                  (woT, wo_bf, "wob")):
            s4 = wsrc.rearrange("(hc p) (m f) -> p m hc f", p=128, f=128)
            d4 = wdst.rearrange("p m (hc f) -> p m hc f", f=128)
            for m in range(nt):
                dma("gpsimd", d4[:, m], s4[:, m], writes=[res])
        sv = wvT.rearrange("(hc p) f -> p hc f", p=128)
        for hc in range(nt):
            dma("gpsimd", wv_bf[:, hc], sv[:, hc], writes=["wvb"])

        dec_jobs = []
        for src, dst, ntile in ((gidx, wg_hb, c.ngu), (uidx, wu_hb, c.ngu),
                                (didx, wd_hb, c.nd)):
            for a0 in range(0, ntile, c.dec_slab):
                w = min(c.dec_slab, ntile - a0) * 128
                dec_jobs.append((src[:, a0:a0 + w // 128, :],
                                 dst[:, a0:a0 + w // 128, :], w))

        def _dec_load(n):
            src, dst, w = dec_jobs[n]
            par = n % 2
            src2 = src.rearrange("p t f -> p (t f)")
            dma("gpsimd", dec_bs[par][:, 0:w], src2,
                writes=[f"dec_b{par}"])

        def _dec_jg(n, k):
            src, dst, w = dec_jobs[n]
            par = n % 2
            add_eng = "gpsimd" if n % 4 == 0 else "vector"
            da, dares = dec_as[par], f"dec_a{par}"
            db, dbres = dec_bs[par], f"dec_b{par}"
            for j in range(4 * k, 4 * k + 4):
                if j == 0:
                    OP("vector", lambda V, w=w, da=da, db=db: V.tensor_scalar(
                        da[:, 0:w], db[:, 0:w], 0.0, float(lut_vals[0]),
                        ALU.is_equal, ALU.mult),
                       reads=[dbres], writes=[dares])
                else:
                    tb, tres = dec_ts[j % 2], f"dec_t{j % 2}"
                    OP("vector", lambda V, w=w, j=j, tb=tb, db=db:
                       V.tensor_scalar(
                        tb[:, 0:w], db[:, 0:w], float(j),
                        float(lut_vals[j]), ALU.is_equal, ALU.mult),
                       reads=[dbres], writes=[tres])
                    OP(add_eng, lambda E, w=w, da=da, tb=tb: E.tensor_add(
                        da[:, 0:w], da[:, 0:w], tb[:, 0:w]),
                       reads=[tres, dares], writes=[dares])

        def _dec_store(n):
            src, dst, w = dec_jobs[n]
            par = n % 2
            dst2 = dst.rearrange("p t f -> p (t f)")
            dma("gpsimd", dst2, dec_as[par][:, 0:w],
                reads=[f"dec_a{par}"], writes=["dec_o"])

        dec_seq = []
        N_dec = len(dec_jobs)
        dec_seq.append(lambda: _dec_load(0))
        for n in range(N_dec):
            if n + 1 < N_dec:
                dec_seq.append(lambda n=n: _dec_load(n + 1))
            for k in range(4):
                dec_seq.append(lambda n=n, k=k: _dec_jg(n, k))
            dec_seq.append(lambda n=n: _dec_store(n))
        dec_pos = [0]

        def dec_steps(n):
            while n > 0 and dec_pos[0] < len(dec_seq):
                dec_seq[dec_pos[0]]()
                dec_pos[0] += 1
                n -= 1

        def dec_flush():
            dec_steps(len(dec_seq))

        RS = (((R0, "R0"), (R1, "R1"), (R2, "R2")),
              ((R0b, "R0b"), (R1b, "R1b"), (R2b, "R2b")))

        def ln_finalize(width, par=0, off=0):
            (Rx0, r0), (Rx1, r1), (Rx2, r2) = RS[par]
            sl = slice(off, off + width)
            inv_n = 1.0 / c.dm
            OP("vector", lambda V: V.tensor_scalar_mul(
                Rx0[0:1, sl], Rx0[0:1, sl], inv_n),
               reads=[r0], writes=[r0])
            OP("vector", lambda V: V.tensor_scalar_mul(
                Rx1[0:1, sl], Rx1[0:1, sl], inv_n),
               reads=[r1], writes=[r1])
            OP("vector", lambda V: V.tensor_mul(
                Rx2[0:1, sl], Rx0[0:1, sl], Rx0[0:1, sl]),
               reads=[r0], writes=[r2])
            OP("vector", lambda V: V.tensor_sub(
                Rx1[0:1, sl], Rx1[0:1, sl], Rx2[0:1, sl]),
               reads=[r1, r2], writes=[r1])
            OP("vector", lambda V: V.tensor_scalar_add(
                Rx1[0:1, sl], Rx1[0:1, sl], float(c.eps)),
               reads=[r1], writes=[r1])
            OP("scalar", lambda A: A.activation(
                Rx1[0:1, sl], Rx1[0:1, sl], AF.Ln),
               reads=[r1], writes=[r1])
            OP("scalar", lambda A: A.activation(
                Rx1[0:1, sl], Rx1[0:1, sl], AF.Exp, scale=-0.5),
               reads=[r1], writes=[r1])
            OP("vector", lambda V: V.tensor_mul(
                Rx2[0:1, sl], Rx0[0:1, sl], Rx1[0:1, sl]),
               reads=[r0, r1], writes=[r2])

        def bcast_rows(width, off=0, par=0, banks=(4, 5)):
            (Rx0, r0), (Rx1, r1), (Rx2, r2) = RS[par]
            b0, b1 = banks
            OP("tensor", lambda T: T.matmul(
                Q4(b0, width), ones_row[:, :], Rx1[0:1, off:off + width],
                start=True, stop=True),
               reads=["onesr", r1], writes=[f"PSq{b0}"])
            OP("tensor", lambda T: T.matmul(
                Q4(b1, width), ones_row[:, :], Rx2[0:1, off:off + width],
                start=True, stop=True),
               reads=["onesr", r2], writes=[f"PSq{b1}"])

        def stats_tiles(load_fn, keep, t0, t1):
            ld_bufs = ((stat_ld[:, 0:cw], "stld"), (tf1[:, 0:cw], "tf1"))
            sq_bufs = ((stat_sq[:, 0:cw], "stsq"), (sqb, "tf2"))
            for t in range(t0, t1):
                ld, ldres = ld_bufs[t % 2]
                sq, sqres = sq_bufs[t % 2]
                kd, kres = keep(t)
                dma("sync", ld, load_fn(t), writes=[ldres])
                OP("scalar", lambda A, ld=ld, kd=kd: A.activation(
                    kd, ld, AF.Copy), reads=[ldres], writes=kres)
                OP("scalar", lambda A, ld=ld, sq=sq: A.activation(
                    sq, ld, AF.Square), reads=[ldres], writes=[sqres])
                OP("tensor", lambda T, kd=kd, t=t: T.matmul(
                    Q4(2)[0:1, :], ones_col[:, :], kd,
                    start=(t == 0), stop=(t == nt - 1)),
                   reads=["ones"] + kres, writes=["PSq2"])
                OP("tensor", lambda T, sq=sq, t=t: T.matmul(
                    Q4(3)[0:1, :], ones_col[:, :], sq,
                    start=(t == 0), stop=(t == nt - 1)),
                   reads=["ones", sqres], writes=["PSq3"])

        def stats_rows(par=0, off=0):
            (Rx0, r0), (Rx1, r1), _ = RS[par]
            OP("vector", lambda V: V.tensor_copy(Rx0[0:1, off:off + cw],
                                                 Q4(2)[0:1, :]),
               reads=["PSq2"], writes=[r0])
            OP("vector", lambda V: V.tensor_copy(Rx1[0:1, off:off + cw],
                                                 Q4(3)[0:1, :]),
               reads=["PSq3"], writes=[r1])

        def chunk_stats(load_fn, keep, par=0):
            stats_tiles(load_fn, keep, 0, nt)
            stats_rows(par)

        def hb_stats(src_hb, src_f32, extra_reads=()):
            for s in range(c.n_oc):
                sl = slice(s * cw, (s + 1) * cw)
                for t in range(nt):
                    if src_f32:
                        ld, ldres = ((stat_ld[:, 0:cw], "stld") if t % 2 == 0
                                     else (tf1[:, 0:cw], "tf1"))
                        dma("sync", ld, src_hb[t][:, sl],
                            reads=list(extra_reads), writes=[ldres])
                        mi, mires = ((stat_sq[:, 0:cw], "stsq") if t % 2 == 0
                                     else (sqb, "tf2"))
                        OP("scalar", lambda A, ld=ld, mi=mi: A.activation(
                            mi, ld, AF.Copy), reads=[ldres], writes=[mires])
                        OP("tensor", lambda T, mi=mi, t=t: T.matmul(
                            Q4(4)[0:1, :], ones_col[:, :], mi,
                            start=(t == 0), stop=(t == nt - 1)),
                           reads=["ones", mires], writes=["PSq4"])
                        OP("scalar", lambda A, ld=ld, mi=mi: A.activation(
                            mi, ld, AF.Square), reads=[ldres, mires],
                           writes=[mires])
                        OP("tensor", lambda T, mi=mi, t=t: T.matmul(
                            Q4(5)[0:1, :], ones_col[:, :], mi,
                            start=(t == 0), stop=(t == nt - 1)),
                           reads=["ones", mires], writes=["PSq5"])
                    else:
                        ld, ldres = ((ldb_v, "stld") if t % 2 == 0
                                     else (tf1b, "tf1"))
                        sq, sqres = ((stat_sq[:, 0:cw], "stsq") if t % 2 == 0
                                     else (sqb, "tf2"))
                        dma("sync", ld, src_hb[t][:, sl],
                            reads=list(extra_reads), writes=[ldres])
                        OP("scalar", lambda A, ld=ld, sq=sq: A.activation(
                            sq, ld, AF.Square), reads=[ldres],
                           writes=[sqres])
                        OP("tensor", lambda T, ld=ld, t=t: T.matmul(
                            Q4(4)[0:1, :], ones_col[:, :], ld,
                            start=(t == 0), stop=(t == nt - 1)),
                           reads=["ones", ldres], writes=["PSq4"])
                        OP("tensor", lambda T, sq=sq, t=t: T.matmul(
                            Q4(5)[0:1, :], ones_col[:, :], sq,
                            start=(t == 0), stop=(t == nt - 1)),
                           reads=["ones", sqres], writes=["PSq5"])
                OP("vector", lambda V, sl=sl: V.tensor_copy(
                    R0[0:1, sl], Q4(4)[0:1, :]), reads=["PSq4"],
                   writes=["R0"])
                OP("vector", lambda V, sl=sl: V.tensor_copy(
                    R1[0:1, sl], Q4(5)[0:1, :]), reads=["PSq5"],
                   writes=["R1"])

        def xn_own(t, s):
            lo = t * c.s_own + s * cw
            return big1[:, lo:lo + cw], pages("big1", lo, lo + cw)

        for s in range(c.n_oc):
            sl = slice(s * cw, (s + 1) * cw)
            chunk_stats(lambda t, sl=sl: xownT[t][:, sl],
                        lambda t, s=s: xn_own(t, s), par=s % 2)
            ln_finalize(cw, par=s % 2)
            bcast_rows(cw, par=s % 2)
            for t in range(nt):
                xn, xnres = xn_own(t, s)
                OP("vector", lambda V, xn=xn: V.tensor_mul(
                    tf3[:, 0:cw], xn, Q4(4)),
                   reads=xnres + ["PSq4"], writes=["tf3"])
                OP("vector", lambda V, xn=xn: V.tensor_sub(
                    xn, tf3[:, 0:cw], Q4(5)),
                   reads=["tf3", "PSq5"], writes=xnres)
            dec_steps(4)

        def wload(wbf, m, buf, res, src_res):
            dma("sync", buf[:, :], wbf[:, m, :], reads=[src_res],
                writes=[res])

        def rope_pair(getA, resA, outA, outAres, outB, outBres,
                      getB=None, resB=None, cs=None):
            getB = QB if getB is None else getB
            resB = ["QB"] if resB is None else resB
            cosv, sinv, m_cv, m_sv = ((cosb, sinb, m_cosb, m_sinb)
                                      if cs is None else cs)
            OP("vector", lambda V, aA=getA, cosv=cosv: V.tensor_mul(
                tf1[:, 0:cw], aA, cosv),
               reads=list(resA) + m_cv, writes=["tf1"])
            OP("vector", lambda V, aB=getB, sinv=sinv: V.tensor_mul(
                tf2[:, 0:cw], aB, sinv),
               reads=list(resB) + m_sv, writes=["tf2"])
            OP("vector", lambda V, o=outA: V.tensor_sub(o, tf1[:, 0:cw],
                                                        tf2[:, 0:cw]),
               reads=["tf1", "tf2"], writes=outAres)
            OP("vector", lambda V, aB=getB, cosv=cosv: V.tensor_mul(
                tf1[:, 0:cw], aB, cosv),
               reads=list(resB) + m_cv, writes=["tf1"])
            OP("vector", lambda V, aA=getA, sinv=sinv: V.tensor_mul(
                tf2[:, 0:cw], aA, sinv),
               reads=list(resA) + m_sv, writes=["tf2"])
            OP("vector", lambda V, o=outB: V.tensor_add(o, tf1[:, 0:cw],
                                                        tf2[:, 0:cw]),
               reads=["tf1", "tf2"], writes=outBres)

        def row_norm_max(src, sres, dstrow, dres, mh):
            OP("scalar", lambda A, src=src: A.activation(
                stat_sq[:, 0:cw], src, AF.Square),
               reads=sres, writes=["stsq"])
            OP("tensor", lambda T: T.matmul(
                Q4(2)[0:1, :], ones_col[:, :], stat_sq[:, 0:cw],
                start=True, stop=True),
               reads=["ones", "stsq"], writes=["PSq2"])
            OP("scalar", lambda A: A.activation(R2[0:1, 0:cw],
                                                Q4(2)[0:1, :], AF.Copy),
               reads=["PSq2"], writes=["R2"])
            OP("vector", lambda V: V.reduce_max(
                sc1[0:1, 0:1], R2[0:1, 0:cw], axis=mybir.AxisListType.X),
               reads=["R2"], writes=["sc1"])
            OP("vector", lambda V, mh=mh, dr=dstrow: V.tensor_max(
                dr[0:1, mh:mh + 1], dr[0:1, mh:mh + 1], sc1[0:1, 0:1]),
               reads=["sc1", dres], writes=[dres])

        wload(wq_bf, 0, wbuf0, "wb0", "wqb")
        for tp in range(hp):
            wload(wq_bf, tp + hp, wbuf1, "wb1", "wqb")
            for s in range(c.n_oc):
                for hc in range(nt):
                    rhs, rres = xn_own(hc, s)
                    OP("tensor", lambda T, hc=hc, rhs=rhs: T.matmul(
                        Q4(0), wbuf0[:, hc * 128:(hc + 1) * 128], rhs,
                        start=(hc == 0), stop=(hc == nt - 1)),
                       reads=["wb0"] + rres, writes=["PSq0"])
                lo = (tp * c.n_oc + s) * cw
                st_ = big0[:, lo:lo + cw]
                stres = pages("big0", lo, lo + cw)
                OP("scalar", lambda A, st_=st_: A.activation(st_, Q4(0),
                                                             AF.Copy),
                   reads=["PSq0"], writes=stres)
            if tp + 1 < hp:
                wload(wq_bf, tp + 1, wbuf0, "wb0", "wqb")
            for s in range(c.n_oc):
                sl = slice(s * cw, (s + 1) * cw)
                for hc in range(nt):
                    rhs, rres = xn_own(hc, s)
                    OP("tensor", lambda T, hc=hc, rhs=rhs: T.matmul(
                        Q4(1), wbuf1[:, hc * 128:(hc + 1) * 128], rhs,
                        start=(hc == 0), stop=(hc == nt - 1)),
                       reads=["wb1"] + rres, writes=["PSq1"])
                OP("scalar", lambda A: A.activation(QB, Q4(1), AF.Copy),
                   reads=["PSq1"], writes=["QB"])
                dma("sync", cosb, cosq[tp][:, sl], writes=m_cosb)
                dma("sync", sinb, sinq[tp][:, sl], writes=m_sinb)
                lo = (tp * c.n_oc + s) * cw
                st_ = big0[:, lo:lo + cw]
                stres = pages("big0", lo, lo + cw)
                rope_pair(st_, stres, QO, ["QO"], QO2, ["QO2"])
                dma("sync", qT_h[tp][:, sl], QO, reads=["QO"], writes=["qTh"])
                dma("sync", qT_h[tp + hp][:, sl], QO2, reads=["QO2"],
                    writes=["qTh"])
                row_norm_max(QO, ["QO"], mq2, "mq2", tp)
                row_norm_max(QO2, ["QO2"], mq2, "mq2", tp + hp)
            dec_steps(2)

        def xn_area(t, apar):
            lo = (apar * nt + t) * cw
            return big0[:, lo:lo + cw], pages("big0", lo, lo + cw)

        wv_b1 = big1[:, 0:nt * c.dm]
        m_wv = pages("big1", 0, nt * c.dm)
        dma("sync", wv_b1.rearrange("p (hc f) -> p hc f", f=c.dm), wv_bf,
            reads=["wvb"], writes=m_wv)
        n_sub = cw // 128
        vch_lo = 2 * nt * cw
        VB = (0, 1, 6, 7)

        def vch_sl(sub, d0, d1):
            lo = vch_lo + sub * c.dm + d0
            return (big0[:, lo:lo + (d1 - d0)],
                    pages("big0", lo, lo + (d1 - d0)))

        def c12_stats_part(ch, t0, t1):
            sl_f = slice(ch * cw, (ch + 1) * cw)
            apar = ch % 2
            stats_tiles(lambda t, sl_f=sl_f: xT[t][:, sl_f],
                        lambda t, apar=apar: xn_area(t, apar), t0, t1)

        def c12_norm(ch):
            sl_f = slice(ch * cw, (ch + 1) * cw)
            apar = ch % 2
            stats_rows(par=apar)
            ln_finalize(cw, par=apar)
            bcast_rows(cw, par=apar)
            for t in range(nt):
                xn, xnres = xn_area(t, apar)
                OP("vector", lambda V, xn=xn: V.tensor_mul(
                    tf3[:, 0:cw], xn, Q4(4)),
                   reads=xnres + ["PSq4"], writes=["tf3"])
                OP("vector", lambda V, xn=xn: V.tensor_sub(
                    xn, tf3[:, 0:cw], Q4(5)),
                   reads=["tf3", "PSq5"], writes=xnres)
                dma("sync", xn_hb[t][:, sl_f], xn, reads=xnres,
                    writes=["xnhb"])

        def c12_vmms(ch, dq):
            apar = ch % 2
            for sub in range(n_sub):
                bank = VB[sub]
                for hc in range(nt):
                    rhs = wv_b1[:, hc * c.dm + dq * 512:
                                hc * c.dm + (dq + 1) * 512]
                    xs, xsres = xn_area(hc, apar)
                    lhs = xs[:, sub * 128:(sub + 1) * 128]
                    OP("tensor", lambda T, hc=hc, bank=bank, rhs=rhs,
                       lhs=lhs: T.matmul(
                        Q4(bank, 512), lhs, rhs,
                        start=(hc == 0), stop=(hc == nt - 1)),
                       reads=xsres + m_wv, writes=[f"PSq{bank}"])
            for sub in range(n_sub):
                bank = VB[sub]
                vd, vdres = vch_sl(sub, dq * 512, (dq + 1) * 512)
                OP("scalar", lambda A, vd=vd, bank=bank: A.activation(
                    vd, Q4(bank, 512), AF.Copy),
                   reads=[f"PSq{bank}"], writes=vdres)

        c12_stats_part(0, 0, nt)
        c12_norm(0)
        for ch in range(c.n_ch):
            for dq in range(c.dm // 512):
                if ch + 1 < c.n_ch:
                    tpg = nt // (c.dm // 512)
                    c12_stats_part(ch + 1, dq * tpg, (dq + 1) * tpg)
                c12_vmms(ch, dq)
                dec_steps(5)
            for sub in range(n_sub):
                vrow, vrres = vch_sl(sub, 0, c.dm)
                dma("sync", v_hb[ch * cw + sub * 128:
                                 ch * cw + (sub + 1) * 128, :],
                    vrow, reads=vrres, writes=["vhb"])
            if ch + 1 < c.n_ch:
                c12_norm(ch + 1)
            dec_steps(6)

        wk_b1 = big1[:, 0:nt * c.dm]
        m_wk = pages("big1", 0, nt * c.dm)
        dma("sync", wk_b1.rearrange("p (m f) -> p m f", f=c.dm), wk_bf,
            reads=["wkb"] + m_wv, writes=m_wk)

        kslots = [((QA, "QA"), (QB, "QB")), ((K2A, "K2A"), (K2B, "K2B")),
                  ((K3A, "K3A"), (K3B, "K3B"))]

        def c3_xn_loads(ch):
            apar = ch % 2
            sl_f = slice(ch * cw, (ch + 1) * cw)
            for t in range(nt):
                xs, xsres = xn_area(t, apar)
                dma("sync", xs, xn_hb[t][:, sl_f], reads=["xnhb"],
                    writes=xsres)

        def c3_vch_loads(ch):
            for sub in range(n_sub):
                vrow, vrres = vch_sl(sub, 0, c.dm)
                dma("sync", vrow,
                    v_hb[ch * cw + sub * 128:ch * cw + (sub + 1) * 128, :],
                    reads=["vhb"], writes=vrres)

        def c3_front_pre(u):
            ch, tp = divmod(u, hp)
            par = u % 3
            sl_f = slice(ch * cw, (ch + 1) * cw)
            if tp == min(4, hp - 1) and ch + 1 < c.n_ch:
                c3_xn_loads(ch + 1)
            dma("sync", cos_s[par], cosk[tp][:, sl_f], writes=m_cos_s[par])
            dma("sync", sin_s[par], sink[tp][:, sl_f], writes=m_sin_s[par])

        def c3_kmm(u, half):
            ch, tp = divmod(u, hp)
            apar = ch % 2
            bank = half
            m = tp + half * hp
            wslab = wk_b1[:, m * c.dm:(m + 1) * c.dm]
            for hc in range(nt):
                xs, xsres = xn_area(hc, apar)
                OP("tensor", lambda T, hc=hc, wslab=wslab, bank=bank,
                   xnv=xs: T.matmul(
                    Q4(bank), wslab[:, hc * 128:(hc + 1) * 128],
                    xnv, start=(hc == 0), stop=(hc == nt - 1)),
                   reads=m_wk + xsres, writes=[f"PSq{bank}"])

        def c3_evac(u):
            par = u % 3
            (kA, kAres), (kB, kBres) = kslots[par]
            for (bank, dst, dres) in ((0, kA, kAres), (1, kB, kBres)):
                OP("scalar", lambda A, dst=dst, bank=bank: A.activation(
                    dst, Q4(bank), AF.Copy),
                   reads=[f"PSq{bank}"], writes=[dres])

        SQK2 = ldb_v
        m_SQK2 = ["stld"]

        def c3_back1(u):
            ch, tp = divmod(u, hp)
            par = u % 3
            sl_f = slice(ch * cw, (ch + 1) * cw)
            (kA, kAres), (kB, kBres) = kslots[par]
            rope_pair(kA, [kAres], QO, ["QO"], QO2, ["QO2"],
                      getB=kB, resB=[kBres],
                      cs=(cos_s[par], sin_s[par], m_cos_s[par],
                          m_sin_s[par]))
            for (kout, kres, mh) in ((QO, "QO", tp),
                                     (QO2, "QO2", tp + hp)):
                dma("sync", kT_h[mh][:, sl_f], kout, reads=[kres],
                    writes=["kTh"])
            for sub in range(n_sub):
                OP("tensor", lambda T, sub=sub:
                   T.transpose(psb[:, sub * 128:(sub + 1) * 128],
                               QO[:, sub * 128:(sub + 1) * 128],
                               ident_s[:, :]),
                   reads=["QO", "ident"], writes=["PSq7"])
            OP("scalar", lambda A: A.activation(SQK, psb[:, 0:cw],
                                                AF.Copy),
               reads=["PSq7"], writes=m_SQK)
            for sub in range(n_sub):
                OP("tensor", lambda T, sub=sub:
                   T.transpose(psb[:, sub * 128:(sub + 1) * 128],
                               QO2[:, sub * 128:(sub + 1) * 128],
                               ident_s[:, :]),
                   reads=["QO2", "ident"], writes=["PSq7"])
            OP("scalar", lambda A: A.activation(SQK2, psb[:, 0:cw],
                                                AF.Copy),
               reads=["PSq7"], writes=m_SQK2)
            OP("vector", lambda V: V.tensor_scalar_min(
                tf3[:, 0:cw], SQK, 0.0), reads=m_SQK, writes=["tf3"])
            OP("vector", lambda V: V.tensor_scalar_min(
                tf5[:, 0:cw], SQK2, 0.0), reads=m_SQK2, writes=["tf5"])
            OP("scalar", lambda A: A.activation(
                tf4[:, 0:cw], tf3[:, 0:cw], AF.Exp),
               reads=["tf3"], writes=["tf4"])
            OP("scalar", lambda A: A.activation(
                tf2[:, 0:cw], tf5[:, 0:cw], AF.Exp),
               reads=["tf5"], writes=["tf2"])
            OP("vector", lambda V: V.tensor_scalar_add(
                tf3[:, 0:cw], SQK, 1.0), reads=m_SQK, writes=["tf3"])
            OP("vector", lambda V: V.tensor_max(
                SQK, tf3[:, 0:cw], tf4[:, 0:cw]),
               reads=["tf3", "tf4"], writes=m_SQK)
            OP("vector", lambda V: V.tensor_scalar_add(
                tf5[:, 0:cw], SQK2, 1.0), reads=m_SQK2, writes=["tf5"])
            OP("vector", lambda V: V.tensor_max(
                SQK2, tf5[:, 0:cw], tf2[:, 0:cw]),
               reads=["tf5", "tf2"], writes=m_SQK2)

        def c3_back2(u):
            ch, tp = divmod(u, hp)
            for (kf, kfm, mh) in ((SQK, m_SQK, tp), (SQK2, m_SQK2, tp + hp)):
                for sub in range(n_sub):
                    lhs = kf[:, sub * 128:(sub + 1) * 128]
                    rhs, rres = vch_sl(sub, mh * 128, (mh + 1) * 128)
                    OP("tensor", lambda T, lhs=lhs, rhs=rhs, sub=sub:
                       T.matmul(MEMPS[:, 0:128], lhs, rhs,
                                start=(sub == 0), stop=False),
                       reads=kfm + rres, writes=["PSq6"])
                    OP("tensor", lambda T, lhs=lhs, sub=sub: T.matmul(
                        MEMPS[:, 128:129], lhs, ones_col[:, :],
                        start=False, stop=(sub == n_sub - 1)),
                       reads=kfm + ["ones"], writes=["PSq6"])
                msl = slice(mh * 129, (mh + 1) * 129)
                OP("vector", lambda V, msl=msl: V.tensor_add(
                    memory[:, msl], memory[:, msl], MEMPS[:, 0:129]),
                   reads=["PSq6"], writes=["memory"])
            row_norm_max(QO, ["QO"], mk2, "mk2", tp)
            row_norm_max(QO2, ["QO2"], mk2, "mk2", tp + hp)
            if tp == hp - 1 and ch + 1 < c.n_ch:
                c3_vch_loads(ch + 1)
            dec_steps(1)

        c3_xn_loads(0)
        c3_vch_loads(0)
        n_u = c.n_ch * hp
        for u in range(n_u + 2):
            if 1 <= u <= n_u:
                c3_evac(u - 1)
            if u < n_u:
                c3_front_pre(u)
                c3_kmm(u, 0)
            if u >= 2:
                c3_back1(u - 2)
            if u < n_u:
                c3_kmm(u, 1)
            if u >= 2:
                c3_back2(u - 2)

        OP("vector", lambda V: V.reduce_max(sc1[0:1, 0:1], mq2[0:1, :],
                                            axis=mybir.AxisListType.X),
           reads=["mq2"], writes=["sc1"])
        OP("vector", lambda V: V.reduce_max(sc1[0:1, 1:2], mk2[0:1, :],
                                            axis=mybir.AxisListType.X),
           reads=["mk2"], writes=["sc1"])
        OP("vector", lambda V: V.tensor_mul(sc1[0:1, 2:3], sc1[0:1, 0:1],
                                            sc1[0:1, 1:2]),
           reads=["sc1"], writes=["sc1"])
        OP("scalar", lambda A: A.activation(sc1[0:1, 3:4], sc1[0:1, 2:3],
                                            AF.Ln), reads=["sc1"],
           writes=["sc1"])
        OP("scalar", lambda A: A.activation(sc1[0:1, 3:4], sc1[0:1, 3:4],
                                            AF.Exp, scale=0.5),
           reads=["sc1"], writes=["sc1"])
        OP("vector", lambda V: V.tensor_scalar_mul(sc1[0:1, 3:4],
                                                   sc1[0:1, 3:4],
                                                   -float(c.scale)),
           reads=["sc1"], writes=["sc1"])
        OP("tensor", lambda T: T.matmul(Q4(4, 1)[:, 0:1], ones_row[:, :],
                                        sc1[0:1, 3:4], start=True, stop=True),
           reads=["onesr", "sc1"], writes=["PSq4"])
        OP("vector", lambda V: V.tensor_copy(biasC[:, :], Q4(4, 1)[:, 0:1]),
           reads=["PSq4"], writes=["biasC"])

        qbuf = mid[:, 0:c.s_own]
        qf = mid[:, c.s_own:2 * c.s_own]
        pbs = [mid[:, 2 * c.s_own + i * 2 * cw:
                   2 * c.s_own + (i + 1) * 2 * cw] for i in range(3)]
        m_qbuf = pages("mid", 0, c.s_own, 1024)
        m_qf = pages("mid", c.s_own, 2 * c.s_own, 1024)
        m_pbs = [pages("mid", 2 * c.s_own + i * 2 * cw,
                       2 * c.s_own + (i + 1) * 2 * cw, 1024)
                 for i in range(3)]

        kbuf = big0[:, 0:c.s_full]
        vb0 = big0[:, c.s_full:2 * c.s_full]
        vb1 = big0[:, 2 * c.s_full:3 * c.s_full]
        m_kbuf = pages("big0", 0, c.s_full)
        m_vb = [pages("big0", c.s_full, 2 * c.s_full),
                pages("big0", 2 * c.s_full, 3 * c.s_full)]

        v_re = v_hb.rearrange("(cb p) d -> p cb d", p=128)
        n_cb = c.s_full // 128

        def load_vhead(h, vb, vres):
            npiece = 8 if n_cb % 8 == 0 else 1
            step = n_cb // npiece
            for pc in range(npiece):
                src = v_re[:, pc * step:(pc + 1) * step,
                           h * 128:(h + 1) * 128]
                dst = vb[:, pc * step * 128:(pc + 1) * step * 128]
                dst3 = dst.rearrange("p (cb d) -> p cb d", d=128)
                dma("sync", dst3, src, reads=["vhb"], writes=vres)

        load_vhead(0, vb0, m_vb[0])
        for h in range(nt):
            vcur, m_vcur = (vb0, m_vb[0]) if h % 2 == 0 else (vb1, m_vb[1])
            if h + 1 < nt:
                vnxt, m_vnxt = (vb1, m_vb[1]) if h % 2 == 0 else \
                    (vb0, m_vb[0])
                load_vhead(h + 1, vnxt, m_vnxt)
            for pc in range(c.s_full // 2048):
                dma("sync", kbuf[:, pc * 2048:(pc + 1) * 2048],
                    kT_h[h][:, pc * 2048:(pc + 1) * 2048], reads=["kTh"],
                    writes=pages("big0", pc * 2048, (pc + 1) * 2048))
            for pc in range(c.s_own // 1024):
                dma("sync", qbuf[:, pc * 1024:(pc + 1) * 1024],
                    qT_h[h][:, pc * 1024:(pc + 1) * 1024], reads=["qTh"],
                    writes=pages("mid", pc * 1024, (pc + 1) * 1024, 1024))
            for s in range(c.n_oc):
                qsl = qbuf[:, s * cw:(s + 1) * cw]
                m_qs = pages("mid", s * cw, (s + 1) * cw, 1024)
                OP("vector", lambda V, qsl=qsl: V.tensor_scalar_min(
                    tf1[:, 0:cw], qsl, 0.0), reads=m_qs, writes=["tf1"])
                OP("scalar", lambda A: A.activation(tf2[:, 0:cw],
                                                    tf1[:, 0:cw], AF.Exp),
                   reads=["tf1"], writes=["tf2"])
                OP("vector", lambda V, qsl=qsl: V.tensor_scalar_add(
                    tf1[:, 0:cw], qsl, 1.0), reads=m_qs, writes=["tf1"])
                OP("vector", lambda V, s=s: V.tensor_max(
                    qf[:, s * cw:(s + 1) * cw], tf1[:, 0:cw], tf2[:, 0:cw]),
                   reads=["tf1", "tf2"], writes=m_qf)
            OP("vector", lambda V, h=h: V.tensor_copy(
                memb[:, 0:129], memory[:, h * 129:(h + 1) * 129]),
               reads=["memory"], writes=["memb"])
            OP("tensor", lambda T, h=h: T.matmul(
                Q4(4, 128)[0:1, :], g_row[0:1, h:h + 1], ones_row[:, :],
                start=True, stop=True),
               reads=["grow", "onesr"], writes=["PSq4"])
            OP("vector", lambda V: V.tensor_copy(gcol[0:1, :],
                                                 Q4(4, 128)[0:1, :]),
               reads=["PSq4"], writes=["gcol"])
            OP("tensor", lambda T, h=h: T.matmul(
                Q4(4, 128)[0:1, :], og_row[0:1, h:h + 1], ones_row[:, :],
                start=True, stop=True),
               reads=["ogrow", "onesr"], writes=["PSq4"])
            OP("vector", lambda V: V.tensor_copy(ogcol[0:1, :],
                                                 Q4(4, 128)[0:1, :]),
               reads=["PSq4"], writes=["ogcol"])
            def mk_qb(qb, vcur, m_vcur):
                qsl = slice(qb * cw, (qb + 1) * cw)
                m_q = pages("mid", qb * cw, (qb + 1) * cw, 1024)

                def emit_sc(g):
                    par = g % 3
                    for half in range(2):
                        kc = 2 * g + half
                        bank = 2 * par + half
                        m_k = pages("big0", kc * 128, (kc + 1) * 128)
                        OP("tensor", lambda T, kc=kc, bank=bank:
                           T.matmul(Q4(bank),
                                    kbuf[:, kc * 128:(kc + 1) * 128],
                                    qbuf[:, qsl], start=True, stop=True),
                           reads=m_k + m_q, writes=[f"PSq{bank}"])

                def emit_exp(g):
                    par = g % 3
                    pbx, m_pbx = pbs[par], m_pbs[par]
                    pbx3 = pbx.rearrange("p (b x) -> p b x", x=cw)
                    OP("scalar", lambda A, par=par, pbx3=pbx3: A.activation(
                        pbx3, QPAIR(par), AF.Exp, bias=biasC[:, 0:1],
                        scale=c.scale),
                       reads=[f"PSq{2 * par}", f"PSq{2 * par + 1}", "biasC"],
                       writes=m_pbx)

                def emit_pv(g):
                    par = g % 3
                    pbx, m_pbx = pbs[par], m_pbs[par]
                    for half in range(2):
                        kc = 2 * g + half
                        OP("tensor", lambda T, kc=kc, half=half, pbx=pbx:
                           T.matmul(
                            PVPS, vcur[:, kc * 128:(kc + 1) * 128],
                            pbx[:, half * cw:(half + 1) * cw],
                            start=(kc == 0), stop=(kc == c.n_kc - 1)),
                           reads=m_vcur + m_pbx, writes=["PSq6"])
                        OP("tensor", lambda T, kc=kc, half=half, pbx=pbx:
                           T.matmul(DEN, ones_col[:, :],
                                    pbx[:, half * cw:(half + 1) * cw],
                                    start=(kc == 0),
                                    stop=(kc == c.n_kc - 1)),
                           reads=["ones"] + m_pbx, writes=["PSq7"])

                def emit_lin():
                    OP("tensor", lambda T: T.matmul(
                        PVPS, memb[:, 0:128], qf[:, qsl],
                        start=True, stop=True),
                       reads=["memb"] + m_qf, writes=["PSq6"])
                    OP("scalar", lambda A: A.activation(tf3[:, 0:cw], PVPS,
                                                        AF.Copy),
                       reads=["PSq6"], writes=["tf3"])
                    OP("tensor", lambda T: T.matmul(
                        DEN, memb[:, 128:129], qf[:, qsl],
                        start=True, stop=True),
                       reads=["memb"] + m_qf, writes=["PSq7"])
                    OP("vector", lambda V: V.tensor_copy(
                        R1[0:1, qsl], DEN), reads=["PSq7"], writes=["R1"])
                    OP("vector", lambda V: V.reciprocal(
                        R1[0:1, qsl], R1[0:1, qsl]), reads=["R1"],
                       writes=["R1"])

                def emit_epi():
                    OP("vector", lambda V: V.tensor_copy(
                        R0[0:1, qsl], DEN), reads=["PSq7"], writes=["R0"])
                    OP("scalar", lambda A: A.activation(
                        tf4[:, 0:cw], PVPS, AF.Copy),
                       reads=["PSq6"], writes=["tf4"])
                    OP("vector", lambda V: V.reciprocal(
                        R0[0:1, qsl], R0[0:1, qsl]), reads=["R0"],
                       writes=["R0"])
                    OP("tensor", lambda T: T.matmul(
                        PVPS, gcol[0:1, :], R1[0:1, qsl],
                        start=True, stop=True),
                       reads=["gcol", "R1"], writes=["PSq6"])
                    OP("tensor", lambda T: T.matmul(
                        Q4(7), ogcol[0:1, :], R0[0:1, qsl],
                        start=True, stop=True),
                       reads=["ogcol", "R0"], writes=["PSq7"])
                    lo = h * c.s_own + qb * cw
                    cmb = big1[:, lo:lo + cw]
                    cres = pages("big1", lo, lo + cw)
                    OP("vector", lambda V: V.tensor_mul(
                        tf3[:, 0:cw], tf3[:, 0:cw], PVPS),
                       reads=["tf3", "PSq6"], writes=["tf3"])
                    OP("vector", lambda V: V.tensor_mul(
                        tf4[:, 0:cw], tf4[:, 0:cw], Q4(7)),
                       reads=["tf4", "PSq7"], writes=["tf4"])
                    OP("vector", lambda V, cmb=cmb, cres=cres: V.tensor_add(
                        cmb, tf4[:, 0:cw], tf3[:, 0:cw]),
                       reads=["tf3", "tf4"], writes=cres)

                return emit_sc, emit_exp, emit_pv, emit_lin, emit_epi

            ng = c.n_kc // 2
            prev_epi = None
            for qb in range(c.n_oc):
                e_sc, e_exp, e_pv, e_lin, e_epi = mk_qb(qb, vcur, m_vcur)
                e_sc(0)
                e_sc(1)
                e_exp(0)
                if prev_epi is not None:
                    prev_epi()
                e_lin()
                for g in range(2, ng + 2):
                    if g < ng:
                        e_sc(g)
                        if g in (2, 10, 18):
                            dec_steps(4)
                        elif g == 25:
                            dec_steps(2)
                    if g - 1 < ng:
                        e_exp(g - 1)
                    e_pv(g - 2)
                prev_epi = e_epi
            prev_epi()
            dec_steps(4)

        def park_a(s, ob):
            lo = (s % 2) * nt * cw + ob * cw
            return big0[:, lo:lo + cw], pages("big0", lo, lo + cw)

        def park_r2(t):
            lo = 2 * nt * cw + t * cw
            return big0[:, lo:lo + cw], pages("big0", lo, lo + cw)

        def p4_front(s):
            for ob in range(nt):
                g = s * nt + ob
                wb, wres = (wbuf0, "wb0") if g % 2 == 0 else (wbuf1, "wb1")
                if g == 0:
                    wload(wo_bf, 0, wbuf0, "wb0", "wob")
                if g + 1 < c.n_oc * nt:
                    nwb, nwres = ((wbuf1, "wb1") if g % 2 == 0 else
                                  (wbuf0, "wb0"))
                    wload(wo_bf, (g + 1) % nt, nwb, nwres, "wob")
                bank = ob % 2
                for ic in range(nt):
                    lo = ic * c.s_own + s * cw
                    cap = big1[:, lo:lo + cw]
                    cres = pages("big1", lo, lo + cw)
                    OP("tensor", lambda T, ic=ic, bank=bank, wb=wb, cap=cap:
                       T.matmul(Q4(bank), wb[:, ic * 128:(ic + 1) * 128],
                                cap, start=(ic == 0), stop=(ic == nt - 1)),
                       reads=[wres] + cres, writes=[f"PSq{bank}"])
                pk, pkres = park_a(s, ob)
                OP("scalar", lambda A, pk=pk, bank=bank: A.activation(
                    pk, Q4(bank), AF.Copy),
                   reads=[f"PSq{bank}"], writes=pkres)
                sq, sqres = ((stat_sq[:, 0:cw], "stsq") if ob % 2 == 0
                             else (sqb, "tf2"))
                OP("scalar", lambda A, pk=pk, sq=sq: A.activation(
                    sq, pk, AF.Square), reads=pkres, writes=[sqres])
                OP("tensor", lambda T, pk=pk, ob=ob: T.matmul(
                    Q4(2)[0:1, :], ones_col[:, :], pk,
                    start=(ob == 0), stop=(ob == nt - 1)),
                   reads=["ones"] + pkres, writes=["PSq2"])
                OP("tensor", lambda T, sq=sq, ob=ob: T.matmul(
                    Q4(3)[0:1, :], ones_col[:, :], sq,
                    start=(ob == 0), stop=(ob == nt - 1)),
                   reads=["ones", sqres], writes=["PSq3"])
                dec_steps(1)

        def p4_mid1(s):
            stats_rows(par=0, off=s * cw)
            ln_finalize(cw, par=0, off=s * cw)
            bcast_rows(cw, off=s * cw, banks=(6, 7))

        def p4_back1(s):
            sl = slice(s * cw, (s + 1) * cw)
            for t in range(nt):
                stg, stres, rsd, rres = (
                    (tf3, "tf3", tf4, "tf4") if t % 2 == 0
                    else (stat_ld, "stld", tf5, "tf5"))
                pk, pkres = park_a(s, t)
                dma("sync", rsd[:, 0:cw], xownT[t][:, sl], writes=[rres])
                OP("vector", lambda V, pk=pk, stg=stg: V.tensor_mul(
                    stg[:, 0:cw], pk, Q4(6)),
                   reads=pkres + ["PSq6"], writes=[stres])
                OP("vector", lambda V, stg=stg: V.tensor_sub(
                    stg[:, 0:cw], stg[:, 0:cw], Q4(7)),
                   reads=[stres, "PSq7"], writes=[stres])
                OP("vector", lambda V, stg=stg, rsd=rsd: V.tensor_add(
                    stg[:, 0:cw], stg[:, 0:cw], rsd[:, 0:cw]),
                   reads=[stres, rres], writes=[stres])
                dma("sync", r2_hb[t][:, sl], stg[:, 0:cw], reads=[stres],
                    writes=["r2hb"])
                pr, prres = park_r2(t)
                OP("scalar", lambda A, stg=stg, pr=pr: A.activation(
                    pr, stg[:, 0:cw], AF.Copy),
                   reads=[stres], writes=prres)
                sq, sqres = ((stat_sq[:, 0:cw], "stsq") if t % 2 == 0
                             else (sqb, "tf2"))
                OP("scalar", lambda A, stg=stg, sq=sq: A.activation(
                    sq, stg[:, 0:cw], AF.Square),
                   reads=[stres], writes=[sqres])
                OP("tensor", lambda T, pr=pr, t=t: T.matmul(
                    Q4(4)[0:1, :], ones_col[:, :], pr,
                    start=(t == 0), stop=(t == nt - 1)),
                   reads=["ones"] + prres, writes=["PSq4"])
                OP("tensor", lambda T, sq=sq, t=t: T.matmul(
                    Q4(5)[0:1, :], ones_col[:, :], sq[:, 0:cw],
                    start=(t == 0), stop=(t == nt - 1)),
                   reads=["ones", sqres], writes=["PSq5"])
                dec_steps(1)
            OP("vector", lambda V, s=s: V.tensor_copy(
                R0[0:1, s * cw:(s + 1) * cw], Q4(4)[0:1, :]),
               reads=["PSq4"], writes=["R0"])
            OP("vector", lambda V, s=s: V.tensor_copy(
                R1[0:1, s * cw:(s + 1) * cw], Q4(5)[0:1, :]),
               reads=["PSq5"], writes=["R1"])

        def p4_mid2(s):
            ln_finalize(cw, par=0, off=s * cw)
            bcast_rows(cw, off=s * cw, banks=(6, 7))

        def p4_back2(s):
            for t in range(nt):
                stg, stres = ((tf3, "tf3") if t % 2 == 0 else (tf2, "tf2"))
                pr, prres = park_r2(t)
                OP("vector", lambda V, pr=pr, stg=stg: V.tensor_mul(
                    stg[:, 0:cw], pr, Q4(6)),
                   reads=prres + ["PSq6"], writes=[stres])
                lo = t * c.s_own + s * cw
                xn2 = big1[:, lo:lo + cw]
                x2res = pages("big1", lo, lo + cw)
                OP("vector", lambda V, xn2=xn2, stg=stg: V.tensor_sub(
                    xn2, stg[:, 0:cw], Q4(7)),
                   reads=[stres, "PSq7"], writes=x2res)
            dec_steps(2)

        p4_front(0)
        for s in range(c.n_oc):
            if s >= 1:
                p4_mid2(s - 1)
                p4_back2(s - 1)
            p4_mid1(s)
            if s + 1 < c.n_oc:
                p4_front(s + 1)
            p4_back1(s)
        p4_mid2(c.n_oc - 1)
        p4_back2(c.n_oc - 1)

        def xn2_t(t, s):
            lo = t * c.s_own + s * cw
            return big1[:, lo:lo + cw], pages("big1", lo, lo + cw)

        hbufs = [(big0[:, ic * cw:(ic + 1) * cw],
                  pages("big0", ic * cw, (ic + 1) * cw)) for ic in range(it)]
        fstg_lo = it * cw
        fstg = big0[:, fstg_lo:fstg_lo + cw]
        m_fstg = pages("big0", fstg_lo, fstg_lo + cw)
        sgb = big0[:, fstg_lo + cw:fstg_lo + 2 * cw]
        m_sgb = pages("big0", fstg_lo + cw, fstg_lo + 2 * cw)

        for sbi in range(c.n_oc):
            sl = slice(sbi * cw, (sbi + 1) * cw)
            for ic in range(it):
                wg_s = wg_hb[:, ic * nt:(ic + 1) * nt, :].rearrange(
                    "p t f -> p (t f)")
                wu_s = wu_hb[:, ic * nt:(ic + 1) * nt, :].rearrange(
                    "p t f -> p (t f)")
                dma("sync", wbuf0[:, :], wg_s, reads=["dec_o"],
                    writes=["wb0"])
                dma("sync", wbuf1[:, :], wu_s, reads=["dec_o"],
                    writes=["wb1"])
                for hc in range(nt):
                    x2, x2res = xn2_t(hc, sbi)
                    OP("tensor", lambda T, hc=hc, x2=x2: T.matmul(
                        Q4(0), wbuf0[:, hc * 128:(hc + 1) * 128], x2,
                        start=(hc == 0), stop=(hc == nt - 1)),
                       reads=["wb0"] + x2res, writes=["PSq0"])
                for hc in range(nt):
                    x2, x2res = xn2_t(hc, sbi)
                    OP("tensor", lambda T, hc=hc, x2=x2: T.matmul(
                        Q4(1), wbuf1[:, hc * 128:(hc + 1) * 128], x2,
                        start=(hc == 0), stop=(hc == nt - 1)),
                       reads=["wb1"] + x2res, writes=["PSq1"])
                OP("scalar", lambda A: A.activation(sgb, Q4(0), AF.Silu),
                   reads=["PSq0"], writes=m_sgb)
                hb, hres = hbufs[ic]
                OP("vector", lambda V, hb=hb: V.tensor_mul(hb, sgb, Q4(1)),
                   reads=m_sgb + ["PSq1"], writes=hres)
                if sbi == 0:
                    dec_steps(4)
            if sbi == 0:
                dec_flush()
            for ob in range(nt):
                bank = 2 + ob % 2
                for icg in range(0, it, 16):
                    ilen = min(16, it - icg)
                    wd_s = wd_hb[:, ob * it + icg:ob * it + icg + ilen, :]\
                        .rearrange("p t f -> p (t f)")
                    wb, wres = ((wbuf0, "wb0") if (icg // 16) % 2 == 0 else
                                (wbuf1, "wb1"))
                    dma("sync", wb[:, 0:ilen * 128], wd_s, reads=["dec_o"],
                        writes=[wres])
                    for i2 in range(ilen):
                        ic = icg + i2
                        hb, hres = hbufs[ic]
                        OP("tensor", lambda T, i2=i2, ic=ic, hb=hb, wb=wb,
                           bank=bank: T.matmul(
                            Q4(bank), wb[:, i2 * 128:(i2 + 1) * 128], hb,
                            start=(ic == 0), stop=(ic == it - 1)),
                           reads=[wres] + hres, writes=[f"PSq{bank}"])
                OP("scalar", lambda A, bank=bank: A.activation(
                    fstg, Q4(bank), AF.Copy),
                   reads=[f"PSq{bank}"], writes=m_fstg)
                dma("sync", f_hb[ob][:, sl], fstg, reads=m_fstg,
                    writes=["fhb"])
                sq, sqres = ((stat_sq[:, 0:cw], "stsq") if ob % 2 == 0
                             else (sqb, "tf2"))
                OP("scalar", lambda A, sq=sq: A.activation(
                    sq, fstg, AF.Square), reads=m_fstg, writes=[sqres])
                OP("tensor", lambda T, ob=ob: T.matmul(
                    Q4(4)[0:1, :], ones_col[:, :], fstg,
                    start=(ob == 0), stop=(ob == nt - 1)),
                   reads=["ones"] + m_fstg, writes=["PSq4"])
                OP("tensor", lambda T, sq=sq, ob=ob: T.matmul(
                    Q4(5)[0:1, :], ones_col[:, :], sq,
                    start=(ob == 0), stop=(ob == nt - 1)),
                   reads=["ones", sqres], writes=["PSq5"])
            OP("vector", lambda V, sbi=sbi: V.tensor_copy(
                R0[0:1, sbi * cw:(sbi + 1) * cw], Q4(4)[0:1, :]),
               reads=["PSq4"], writes=["R0"])
            OP("vector", lambda V, sbi=sbi: V.tensor_copy(
                R1[0:1, sbi * cw:(sbi + 1) * cw], Q4(5)[0:1, :]),
               reads=["PSq5"], writes=["R1"])

        ln_finalize(c.s_own)
        for s in range(c.n_oc):
            sl = slice(s * cw, (s + 1) * cw)
            bcast_rows(cw, off=s * cw)
            for t in range(nt):
                ld, ldres, stg, stres, rsd, rres = (
                    (ldb_v, "stld", tf3, "tf3", tf4, "tf4") if t % 2 == 0
                    else (tf1b, "tf1", tf2, "tf2", tf5, "tf5"))
                dma("sync", ld, f_hb[t][:, sl], reads=["fhb"],
                    writes=[ldres])
                OP("vector", lambda V, ld=ld, stg=stg: V.tensor_mul(
                    stg[:, 0:cw], ld, Q4(4)),
                   reads=[ldres, "PSq4"], writes=[stres])
                OP("vector", lambda V, stg=stg: V.tensor_sub(
                    stg[:, 0:cw], stg[:, 0:cw], Q4(5)),
                   reads=[stres, "PSq5"], writes=[stres])
                dma("sync", rsd[:, 0:cw], r2_hb[t][:, sl],
                    reads=["r2hb"], writes=[rres])
                OP("vector", lambda V, stg=stg, rsd=rsd: V.tensor_add(
                    stg[:, 0:cw], stg[:, 0:cw], rsd[:, 0:cw]),
                   reads=[stres, rres], writes=[stres])
                dma("sync", out[t][:, sl], stg[:, 0:cw], reads=[stres],
                    writes=["outd"])

        for eng in ENGINES:
            for q in ("dS", "dG", "dA"):
                for k in range(N_DMA_SEMS):
                    semname = f"{q}{k}"
                    if semname in em.counts:
                        em.wait(eng, (semname, em.counts[semname]))

        with nc.Block() as block:
            em.run(block)

    return nc


def _stage_inputs(c, inputs):
    x = np.asarray(inputs["L0_attn_hidden_states"])
    B = x.shape[0]
    lut = np.asarray(inputs["L0_ffn_lut"]).astype(np.float32)

    wqT = np.ascontiguousarray(np.asarray(inputs["L0_attn_Wq"]).T)
    wkT = np.ascontiguousarray(np.asarray(inputs["L0_attn_Wk"]).T)
    wvT = np.ascontiguousarray(np.asarray(inputs["L0_attn_Wv"]).T)
    woT = np.ascontiguousarray(np.asarray(inputs["L0_attn_Wo"]).T)
    gate = np.ascontiguousarray(
        np.asarray(inputs["L0_attn_gate"]).reshape(1, c.nt).astype(np.float32))

    inv = (1.0 / (10000.0 ** (np.arange(0, c.dm, 2, dtype=np.float64)
                              / c.dm))).astype(np.float64)
    t = np.arange(c.s_full, dtype=np.float64)
    fr = np.outer(inv, t)
    cosk = np.ascontiguousarray(
        np.cos(fr).astype(np.float32).reshape(c.hp, 128, c.s_full))
    sink = np.ascontiguousarray(
        np.sin(fr).astype(np.float32).reshape(c.hp, 128, c.s_full))

    gi = np.asarray(inputs["L0_ffn_gate_idx"]).astype(np.int32)
    ui = np.asarray(inputs["L0_ffn_up_idx"]).astype(np.int32)
    di = np.asarray(inputs["L0_ffn_down_idx"]).astype(np.int32)
    G4 = gi.reshape(c.it, 128, c.nt, 128)
    gidx = np.ascontiguousarray(
        G4.transpose(3, 0, 2, 1).reshape(128, c.ngu, 128))
    U4 = ui.reshape(c.it, 128, c.nt, 128)
    uidx = np.ascontiguousarray(
        U4.transpose(3, 0, 2, 1).reshape(128, c.ngu, 128))
    D4 = di.reshape(c.nt, 128, c.it, 128)
    didx = np.ascontiguousarray(
        D4.transpose(3, 0, 2, 1).reshape(128, c.nd, 128))

    n_cores = 8
    q_per_b = n_cores // B
    in_maps = []
    for core in range(n_cores):
        b = core // q_per_b
        q = core % q_per_b
        xTb = np.ascontiguousarray(x[b].T).reshape(c.nt, 128, c.s_full)
        xown = np.ascontiguousarray(
            x[b, q * c.s_own:(q + 1) * c.s_own, :].T).reshape(
            c.nt, 128, c.s_own)
        cq = np.ascontiguousarray(
            cosk[:, :, q * c.s_own:(q + 1) * c.s_own])
        sq = np.ascontiguousarray(
            sink[:, :, q * c.s_own:(q + 1) * c.s_own])
        in_maps.append({
            "xT": xTb, "xownT": xown, "wqT": wqT, "wkT": wkT, "wvT": wvT,
            "woT": woT, "gate": gate, "cosq": cq, "sinq": sq,
            "cosk": cosk, "sink": sink, "gidx": gidx, "uidx": uidx,
            "didx": didx,
        })
    return in_maps, lut


def kernel(**inputs):
    c = Cfg()
    in_maps, lut = _stage_inputs(c, inputs)
    nc = build_program(c, [float(v) for v in lut])
    res = run_bass_kernel_spmd(nc, in_maps, core_ids=list(range(8)))
    x = np.asarray(inputs["L0_attn_hidden_states"])
    B, S, H = x.shape
    out = np.empty((B, S, H), dtype=np.float32)
    q_per_b = 8 // B
    for core in range(8):
        b = core // q_per_b
        q = core % q_per_b
        o = res.results[core]["outT"].reshape(c.dm, c.s_own)
        out[b, q * c.s_own:(q + 1) * c.s_own, :] = o.T
    return out


# version 15
# speedup vs baseline: 1.0330x; 1.0032x over previous
import math
from contextlib import ExitStack

import numpy as np
import ml_dtypes

import concourse.bass as bass
import concourse.mybir as mybir
from concourse.bass_utils import run_bass_kernel_spmd

F32 = mybir.dt.float32
BF16 = mybir.dt.bfloat16
I32 = mybir.dt.int32
AF = mybir.ActivationFunctionType
ALU = mybir.AluOpType

ENGINES = ("tensor", "vector", "scalar", "gpsimd", "sync")
DMA_SEM = {"sync": "dS", "gpsimd": "dG", "scalar": "dA"}
ENG_SEM = {"tensor": "tS", "vector": "vS", "scalar": "aS", "gpsimd": "gS",
           "sync": "sS"}


class Emitter:
    def __init__(self, nc):
        self.nc = nc
        self.q = {e: [] for e in ENGINES}
        self.sems = {}
        self.counts = {}
        self.last_wait = {}

    def sem(self, name):
        if name not in self.sems:
            self.sems[name] = self.nc.alloc_semaphore(name)
            self.counts[name] = 0
        return self.sems[name]

    def wait(self, eng, ms):
        if ms is None:
            return
        semname, val = ms
        if val <= 0:
            return
        key = (eng, semname)
        if self.last_wait.get(key, 0) >= val:
            return
        self.last_wait[key] = val
        s = self.sem(semname)
        self.q[eng].append(lambda E, s=s, v=val: E.wait_ge(s, v))

    def emit(self, eng, fn, inc):
        semname, k = inc
        s = self.sem(semname)
        self.counts[semname] += k

        def wrapped(E, fn=fn, s=s, k=k):
            fn(E).then_inc(s, k)

        self.q[eng].append(wrapped)
        return (semname, self.counts[semname])

    def run(self, block):
        for e in ENGINES:
            fns = self.q[e]

            def body(E, fns=fns):
                for f in fns:
                    f(E)

            getattr(block, e)(body)


N_DMA_SEMS = 8


class SB:

    def __init__(self, em):
        self.em = em
        self.state = {}
        self.rr = {}

    def _st(self, r):
        if r not in self.state:
            self.state[r] = {"w": None, "r": []}
        return self.state[r]

    def _wait(self, eng, ms):
        if ms is None:
            return
        if eng == "tensor" and ms[0] == "tS":
            return
        self.em.wait(eng, ms)

    def op(self, eng, fn, reads=(), writes=(), dma=False):
        for r in reads:
            self._wait(eng, self._st(r)["w"])
        for r in writes:
            st = self._st(r)
            self._wait(eng, st["w"])
            for ms in st["r"]:
                self._wait(eng, ms)
        if dma:
            q = DMA_SEM[eng]
            k = self.rr.get(q, 0)
            self.rr[q] = (k + 1) % N_DMA_SEMS
            sem = f"{q}{k}"
            prev = self.em.counts.get(sem, 0)
            if prev:
                self.em.wait(eng, (sem, prev))
            ms = self.em.emit(eng, fn, inc=(sem, 16))
        else:
            ms = self.em.emit(eng, fn, inc=(ENG_SEM[eng], 1))
        for r in writes:
            st = self._st(r)
            st["w"] = ms
            st["r"] = []
        for r in reads:
            self._st(r)["r"].append(ms)
        return ms


class Cfg:
    def __init__(self, dm=2048, s_full=8192, s_own=2048, inter=5504):
        self.dm = dm
        self.nt = dm // 128
        self.hp = self.nt // 2
        self.s_full = s_full
        self.s_own = s_own
        self.inter = inter
        self.it = inter // 128
        self.cw = 512 if s_own % 512 == 0 else 128
        self.n_ch = s_full // self.cw
        self.n_oc = s_own // self.cw
        self.n_kc = s_full // 128
        self.scale = 1.0 / math.sqrt(128.0)
        self.eps = 1e-5
        self.ngu = self.it * self.nt
        self.nd = self.nt * self.it
        self.dec_slab = 8


def pages(name, lo, hi, grain=2048):
    return [f"{name}_p{p}" for p in range(lo // grain,
                                          (hi + grain - 1) // grain)]


def build_program(c, lut_vals):
    nc = bass.Bass("TRN2", target_bir_lowering=False)
    cw = c.cw
    nt, hp, it = c.nt, c.hp, c.it

    xT = nc.dram_tensor("xT", [nt, 128, c.s_full], F32,
                        kind="ExternalInput").ap()
    xownT = nc.dram_tensor("xownT", [nt, 128, c.s_own], F32,
                           kind="ExternalInput").ap()
    wqT = nc.dram_tensor("wqT", [c.dm, c.dm], F32, kind="ExternalInput").ap()
    wkT = nc.dram_tensor("wkT", [c.dm, c.dm], F32, kind="ExternalInput").ap()
    wvT = nc.dram_tensor("wvT", [c.dm, c.dm], F32, kind="ExternalInput").ap()
    woT = nc.dram_tensor("woT", [c.dm, c.dm], F32, kind="ExternalInput").ap()
    gate_in = nc.dram_tensor("gate", [1, nt], F32, kind="ExternalInput").ap()
    cosq = nc.dram_tensor("cosq", [hp, 128, c.s_own], F32,
                          kind="ExternalInput").ap()
    sinq = nc.dram_tensor("sinq", [hp, 128, c.s_own], F32,
                          kind="ExternalInput").ap()
    cosk = nc.dram_tensor("cosk", [hp, 128, c.s_full], F32,
                          kind="ExternalInput").ap()
    sink = nc.dram_tensor("sink", [hp, 128, c.s_full], F32,
                          kind="ExternalInput").ap()
    gidx = nc.dram_tensor("gidx", [128, c.ngu, 128], I32,
                          kind="ExternalInput").ap()
    uidx = nc.dram_tensor("uidx", [128, c.ngu, 128], I32,
                          kind="ExternalInput").ap()
    didx = nc.dram_tensor("didx", [128, c.nd, 128], I32,
                          kind="ExternalInput").ap()
    out = nc.dram_tensor("outT", [nt, 128, c.s_own], F32,
                         kind="ExternalOutput").ap()

    wq_bf = nc.dram_tensor("wq_bf", [128, nt, c.dm], BF16).ap()
    wk_bf = nc.dram_tensor("wk_bf", [128, nt, c.dm], BF16).ap()
    wo_bf = nc.dram_tensor("wo_bf", [128, nt, c.dm], BF16).ap()
    wv_bf = nc.dram_tensor("wv_bf", [128, nt, c.dm], BF16).ap()
    xn_hb = nc.dram_tensor("xn_hb", [nt, 128, c.s_full], BF16).ap()
    kT_h = nc.dram_tensor("kT_h", [nt, 128, c.s_full], BF16).ap()
    qT_h = nc.dram_tensor("qT_h", [nt, 128, c.s_own], BF16).ap()
    v_hb = nc.dram_tensor("v_hb", [c.s_full, c.dm], BF16).ap()
    a_hb = nc.dram_tensor("a_hb", [nt, 128, c.s_own], BF16).ap()
    r2_hb = nc.dram_tensor("r2_hb", [nt, 128, c.s_own], F32).ap()
    f_hb = nc.dram_tensor("f_hb", [nt, 128, c.s_own], BF16).ap()
    wg_hb = nc.dram_tensor("wg_hb", [128, c.ngu, 128], BF16).ap()
    wu_hb = nc.dram_tensor("wu_hb", [128, c.ngu, 128], BF16).ap()
    wd_hb = nc.dram_tensor("wd_hb", [128, c.nd, 128], BF16).ap()

    ident_d = nc.inline_tensor(np.eye(128, dtype=ml_dtypes.bfloat16),
                               "ident").ap()

    stack = ExitStack()

    def sbuf(name, shape, dt):
        return stack.enter_context(nc.sbuf_tensor(name, shape, dt))

    with stack:
        big0_w = max(2 * nt * cw + (cw // 128) * c.dm,
                     3 * c.s_full,
                     hp * c.n_oc * cw,
                     it * cw + 2 * cw)
        big0 = sbuf("big0", [128, big0_w], BF16)
        big1 = sbuf("big1", [128, max(nt * c.s_own, nt * c.dm)], BF16)
        wp = sbuf("wp", [128, 8 * cw], BF16)
        mid_w = max(4 * cw, 2 * c.s_own + 3 * 2 * cw)
        mid = sbuf("mid", [128, mid_w], BF16)
        stat_ld = sbuf("stat_ld", [128, cw], F32)
        stat_sq = sbuf("stat_sq", [128, cw], BF16)
        tf1 = sbuf("tf1", [128, cw], F32)
        tf2 = sbuf("tf2", [128, cw], F32)
        tf3 = sbuf("tf3", [128, cw], F32)
        tf4 = sbuf("tf4", [128, cw], F32)
        wbuf0 = sbuf("wbuf0", [128, c.dm], BF16)
        wbuf1 = sbuf("wbuf1", [128, c.dm], BF16)
        memory = sbuf("memory", [128, nt * 129], F32)
        memb = sbuf("memb", [128, 132], BF16)
        dec_bs = [sbuf("dec_b0", [128, 128 * c.dec_slab], BF16),
                  sbuf("dec_b1", [128, 128 * c.dec_slab], BF16)]
        dec_ts = [sbuf("dec_t0", [128, 128 * c.dec_slab], BF16),
                  sbuf("dec_t1", [128, 128 * c.dec_slab], BF16)]
        dec_as = [sbuf("dec_a0", [128, 128 * c.dec_slab], BF16),
                  sbuf("dec_a1", [128, 128 * c.dec_slab], BF16)]
        R0 = sbuf("R0", [1, c.s_own], F32)
        R1 = sbuf("R1", [1, c.s_own], F32)
        R2 = sbuf("R2", [1, c.s_own], F32)
        R0b = sbuf("R0b", [1, cw], F32)
        R1b = sbuf("R1b", [1, cw], F32)
        R2b = sbuf("R2b", [1, cw], F32)
        tf5 = sbuf("tf5", [128, cw], F32)
        mq2 = sbuf("mq2", [1, nt], F32)
        mk2 = sbuf("mk2", [1, nt], F32)
        g_row = sbuf("g_row", [1, nt], F32)
        og_row = sbuf("og_row", [1, nt], F32)
        gcol = sbuf("gcol", [1, 128], F32)
        ogcol = sbuf("ogcol", [1, 128], F32)
        ones_row = sbuf("ones_row", [1, 128], F32)
        ones_col = sbuf("ones_col", [128, 1], BF16)
        biasC = sbuf("biasC", [128, 1], F32)
        sc1 = sbuf("sc1", [1, 8], F32)
        ident_s = sbuf("ident_s", [128, 128], BF16)

        ps = stack.enter_context(nc.psum_tensor("ps", [128, 4096], F32))
        psb = ps[:, 3584:3584 + cw // 2].bitcast(BF16)

        def Q4(i, w=None):
            w = cw if w is None else w
            return ps[:, i * 512:i * 512 + w]

        def QPAIR(par):
            base = par * 1024
            return ps[:, base:base + 1024].rearrange(
                "p (b x) -> p b x", x=512)[:, :, 0:cw]

        DEN = ps[0:1, 3584:3584 + cw]
        PVPS = ps[:, 3072:3072 + cw]
        MEMPS = ps[:, 3072:3201]

        em = Emitter(nc)
        sb = SB(em)
        OP = sb.op

        def dma(eng, dst, src, reads=(), writes=()):
            return OP(eng, lambda E, d=dst, s=src: E.dma_start(out=d, in_=s),
                      reads=reads, writes=writes, dma=True)

        QA = wp[:, 0:cw]
        QB = wp[:, cw:2 * cw]
        QO = wp[:, 2 * cw:3 * cw]
        QO2 = wp[:, 3 * cw:4 * cw]
        K3A = wp[:, 4 * cw:5 * cw]
        K2A = wp[:, 5 * cw:6 * cw]
        K2B = wp[:, 6 * cw:7 * cw]
        K3B = wp[:, 7 * cw:8 * cw]
        sqb = tf2.bitcast(BF16)[:, 0:cw]
        tf1b = tf1.bitcast(BF16)[:, 0:cw]
        ldb_v = stat_ld.bitcast(BF16)[:, 0:cw]
        midf = mid.bitcast(F32)
        cos_s = [midf[:, 1024 * p:1024 * p + cw] for p in range(3)]
        sin_s = [midf[:, 1024 * p + cw:1024 * p + 2 * cw] for p in range(3)]
        m_cos_s = [pages("mid", 2048 * p, 2048 * p + 1024, 1024)
                   for p in range(3)]
        m_sin_s = [pages("mid", 2048 * p + 1024, 2048 * p + 2048, 1024)
                   for p in range(3)]
        cosb, sinb = cos_s[0], sin_s[0]
        m_cosb, m_sinb = m_cos_s[0], m_sin_s[0]
        SQK = mid[:, 6144:6144 + cw]
        m_SQK = pages("mid", 6144, 6144 + cw, 1024)

        OP("vector", lambda V: V.memset(ones_col[:, :], 1.0), writes=["ones"])
        OP("vector", lambda V: V.memset(ones_row[:, :], 1.0), writes=["onesr"])
        OP("vector", lambda V: V.memset(memory[:, :], 0.0), writes=["memory"])
        OP("vector", lambda V: V.memset(mq2[:, :], 0.0), writes=["mq2"])
        OP("vector", lambda V: V.memset(mk2[:, :], 0.0), writes=["mk2"])
        dma("sync", ident_s[:, :], ident_d[:, :], writes=["ident"])
        dma("sync", g_row[:, :], gate_in[:, :], writes=["grow"])
        OP("scalar", lambda A: A.activation(og_row[:, :], g_row[:, :], AF.Exp,
                                            scale=-1.0),
           reads=["grow"], writes=["ogrow"])
        OP("vector", lambda V: V.tensor_scalar_add(og_row[:, :], og_row[:, :],
                                                   1.0),
           reads=["ogrow"], writes=["ogrow"])
        OP("vector", lambda V: V.reciprocal(g_row[:, :], og_row[:, :]),
           reads=["ogrow"], writes=["grow"])
        OP("vector", lambda V: V.tensor_scalar(og_row[:, :], g_row[:, :],
                                               -1.0, 1.0, ALU.mult, ALU.add),
           reads=["grow"], writes=["ogrow"])
        for (wsrc, wdst, res) in ((wqT, wq_bf, "wqb"), (wkT, wk_bf, "wkb"),
                                  (woT, wo_bf, "wob")):
            s4 = wsrc.rearrange("(hc p) (m f) -> p m hc f", p=128, f=128)
            d4 = wdst.rearrange("p m (hc f) -> p m hc f", f=128)
            for m in range(nt):
                dma("gpsimd", d4[:, m], s4[:, m], writes=[res])
        sv = wvT.rearrange("(hc p) f -> p hc f", p=128)
        for hc in range(nt):
            dma("gpsimd", wv_bf[:, hc], sv[:, hc], writes=["wvb"])

        dec_jobs = []
        for src, dst, ntile in ((gidx, wg_hb, c.ngu), (uidx, wu_hb, c.ngu),
                                (didx, wd_hb, c.nd)):
            for a0 in range(0, ntile, c.dec_slab):
                w = min(c.dec_slab, ntile - a0) * 128
                dec_jobs.append((src[:, a0:a0 + w // 128, :],
                                 dst[:, a0:a0 + w // 128, :], w))

        def _dec_load(n):
            src, dst, w = dec_jobs[n]
            par = n % 2
            src2 = src.rearrange("p t f -> p (t f)")
            dma("gpsimd", dec_bs[par][:, 0:w], src2,
                writes=[f"dec_b{par}"])

        def _dec_jg(n, k):
            src, dst, w = dec_jobs[n]
            par = n % 2
            add_eng = "gpsimd" if n % 4 == 0 else "vector"
            da, dares = dec_as[par], f"dec_a{par}"
            db, dbres = dec_bs[par], f"dec_b{par}"
            for j in range(4 * k, 4 * k + 4):
                if j == 0:
                    OP("vector", lambda V, w=w, da=da, db=db: V.tensor_scalar(
                        da[:, 0:w], db[:, 0:w], 0.0, float(lut_vals[0]),
                        ALU.is_equal, ALU.mult),
                       reads=[dbres], writes=[dares])
                else:
                    tb, tres = dec_ts[j % 2], f"dec_t{j % 2}"
                    OP("vector", lambda V, w=w, j=j, tb=tb, db=db:
                       V.tensor_scalar(
                        tb[:, 0:w], db[:, 0:w], float(j),
                        float(lut_vals[j]), ALU.is_equal, ALU.mult),
                       reads=[dbres], writes=[tres])
                    OP(add_eng, lambda E, w=w, da=da, tb=tb: E.tensor_add(
                        da[:, 0:w], da[:, 0:w], tb[:, 0:w]),
                       reads=[tres, dares], writes=[dares])

        def _dec_store(n):
            src, dst, w = dec_jobs[n]
            par = n % 2
            dst2 = dst.rearrange("p t f -> p (t f)")
            dma("gpsimd", dst2, dec_as[par][:, 0:w],
                reads=[f"dec_a{par}"], writes=["dec_o"])

        dec_seq = []
        N_dec = len(dec_jobs)
        dec_seq.append(lambda: _dec_load(0))
        for n in range(N_dec):
            if n + 1 < N_dec:
                dec_seq.append(lambda n=n: _dec_load(n + 1))
            for k in range(4):
                dec_seq.append(lambda n=n, k=k: _dec_jg(n, k))
            dec_seq.append(lambda n=n: _dec_store(n))
        dec_pos = [0]

        def dec_steps(n):
            while n > 0 and dec_pos[0] < len(dec_seq):
                dec_seq[dec_pos[0]]()
                dec_pos[0] += 1
                n -= 1

        def dec_flush():
            dec_steps(len(dec_seq))

        RS = (((R0, "R0"), (R1, "R1"), (R2, "R2")),
              ((R0b, "R0b"), (R1b, "R1b"), (R2b, "R2b")))

        def ln_finalize(width, par=0, off=0):
            (Rx0, r0), (Rx1, r1), (Rx2, r2) = RS[par]
            sl = slice(off, off + width)
            inv_n = 1.0 / c.dm
            OP("vector", lambda V: V.tensor_scalar_mul(
                Rx0[0:1, sl], Rx0[0:1, sl], inv_n),
               reads=[r0], writes=[r0])
            OP("vector", lambda V: V.tensor_scalar_mul(
                Rx1[0:1, sl], Rx1[0:1, sl], inv_n),
               reads=[r1], writes=[r1])
            OP("vector", lambda V: V.tensor_mul(
                Rx2[0:1, sl], Rx0[0:1, sl], Rx0[0:1, sl]),
               reads=[r0], writes=[r2])
            OP("vector", lambda V: V.tensor_sub(
                Rx1[0:1, sl], Rx1[0:1, sl], Rx2[0:1, sl]),
               reads=[r1, r2], writes=[r1])
            OP("vector", lambda V: V.tensor_scalar_add(
                Rx1[0:1, sl], Rx1[0:1, sl], float(c.eps)),
               reads=[r1], writes=[r1])
            OP("scalar", lambda A: A.activation(
                Rx1[0:1, sl], Rx1[0:1, sl], AF.Ln),
               reads=[r1], writes=[r1])
            OP("scalar", lambda A: A.activation(
                Rx1[0:1, sl], Rx1[0:1, sl], AF.Exp, scale=-0.5),
               reads=[r1], writes=[r1])
            OP("vector", lambda V: V.tensor_mul(
                Rx2[0:1, sl], Rx0[0:1, sl], Rx1[0:1, sl]),
               reads=[r0, r1], writes=[r2])

        def bcast_rows(width, off=0, par=0, banks=(4, 5)):
            (Rx0, r0), (Rx1, r1), (Rx2, r2) = RS[par]
            b0, b1 = banks
            OP("tensor", lambda T: T.matmul(
                Q4(b0, width), ones_row[:, :], Rx1[0:1, off:off + width],
                start=True, stop=True),
               reads=["onesr", r1], writes=[f"PSq{b0}"])
            OP("tensor", lambda T: T.matmul(
                Q4(b1, width), ones_row[:, :], Rx2[0:1, off:off + width],
                start=True, stop=True),
               reads=["onesr", r2], writes=[f"PSq{b1}"])

        def stats_tiles(load_fn, keep, t0, t1):
            ld_bufs = ((stat_ld[:, 0:cw], "stld"), (tf1[:, 0:cw], "tf1"))
            sq_bufs = ((stat_sq[:, 0:cw], "stsq"), (sqb, "tf2"))
            for t in range(t0, t1):
                ld, ldres = ld_bufs[t % 2]
                sq, sqres = sq_bufs[t % 2]
                kd, kres = keep(t)
                dma("sync", ld, load_fn(t), writes=[ldres])
                OP("scalar", lambda A, ld=ld, kd=kd: A.activation(
                    kd, ld, AF.Copy), reads=[ldres], writes=kres)
                OP("scalar", lambda A, ld=ld, sq=sq: A.activation(
                    sq, ld, AF.Square), reads=[ldres], writes=[sqres])
                OP("tensor", lambda T, kd=kd, t=t: T.matmul(
                    Q4(2)[0:1, :], ones_col[:, :], kd,
                    start=(t == 0), stop=(t == nt - 1)),
                   reads=["ones"] + kres, writes=["PSq2"])
                OP("tensor", lambda T, sq=sq, t=t: T.matmul(
                    Q4(3)[0:1, :], ones_col[:, :], sq,
                    start=(t == 0), stop=(t == nt - 1)),
                   reads=["ones", sqres], writes=["PSq3"])

        def stats_rows(par=0, off=0):
            (Rx0, r0), (Rx1, r1), _ = RS[par]
            OP("vector", lambda V: V.tensor_copy(Rx0[0:1, off:off + cw],
                                                 Q4(2)[0:1, :]),
               reads=["PSq2"], writes=[r0])
            OP("vector", lambda V: V.tensor_copy(Rx1[0:1, off:off + cw],
                                                 Q4(3)[0:1, :]),
               reads=["PSq3"], writes=[r1])

        def chunk_stats(load_fn, keep, par=0):
            stats_tiles(load_fn, keep, 0, nt)
            stats_rows(par)

        def hb_stats(src_hb, src_f32, extra_reads=()):
            for s in range(c.n_oc):
                sl = slice(s * cw, (s + 1) * cw)
                for t in range(nt):
                    if src_f32:
                        ld, ldres = ((stat_ld[:, 0:cw], "stld") if t % 2 == 0
                                     else (tf1[:, 0:cw], "tf1"))
                        dma("sync", ld, src_hb[t][:, sl],
                            reads=list(extra_reads), writes=[ldres])
                        mi, mires = ((stat_sq[:, 0:cw], "stsq") if t % 2 == 0
                                     else (sqb, "tf2"))
                        OP("scalar", lambda A, ld=ld, mi=mi: A.activation(
                            mi, ld, AF.Copy), reads=[ldres], writes=[mires])
                        OP("tensor", lambda T, mi=mi, t=t: T.matmul(
                            Q4(4)[0:1, :], ones_col[:, :], mi,
                            start=(t == 0), stop=(t == nt - 1)),
                           reads=["ones", mires], writes=["PSq4"])
                        OP("scalar", lambda A, ld=ld, mi=mi: A.activation(
                            mi, ld, AF.Square), reads=[ldres, mires],
                           writes=[mires])
                        OP("tensor", lambda T, mi=mi, t=t: T.matmul(
                            Q4(5)[0:1, :], ones_col[:, :], mi,
                            start=(t == 0), stop=(t == nt - 1)),
                           reads=["ones", mires], writes=["PSq5"])
                    else:
                        ld, ldres = ((ldb_v, "stld") if t % 2 == 0
                                     else (tf1b, "tf1"))
                        sq, sqres = ((stat_sq[:, 0:cw], "stsq") if t % 2 == 0
                                     else (sqb, "tf2"))
                        dma("sync", ld, src_hb[t][:, sl],
                            reads=list(extra_reads), writes=[ldres])
                        OP("scalar", lambda A, ld=ld, sq=sq: A.activation(
                            sq, ld, AF.Square), reads=[ldres],
                           writes=[sqres])
                        OP("tensor", lambda T, ld=ld, t=t: T.matmul(
                            Q4(4)[0:1, :], ones_col[:, :], ld,
                            start=(t == 0), stop=(t == nt - 1)),
                           reads=["ones", ldres], writes=["PSq4"])
                        OP("tensor", lambda T, sq=sq, t=t: T.matmul(
                            Q4(5)[0:1, :], ones_col[:, :], sq,
                            start=(t == 0), stop=(t == nt - 1)),
                           reads=["ones", sqres], writes=["PSq5"])
                OP("vector", lambda V, sl=sl: V.tensor_copy(
                    R0[0:1, sl], Q4(4)[0:1, :]), reads=["PSq4"],
                   writes=["R0"])
                OP("vector", lambda V, sl=sl: V.tensor_copy(
                    R1[0:1, sl], Q4(5)[0:1, :]), reads=["PSq5"],
                   writes=["R1"])

        def xn_own(t, s):
            lo = t * c.s_own + s * cw
            return big1[:, lo:lo + cw], pages("big1", lo, lo + cw)

        for s in range(c.n_oc):
            sl = slice(s * cw, (s + 1) * cw)
            chunk_stats(lambda t, sl=sl: xownT[t][:, sl],
                        lambda t, s=s: xn_own(t, s), par=s % 2)
            ln_finalize(cw, par=s % 2)
            bcast_rows(cw, par=s % 2)
            for t in range(nt):
                xn, xnres = xn_own(t, s)
                OP("vector", lambda V, xn=xn: V.tensor_mul(
                    tf3[:, 0:cw], xn, Q4(4)),
                   reads=xnres + ["PSq4"], writes=["tf3"])
                OP("vector", lambda V, xn=xn: V.tensor_sub(
                    xn, tf3[:, 0:cw], Q4(5)),
                   reads=["tf3", "PSq5"], writes=xnres)
            dec_steps(6)

        def wload(wbf, m, buf, res, src_res):
            dma("sync", buf[:, :], wbf[:, m, :], reads=[src_res],
                writes=[res])

        def rope_pair(getA, resA, outA, outAres, outB, outBres,
                      getB=None, resB=None, cs=None):
            getB = QB if getB is None else getB
            resB = ["QB"] if resB is None else resB
            cosv, sinv, m_cv, m_sv = ((cosb, sinb, m_cosb, m_sinb)
                                      if cs is None else cs)
            OP("vector", lambda V, aA=getA, cosv=cosv: V.tensor_mul(
                tf1[:, 0:cw], aA, cosv),
               reads=list(resA) + m_cv, writes=["tf1"])
            OP("vector", lambda V, aB=getB, sinv=sinv: V.tensor_mul(
                tf2[:, 0:cw], aB, sinv),
               reads=list(resB) + m_sv, writes=["tf2"])
            OP("vector", lambda V, o=outA: V.tensor_sub(o, tf1[:, 0:cw],
                                                        tf2[:, 0:cw]),
               reads=["tf1", "tf2"], writes=outAres)
            OP("vector", lambda V, aB=getB, cosv=cosv: V.tensor_mul(
                tf1[:, 0:cw], aB, cosv),
               reads=list(resB) + m_cv, writes=["tf1"])
            OP("vector", lambda V, aA=getA, sinv=sinv: V.tensor_mul(
                tf2[:, 0:cw], aA, sinv),
               reads=list(resA) + m_sv, writes=["tf2"])
            OP("vector", lambda V, o=outB: V.tensor_add(o, tf1[:, 0:cw],
                                                        tf2[:, 0:cw]),
               reads=["tf1", "tf2"], writes=outBres)

        def row_norm_max(src, sres, dstrow, dres, mh):
            OP("scalar", lambda A, src=src: A.activation(
                stat_sq[:, 0:cw], src, AF.Square),
               reads=sres, writes=["stsq"])
            OP("tensor", lambda T: T.matmul(
                Q4(2)[0:1, :], ones_col[:, :], stat_sq[:, 0:cw],
                start=True, stop=True),
               reads=["ones", "stsq"], writes=["PSq2"])
            OP("scalar", lambda A: A.activation(R2[0:1, 0:cw],
                                                Q4(2)[0:1, :], AF.Copy),
               reads=["PSq2"], writes=["R2"])
            OP("vector", lambda V: V.reduce_max(
                sc1[0:1, 0:1], R2[0:1, 0:cw], axis=mybir.AxisListType.X),
               reads=["R2"], writes=["sc1"])
            OP("vector", lambda V, mh=mh, dr=dstrow: V.tensor_max(
                dr[0:1, mh:mh + 1], dr[0:1, mh:mh + 1], sc1[0:1, 0:1]),
               reads=["sc1", dres], writes=[dres])

        wload(wq_bf, 0, wbuf0, "wb0", "wqb")
        for tp in range(hp):
            wload(wq_bf, tp + hp, wbuf1, "wb1", "wqb")
            for s in range(c.n_oc):
                for hc in range(nt):
                    rhs, rres = xn_own(hc, s)
                    OP("tensor", lambda T, hc=hc, rhs=rhs: T.matmul(
                        Q4(0), wbuf0[:, hc * 128:(hc + 1) * 128], rhs,
                        start=(hc == 0), stop=(hc == nt - 1)),
                       reads=["wb0"] + rres, writes=["PSq0"])
                lo = (tp * c.n_oc + s) * cw
                st_ = big0[:, lo:lo + cw]
                stres = pages("big0", lo, lo + cw)
                OP("scalar", lambda A, st_=st_: A.activation(st_, Q4(0),
                                                             AF.Copy),
                   reads=["PSq0"], writes=stres)
            if tp + 1 < hp:
                wload(wq_bf, tp + 1, wbuf0, "wb0", "wqb")
            for s in range(c.n_oc):
                sl = slice(s * cw, (s + 1) * cw)
                for hc in range(nt):
                    rhs, rres = xn_own(hc, s)
                    OP("tensor", lambda T, hc=hc, rhs=rhs: T.matmul(
                        Q4(1), wbuf1[:, hc * 128:(hc + 1) * 128], rhs,
                        start=(hc == 0), stop=(hc == nt - 1)),
                       reads=["wb1"] + rres, writes=["PSq1"])
                OP("scalar", lambda A: A.activation(QB, Q4(1), AF.Copy),
                   reads=["PSq1"], writes=["QB"])
                dma("sync", cosb, cosq[tp][:, sl], writes=m_cosb)
                dma("sync", sinb, sinq[tp][:, sl], writes=m_sinb)
                lo = (tp * c.n_oc + s) * cw
                st_ = big0[:, lo:lo + cw]
                stres = pages("big0", lo, lo + cw)
                rope_pair(st_, stres, QO, ["QO"], QO2, ["QO2"])
                dma("sync", qT_h[tp][:, sl], QO, reads=["QO"], writes=["qTh"])
                dma("sync", qT_h[tp + hp][:, sl], QO2, reads=["QO2"],
                    writes=["qTh"])
                row_norm_max(QO, ["QO"], mq2, "mq2", tp)
                row_norm_max(QO2, ["QO2"], mq2, "mq2", tp + hp)
            dec_steps(3)

        def xn_area(t, apar):
            lo = (apar * nt + t) * cw
            return big0[:, lo:lo + cw], pages("big0", lo, lo + cw)

        wv_b1 = big1[:, 0:nt * c.dm]
        m_wv = pages("big1", 0, nt * c.dm)
        dma("sync", wv_b1.rearrange("p (hc f) -> p hc f", f=c.dm), wv_bf,
            reads=["wvb"], writes=m_wv)
        n_sub = cw // 128
        vch_lo = 2 * nt * cw
        VB = (0, 1, 6, 7)

        def vch_sl(sub, d0, d1):
            lo = vch_lo + sub * c.dm + d0
            return (big0[:, lo:lo + (d1 - d0)],
                    pages("big0", lo, lo + (d1 - d0)))

        def c12_stats_part(ch, t0, t1):
            sl_f = slice(ch * cw, (ch + 1) * cw)
            apar = ch % 2
            stats_tiles(lambda t, sl_f=sl_f: xT[t][:, sl_f],
                        lambda t, apar=apar: xn_area(t, apar), t0, t1)

        def c12_norm(ch):
            sl_f = slice(ch * cw, (ch + 1) * cw)
            apar = ch % 2
            stats_rows(par=apar)
            ln_finalize(cw, par=apar)
            bcast_rows(cw, par=apar)
            for t in range(nt):
                xn, xnres = xn_area(t, apar)
                OP("vector", lambda V, xn=xn: V.tensor_mul(
                    tf3[:, 0:cw], xn, Q4(4)),
                   reads=xnres + ["PSq4"], writes=["tf3"])
                OP("vector", lambda V, xn=xn: V.tensor_sub(
                    xn, tf3[:, 0:cw], Q4(5)),
                   reads=["tf3", "PSq5"], writes=xnres)
                dma("sync", xn_hb[t][:, sl_f], xn, reads=xnres,
                    writes=["xnhb"])

        def c12_vmms(ch, dq):
            apar = ch % 2
            for sub in range(n_sub):
                bank = VB[sub]
                for hc in range(nt):
                    rhs = wv_b1[:, hc * c.dm + dq * 512:
                                hc * c.dm + (dq + 1) * 512]
                    xs, xsres = xn_area(hc, apar)
                    lhs = xs[:, sub * 128:(sub + 1) * 128]
                    OP("tensor", lambda T, hc=hc, bank=bank, rhs=rhs,
                       lhs=lhs: T.matmul(
                        Q4(bank, 512), lhs, rhs,
                        start=(hc == 0), stop=(hc == nt - 1)),
                       reads=xsres + m_wv, writes=[f"PSq{bank}"])
            for sub in range(n_sub):
                bank = VB[sub]
                vd, vdres = vch_sl(sub, dq * 512, (dq + 1) * 512)
                OP("scalar", lambda A, vd=vd, bank=bank: A.activation(
                    vd, Q4(bank, 512), AF.Copy),
                   reads=[f"PSq{bank}"], writes=vdres)

        c12_stats_part(0, 0, nt)
        c12_norm(0)
        for ch in range(c.n_ch):
            for dq in range(c.dm // 512):
                if ch + 1 < c.n_ch:
                    tpg = nt // (c.dm // 512)
                    c12_stats_part(ch + 1, dq * tpg, (dq + 1) * tpg)
                c12_vmms(ch, dq)
                dec_steps(5)
            for sub in range(n_sub):
                vrow, vrres = vch_sl(sub, 0, c.dm)
                dma("sync", v_hb[ch * cw + sub * 128:
                                 ch * cw + (sub + 1) * 128, :],
                    vrow, reads=vrres, writes=["vhb"])
            if ch + 1 < c.n_ch:
                c12_norm(ch + 1)
            dec_steps(6)

        wk_b1 = big1[:, 0:nt * c.dm]
        m_wk = pages("big1", 0, nt * c.dm)
        dma("sync", wk_b1.rearrange("p (m f) -> p m f", f=c.dm), wk_bf,
            reads=["wkb"] + m_wv, writes=m_wk)

        kslots = [((QA, "QA"), (QB, "QB")), ((K2A, "K2A"), (K2B, "K2B")),
                  ((K3A, "K3A"), (K3B, "K3B"))]

        def c3_xn_loads(ch):
            apar = ch % 2
            sl_f = slice(ch * cw, (ch + 1) * cw)
            for t in range(nt):
                xs, xsres = xn_area(t, apar)
                dma("sync", xs, xn_hb[t][:, sl_f], reads=["xnhb"],
                    writes=xsres)

        def c3_vch_loads(ch):
            for sub in range(n_sub):
                vrow, vrres = vch_sl(sub, 0, c.dm)
                dma("sync", vrow,
                    v_hb[ch * cw + sub * 128:ch * cw + (sub + 1) * 128, :],
                    reads=["vhb"], writes=vrres)

        def c3_front_pre(u):
            ch, tp = divmod(u, hp)
            par = u % 3
            sl_f = slice(ch * cw, (ch + 1) * cw)
            if tp == min(4, hp - 1) and ch + 1 < c.n_ch:
                c3_xn_loads(ch + 1)
            dma("sync", cos_s[par], cosk[tp][:, sl_f], writes=m_cos_s[par])
            dma("sync", sin_s[par], sink[tp][:, sl_f], writes=m_sin_s[par])

        def c3_kmm(u, half):
            ch, tp = divmod(u, hp)
            apar = ch % 2
            bank = half
            m = tp + half * hp
            wslab = wk_b1[:, m * c.dm:(m + 1) * c.dm]
            for hc in range(nt):
                xs, xsres = xn_area(hc, apar)
                OP("tensor", lambda T, hc=hc, wslab=wslab, bank=bank,
                   xnv=xs: T.matmul(
                    Q4(bank), wslab[:, hc * 128:(hc + 1) * 128],
                    xnv, start=(hc == 0), stop=(hc == nt - 1)),
                   reads=m_wk + xsres, writes=[f"PSq{bank}"])

        def c3_evac(u):
            par = u % 3
            (kA, kAres), (kB, kBres) = kslots[par]
            for (bank, dst, dres) in ((0, kA, kAres), (1, kB, kBres)):
                OP("scalar", lambda A, dst=dst, bank=bank: A.activation(
                    dst, Q4(bank), AF.Copy),
                   reads=[f"PSq{bank}"], writes=[dres])

        SQK2 = ldb_v
        m_SQK2 = ["stld"]

        def c3_back1(u):
            ch, tp = divmod(u, hp)
            par = u % 3
            sl_f = slice(ch * cw, (ch + 1) * cw)
            (kA, kAres), (kB, kBres) = kslots[par]
            rope_pair(kA, [kAres], QO, ["QO"], QO2, ["QO2"],
                      getB=kB, resB=[kBres],
                      cs=(cos_s[par], sin_s[par], m_cos_s[par],
                          m_sin_s[par]))
            for (kout, kres, mh) in ((QO, "QO", tp),
                                     (QO2, "QO2", tp + hp)):
                dma("sync", kT_h[mh][:, sl_f], kout, reads=[kres],
                    writes=["kTh"])
            for sub in range(n_sub):
                OP("tensor", lambda T, sub=sub:
                   T.transpose(psb[:, sub * 128:(sub + 1) * 128],
                               QO[:, sub * 128:(sub + 1) * 128],
                               ident_s[:, :]),
                   reads=["QO", "ident"], writes=["PSq7"])
            OP("scalar", lambda A: A.activation(SQK, psb[:, 0:cw],
                                                AF.Copy),
               reads=["PSq7"], writes=m_SQK)
            for sub in range(n_sub):
                OP("tensor", lambda T, sub=sub:
                   T.transpose(psb[:, sub * 128:(sub + 1) * 128],
                               QO2[:, sub * 128:(sub + 1) * 128],
                               ident_s[:, :]),
                   reads=["QO2", "ident"], writes=["PSq7"])
            OP("scalar", lambda A: A.activation(SQK2, psb[:, 0:cw],
                                                AF.Copy),
               reads=["PSq7"], writes=m_SQK2)
            OP("vector", lambda V: V.tensor_scalar_min(
                tf3[:, 0:cw], SQK, 0.0), reads=m_SQK, writes=["tf3"])
            OP("vector", lambda V: V.tensor_scalar_min(
                tf5[:, 0:cw], SQK2, 0.0), reads=m_SQK2, writes=["tf5"])
            OP("scalar", lambda A: A.activation(
                tf4[:, 0:cw], tf3[:, 0:cw], AF.Exp),
               reads=["tf3"], writes=["tf4"])
            OP("scalar", lambda A: A.activation(
                tf2[:, 0:cw], tf5[:, 0:cw], AF.Exp),
               reads=["tf5"], writes=["tf2"])
            OP("vector", lambda V: V.tensor_scalar_add(
                tf3[:, 0:cw], SQK, 1.0), reads=m_SQK, writes=["tf3"])
            OP("vector", lambda V: V.tensor_max(
                SQK, tf3[:, 0:cw], tf4[:, 0:cw]),
               reads=["tf3", "tf4"], writes=m_SQK)
            OP("vector", lambda V: V.tensor_scalar_add(
                tf5[:, 0:cw], SQK2, 1.0), reads=m_SQK2, writes=["tf5"])
            OP("vector", lambda V: V.tensor_max(
                SQK2, tf5[:, 0:cw], tf2[:, 0:cw]),
               reads=["tf5", "tf2"], writes=m_SQK2)

        def c3_back2(u):
            ch, tp = divmod(u, hp)
            for (kf, kfm, mh) in ((SQK, m_SQK, tp), (SQK2, m_SQK2, tp + hp)):
                for sub in range(n_sub):
                    lhs = kf[:, sub * 128:(sub + 1) * 128]
                    rhs, rres = vch_sl(sub, mh * 128, (mh + 1) * 128)
                    OP("tensor", lambda T, lhs=lhs, rhs=rhs, sub=sub:
                       T.matmul(MEMPS[:, 0:128], lhs, rhs,
                                start=(sub == 0), stop=False),
                       reads=kfm + rres, writes=["PSq6"])
                    OP("tensor", lambda T, lhs=lhs, sub=sub: T.matmul(
                        MEMPS[:, 128:129], lhs, ones_col[:, :],
                        start=False, stop=(sub == n_sub - 1)),
                       reads=kfm + ["ones"], writes=["PSq6"])
                msl = slice(mh * 129, (mh + 1) * 129)
                OP("vector", lambda V, msl=msl: V.tensor_add(
                    memory[:, msl], memory[:, msl], MEMPS[:, 0:129]),
                   reads=["PSq6"], writes=["memory"])
            row_norm_max(QO, ["QO"], mk2, "mk2", tp)
            row_norm_max(QO2, ["QO2"], mk2, "mk2", tp + hp)
            if tp == hp - 1 and ch + 1 < c.n_ch:
                c3_vch_loads(ch + 1)
            if (ch * hp + tp) % 2 == 0:
                dec_steps(1)

        c3_xn_loads(0)
        c3_vch_loads(0)
        n_u = c.n_ch * hp
        for u in range(n_u + 2):
            if 1 <= u <= n_u:
                c3_evac(u - 1)
            if u < n_u:
                c3_front_pre(u)
                c3_kmm(u, 0)
            if u >= 2:
                c3_back1(u - 2)
            if u < n_u:
                c3_kmm(u, 1)
            if u >= 2:
                c3_back2(u - 2)

        OP("vector", lambda V: V.reduce_max(sc1[0:1, 0:1], mq2[0:1, :],
                                            axis=mybir.AxisListType.X),
           reads=["mq2"], writes=["sc1"])
        OP("vector", lambda V: V.reduce_max(sc1[0:1, 1:2], mk2[0:1, :],
                                            axis=mybir.AxisListType.X),
           reads=["mk2"], writes=["sc1"])
        OP("vector", lambda V: V.tensor_mul(sc1[0:1, 2:3], sc1[0:1, 0:1],
                                            sc1[0:1, 1:2]),
           reads=["sc1"], writes=["sc1"])
        OP("scalar", lambda A: A.activation(sc1[0:1, 3:4], sc1[0:1, 2:3],
                                            AF.Ln), reads=["sc1"],
           writes=["sc1"])
        OP("scalar", lambda A: A.activation(sc1[0:1, 3:4], sc1[0:1, 3:4],
                                            AF.Exp, scale=0.5),
           reads=["sc1"], writes=["sc1"])
        OP("vector", lambda V: V.tensor_scalar_mul(sc1[0:1, 3:4],
                                                   sc1[0:1, 3:4],
                                                   -float(c.scale)),
           reads=["sc1"], writes=["sc1"])
        OP("tensor", lambda T: T.matmul(Q4(4, 1)[:, 0:1], ones_row[:, :],
                                        sc1[0:1, 3:4], start=True, stop=True),
           reads=["onesr", "sc1"], writes=["PSq4"])
        OP("vector", lambda V: V.tensor_copy(biasC[:, :], Q4(4, 1)[:, 0:1]),
           reads=["PSq4"], writes=["biasC"])

        qbuf = mid[:, 0:c.s_own]
        qf = mid[:, c.s_own:2 * c.s_own]
        pbs = [mid[:, 2 * c.s_own + i * 2 * cw:
                   2 * c.s_own + (i + 1) * 2 * cw] for i in range(3)]
        m_qbuf = pages("mid", 0, c.s_own, 1024)
        m_qf = pages("mid", c.s_own, 2 * c.s_own, 1024)
        m_pbs = [pages("mid", 2 * c.s_own + i * 2 * cw,
                       2 * c.s_own + (i + 1) * 2 * cw, 1024)
                 for i in range(3)]

        kbuf = big0[:, 0:c.s_full]
        vb0 = big0[:, c.s_full:2 * c.s_full]
        vb1 = big0[:, 2 * c.s_full:3 * c.s_full]
        m_kbuf = pages("big0", 0, c.s_full)
        m_vb = [pages("big0", c.s_full, 2 * c.s_full),
                pages("big0", 2 * c.s_full, 3 * c.s_full)]

        v_re = v_hb.rearrange("(cb p) d -> p cb d", p=128)
        n_cb = c.s_full // 128

        def load_vhead(h, vb, vres):
            npiece = 8 if n_cb % 8 == 0 else 1
            step = n_cb // npiece
            for pc in range(npiece):
                src = v_re[:, pc * step:(pc + 1) * step,
                           h * 128:(h + 1) * 128]
                dst = vb[:, pc * step * 128:(pc + 1) * step * 128]
                dst3 = dst.rearrange("p (cb d) -> p cb d", d=128)
                dma("sync", dst3, src, reads=["vhb"], writes=vres)

        load_vhead(0, vb0, m_vb[0])
        for h in range(nt):
            vcur, m_vcur = (vb0, m_vb[0]) if h % 2 == 0 else (vb1, m_vb[1])
            if h + 1 < nt:
                vnxt, m_vnxt = (vb1, m_vb[1]) if h % 2 == 0 else \
                    (vb0, m_vb[0])
                load_vhead(h + 1, vnxt, m_vnxt)
            for pc in range(c.s_full // 2048):
                dma("sync", kbuf[:, pc * 2048:(pc + 1) * 2048],
                    kT_h[h][:, pc * 2048:(pc + 1) * 2048], reads=["kTh"],
                    writes=pages("big0", pc * 2048, (pc + 1) * 2048))
            for pc in range(c.s_own // 1024):
                dma("sync", qbuf[:, pc * 1024:(pc + 1) * 1024],
                    qT_h[h][:, pc * 1024:(pc + 1) * 1024], reads=["qTh"],
                    writes=pages("mid", pc * 1024, (pc + 1) * 1024, 1024))
            for s in range(c.n_oc):
                qsl = qbuf[:, s * cw:(s + 1) * cw]
                m_qs = pages("mid", s * cw, (s + 1) * cw, 1024)
                OP("vector", lambda V, qsl=qsl: V.tensor_scalar_min(
                    tf1[:, 0:cw], qsl, 0.0), reads=m_qs, writes=["tf1"])
                OP("scalar", lambda A: A.activation(tf2[:, 0:cw],
                                                    tf1[:, 0:cw], AF.Exp),
                   reads=["tf1"], writes=["tf2"])
                OP("vector", lambda V, qsl=qsl: V.tensor_scalar_add(
                    tf1[:, 0:cw], qsl, 1.0), reads=m_qs, writes=["tf1"])
                OP("vector", lambda V, s=s: V.tensor_max(
                    qf[:, s * cw:(s + 1) * cw], tf1[:, 0:cw], tf2[:, 0:cw]),
                   reads=["tf1", "tf2"],
                   writes=pages("mid", c.s_own + s * cw,
                                c.s_own + (s + 1) * cw, 1024))
            OP("vector", lambda V, h=h: V.tensor_copy(
                memb[:, 0:129], memory[:, h * 129:(h + 1) * 129]),
               reads=["memory"], writes=["memb"])
            OP("tensor", lambda T, h=h: T.matmul(
                Q4(4, 128)[0:1, :], g_row[0:1, h:h + 1], ones_row[:, :],
                start=True, stop=True),
               reads=["grow", "onesr"], writes=["PSq4"])
            OP("vector", lambda V: V.tensor_copy(gcol[0:1, :],
                                                 Q4(4, 128)[0:1, :]),
               reads=["PSq4"], writes=["gcol"])
            OP("tensor", lambda T, h=h: T.matmul(
                Q4(4, 128)[0:1, :], og_row[0:1, h:h + 1], ones_row[:, :],
                start=True, stop=True),
               reads=["ogrow", "onesr"], writes=["PSq4"])
            OP("vector", lambda V: V.tensor_copy(ogcol[0:1, :],
                                                 Q4(4, 128)[0:1, :]),
               reads=["PSq4"], writes=["ogcol"])
            def mk_qb(qb, vcur, m_vcur):
                qsl = slice(qb * cw, (qb + 1) * cw)
                m_q = pages("mid", qb * cw, (qb + 1) * cw, 1024)

                def emit_sc(g):
                    par = g % 3
                    for half in range(2):
                        kc = 2 * g + half
                        bank = 2 * par + half
                        m_k = pages("big0", kc * 128, (kc + 1) * 128)
                        OP("tensor", lambda T, kc=kc, bank=bank:
                           T.matmul(Q4(bank),
                                    kbuf[:, kc * 128:(kc + 1) * 128],
                                    qbuf[:, qsl], start=True, stop=True),
                           reads=m_k + m_q, writes=[f"PSq{bank}"])

                def emit_exp(g):
                    par = g % 3
                    pbx, m_pbx = pbs[par], m_pbs[par]
                    pbx3 = pbx.rearrange("p (b x) -> p b x", x=cw)
                    OP("scalar", lambda A, par=par, pbx3=pbx3: A.activation(
                        pbx3, QPAIR(par), AF.Exp, bias=biasC[:, 0:1],
                        scale=c.scale),
                       reads=[f"PSq{2 * par}", f"PSq{2 * par + 1}", "biasC"],
                       writes=m_pbx)

                def emit_pv(g):
                    par = g % 3
                    pbx, m_pbx = pbs[par], m_pbs[par]
                    for half in range(2):
                        kc = 2 * g + half
                        OP("tensor", lambda T, kc=kc, half=half, pbx=pbx:
                           T.matmul(
                            PVPS, vcur[:, kc * 128:(kc + 1) * 128],
                            pbx[:, half * cw:(half + 1) * cw],
                            start=(kc == 0), stop=(kc == c.n_kc - 1)),
                           reads=m_vcur + m_pbx, writes=["PSq6"])
                        OP("tensor", lambda T, kc=kc, half=half, pbx=pbx:
                           T.matmul(DEN, ones_col[:, :],
                                    pbx[:, half * cw:(half + 1) * cw],
                                    start=(kc == 0),
                                    stop=(kc == c.n_kc - 1)),
                           reads=["ones"] + m_pbx, writes=["PSq7"])

                def emit_lin():
                    m_qfq = pages("mid", c.s_own + qb * cw,
                                  c.s_own + (qb + 1) * cw, 1024)
                    OP("tensor", lambda T: T.matmul(
                        PVPS, memb[:, 0:128], qf[:, qsl],
                        start=True, stop=True),
                       reads=["memb"] + m_qfq, writes=["PSq6"])
                    OP("scalar", lambda A: A.activation(tf3[:, 0:cw], PVPS,
                                                        AF.Copy),
                       reads=["PSq6"], writes=["tf3"])
                    OP("tensor", lambda T: T.matmul(
                        DEN, memb[:, 128:129], qf[:, qsl],
                        start=True, stop=True),
                       reads=["memb"] + m_qfq, writes=["PSq7"])
                    OP("vector", lambda V: V.tensor_copy(
                        R1[0:1, qsl], DEN), reads=["PSq7"], writes=["R1"])
                    OP("vector", lambda V: V.reciprocal(
                        R1[0:1, qsl], R1[0:1, qsl]), reads=["R1"],
                       writes=["R1"])

                def emit_epi():
                    OP("vector", lambda V: V.tensor_copy(
                        R0[0:1, qsl], DEN), reads=["PSq7"], writes=["R0"])
                    OP("scalar", lambda A: A.activation(
                        tf4[:, 0:cw], PVPS, AF.Copy),
                       reads=["PSq6"], writes=["tf4"])
                    OP("vector", lambda V: V.reciprocal(
                        R0[0:1, qsl], R0[0:1, qsl]), reads=["R0"],
                       writes=["R0"])
                    OP("tensor", lambda T: T.matmul(
                        PVPS, gcol[0:1, :], R1[0:1, qsl],
                        start=True, stop=True),
                       reads=["gcol", "R1"], writes=["PSq6"])
                    OP("tensor", lambda T: T.matmul(
                        Q4(7), ogcol[0:1, :], R0[0:1, qsl],
                        start=True, stop=True),
                       reads=["ogcol", "R0"], writes=["PSq7"])
                    lo = h * c.s_own + qb * cw
                    cmb = big1[:, lo:lo + cw]
                    cres = pages("big1", lo, lo + cw)
                    OP("vector", lambda V: V.tensor_mul(
                        tf3[:, 0:cw], tf3[:, 0:cw], PVPS),
                       reads=["tf3", "PSq6"], writes=["tf3"])
                    OP("vector", lambda V: V.tensor_mul(
                        tf4[:, 0:cw], tf4[:, 0:cw], Q4(7)),
                       reads=["tf4", "PSq7"], writes=["tf4"])
                    OP("vector", lambda V, cmb=cmb, cres=cres: V.tensor_add(
                        cmb, tf4[:, 0:cw], tf3[:, 0:cw]),
                       reads=["tf3", "tf4"], writes=cres)

                return emit_sc, emit_exp, emit_pv, emit_lin, emit_epi

            ng = c.n_kc // 2
            prev_epi = None
            for qb in range(c.n_oc):
                e_sc, e_exp, e_pv, e_lin, e_epi = mk_qb(qb, vcur, m_vcur)
                e_sc(0)
                e_sc(1)
                e_exp(0)
                if prev_epi is not None:
                    prev_epi()
                e_lin()
                for g in range(2, ng + 2):
                    if g < ng:
                        e_sc(g)
                        if g in (2, 10, 18):
                            dec_steps(4)
                        elif g == 25:
                            dec_steps(2)
                    if g - 1 < ng:
                        e_exp(g - 1)
                    e_pv(g - 2)
                prev_epi = e_epi
            prev_epi()
            dec_steps(4)

        def park_a(s, ob):
            lo = (s % 2) * nt * cw + ob * cw
            return big0[:, lo:lo + cw], pages("big0", lo, lo + cw)

        def park_r2(t):
            lo = 2 * nt * cw + t * cw
            return big0[:, lo:lo + cw], pages("big0", lo, lo + cw)

        def p4_front(s):
            for ob in range(nt):
                g = s * nt + ob
                wb, wres = (wbuf0, "wb0") if g % 2 == 0 else (wbuf1, "wb1")
                if g == 0:
                    wload(wo_bf, 0, wbuf0, "wb0", "wob")
                if g + 1 < c.n_oc * nt:
                    nwb, nwres = ((wbuf1, "wb1") if g % 2 == 0 else
                                  (wbuf0, "wb0"))
                    wload(wo_bf, (g + 1) % nt, nwb, nwres, "wob")
                bank = ob % 2
                for ic in range(nt):
                    lo = ic * c.s_own + s * cw
                    cap = big1[:, lo:lo + cw]
                    cres = pages("big1", lo, lo + cw)
                    OP("tensor", lambda T, ic=ic, bank=bank, wb=wb, cap=cap:
                       T.matmul(Q4(bank), wb[:, ic * 128:(ic + 1) * 128],
                                cap, start=(ic == 0), stop=(ic == nt - 1)),
                       reads=[wres] + cres, writes=[f"PSq{bank}"])
                pk, pkres = park_a(s, ob)
                OP("scalar", lambda A, pk=pk, bank=bank: A.activation(
                    pk, Q4(bank), AF.Copy),
                   reads=[f"PSq{bank}"], writes=pkres)
                sq, sqres = ((stat_sq[:, 0:cw], "stsq") if ob % 2 == 0
                             else (sqb, "tf2"))
                OP("scalar", lambda A, pk=pk, sq=sq: A.activation(
                    sq, pk, AF.Square), reads=pkres, writes=[sqres])
                OP("tensor", lambda T, pk=pk, ob=ob: T.matmul(
                    Q4(2)[0:1, :], ones_col[:, :], pk,
                    start=(ob == 0), stop=(ob == nt - 1)),
                   reads=["ones"] + pkres, writes=["PSq2"])
                OP("tensor", lambda T, sq=sq, ob=ob: T.matmul(
                    Q4(3)[0:1, :], ones_col[:, :], sq,
                    start=(ob == 0), stop=(ob == nt - 1)),
                   reads=["ones", sqres], writes=["PSq3"])
                dec_steps(1)

        def p4_mid1(s):
            stats_rows(par=0, off=s * cw)
            ln_finalize(cw, par=0, off=s * cw)
            bcast_rows(cw, off=s * cw, banks=(6, 7))

        def p4_back1(s):
            sl = slice(s * cw, (s + 1) * cw)
            for t in range(nt):
                stg, stres, rsd, rres = (
                    (tf3, "tf3", tf4, "tf4") if t % 2 == 0
                    else (stat_ld, "stld", tf5, "tf5"))
                pk, pkres = park_a(s, t)
                dma("sync", rsd[:, 0:cw], xownT[t][:, sl], writes=[rres])
                OP("vector", lambda V, pk=pk, stg=stg: V.tensor_mul(
                    stg[:, 0:cw], pk, Q4(6)),
                   reads=pkres + ["PSq6"], writes=[stres])
                OP("vector", lambda V, stg=stg: V.tensor_sub(
                    stg[:, 0:cw], stg[:, 0:cw], Q4(7)),
                   reads=[stres, "PSq7"], writes=[stres])
                OP("vector", lambda V, stg=stg, rsd=rsd: V.tensor_add(
                    stg[:, 0:cw], stg[:, 0:cw], rsd[:, 0:cw]),
                   reads=[stres, rres], writes=[stres])
                dma("sync", r2_hb[t][:, sl], stg[:, 0:cw], reads=[stres],
                    writes=["r2hb"])
                pr, prres = park_r2(t)
                OP("scalar", lambda A, stg=stg, pr=pr: A.activation(
                    pr, stg[:, 0:cw], AF.Copy),
                   reads=[stres], writes=prres)
                sq, sqres = ((stat_sq[:, 0:cw], "stsq") if t % 2 == 0
                             else (sqb, "tf2"))
                OP("scalar", lambda A, stg=stg, sq=sq: A.activation(
                    sq, stg[:, 0:cw], AF.Square),
                   reads=[stres], writes=[sqres])
                OP("tensor", lambda T, pr=pr, t=t: T.matmul(
                    Q4(4)[0:1, :], ones_col[:, :], pr,
                    start=(t == 0), stop=(t == nt - 1)),
                   reads=["ones"] + prres, writes=["PSq4"])
                OP("tensor", lambda T, sq=sq, t=t: T.matmul(
                    Q4(5)[0:1, :], ones_col[:, :], sq[:, 0:cw],
                    start=(t == 0), stop=(t == nt - 1)),
                   reads=["ones", sqres], writes=["PSq5"])
                dec_steps(1)
            OP("vector", lambda V, s=s: V.tensor_copy(
                R0[0:1, s * cw:(s + 1) * cw], Q4(4)[0:1, :]),
               reads=["PSq4"], writes=["R0"])
            OP("vector", lambda V, s=s: V.tensor_copy(
                R1[0:1, s * cw:(s + 1) * cw], Q4(5)[0:1, :]),
               reads=["PSq5"], writes=["R1"])

        def p4_mid2(s):
            ln_finalize(cw, par=0, off=s * cw)
            bcast_rows(cw, off=s * cw, banks=(6, 7))

        def p4_back2(s):
            for t in range(nt):
                stg, stres = ((tf3, "tf3") if t % 2 == 0 else (tf2, "tf2"))
                pr, prres = park_r2(t)
                OP("vector", lambda V, pr=pr, stg=stg: V.tensor_mul(
                    stg[:, 0:cw], pr, Q4(6)),
                   reads=prres + ["PSq6"], writes=[stres])
                lo = t * c.s_own + s * cw
                xn2 = big1[:, lo:lo + cw]
                x2res = pages("big1", lo, lo + cw)
                OP("vector", lambda V, xn2=xn2, stg=stg: V.tensor_sub(
                    xn2, stg[:, 0:cw], Q4(7)),
                   reads=[stres, "PSq7"], writes=x2res)
            dec_steps(2)

        p4_front(0)
        for s in range(c.n_oc):
            if s >= 1:
                p4_mid2(s - 1)
                p4_back2(s - 1)
            p4_mid1(s)
            if s + 1 < c.n_oc:
                p4_front(s + 1)
            p4_back1(s)
        p4_mid2(c.n_oc - 1)
        p4_back2(c.n_oc - 1)

        def xn2_t(t, s):
            lo = t * c.s_own + s * cw
            return big1[:, lo:lo + cw], pages("big1", lo, lo + cw)

        hbufs = [(big0[:, ic * cw:(ic + 1) * cw],
                  pages("big0", ic * cw, (ic + 1) * cw)) for ic in range(it)]
        fstg_lo = it * cw
        fstg = big0[:, fstg_lo:fstg_lo + cw]
        m_fstg = pages("big0", fstg_lo, fstg_lo + cw)
        sgb = big0[:, fstg_lo + cw:fstg_lo + 2 * cw]
        m_sgb = pages("big0", fstg_lo + cw, fstg_lo + 2 * cw)

        for sbi in range(c.n_oc):
            sl = slice(sbi * cw, (sbi + 1) * cw)
            for ic in range(it):
                wg_s = wg_hb[:, ic * nt:(ic + 1) * nt, :].rearrange(
                    "p t f -> p (t f)")
                wu_s = wu_hb[:, ic * nt:(ic + 1) * nt, :].rearrange(
                    "p t f -> p (t f)")
                dma("sync", wbuf0[:, :], wg_s, reads=["dec_o"],
                    writes=["wb0"])
                dma("sync", wbuf1[:, :], wu_s, reads=["dec_o"],
                    writes=["wb1"])
                for hc in range(nt):
                    x2, x2res = xn2_t(hc, sbi)
                    OP("tensor", lambda T, hc=hc, x2=x2: T.matmul(
                        Q4(0), wbuf0[:, hc * 128:(hc + 1) * 128], x2,
                        start=(hc == 0), stop=(hc == nt - 1)),
                       reads=["wb0"] + x2res, writes=["PSq0"])
                for hc in range(nt):
                    x2, x2res = xn2_t(hc, sbi)
                    OP("tensor", lambda T, hc=hc, x2=x2: T.matmul(
                        Q4(1), wbuf1[:, hc * 128:(hc + 1) * 128], x2,
                        start=(hc == 0), stop=(hc == nt - 1)),
                       reads=["wb1"] + x2res, writes=["PSq1"])
                OP("scalar", lambda A: A.activation(sgb, Q4(0), AF.Silu),
                   reads=["PSq0"], writes=m_sgb)
                hb, hres = hbufs[ic]
                OP("vector", lambda V, hb=hb: V.tensor_mul(hb, sgb, Q4(1)),
                   reads=m_sgb + ["PSq1"], writes=hres)
                if sbi == 0:
                    dec_steps(4)
            if sbi == 0:
                dec_flush()
            for ob in range(nt):
                bank = 2 + ob % 2
                for icg in range(0, it, 16):
                    ilen = min(16, it - icg)
                    wd_s = wd_hb[:, ob * it + icg:ob * it + icg + ilen, :]\
                        .rearrange("p t f -> p (t f)")
                    wb, wres = ((wbuf0, "wb0") if (icg // 16) % 2 == 0 else
                                (wbuf1, "wb1"))
                    dma("sync", wb[:, 0:ilen * 128], wd_s, reads=["dec_o"],
                        writes=[wres])
                    for i2 in range(ilen):
                        ic = icg + i2
                        hb, hres = hbufs[ic]
                        OP("tensor", lambda T, i2=i2, ic=ic, hb=hb, wb=wb,
                           bank=bank: T.matmul(
                            Q4(bank), wb[:, i2 * 128:(i2 + 1) * 128], hb,
                            start=(ic == 0), stop=(ic == it - 1)),
                           reads=[wres] + hres, writes=[f"PSq{bank}"])
                OP("scalar", lambda A, bank=bank: A.activation(
                    fstg, Q4(bank), AF.Copy),
                   reads=[f"PSq{bank}"], writes=m_fstg)
                dma("sync", f_hb[ob][:, sl], fstg, reads=m_fstg,
                    writes=["fhb"])
                sq, sqres = ((stat_sq[:, 0:cw], "stsq") if ob % 2 == 0
                             else (sqb, "tf2"))
                OP("scalar", lambda A, sq=sq: A.activation(
                    sq, fstg, AF.Square), reads=m_fstg, writes=[sqres])
                OP("tensor", lambda T, ob=ob: T.matmul(
                    Q4(4)[0:1, :], ones_col[:, :], fstg,
                    start=(ob == 0), stop=(ob == nt - 1)),
                   reads=["ones"] + m_fstg, writes=["PSq4"])
                OP("tensor", lambda T, sq=sq, ob=ob: T.matmul(
                    Q4(5)[0:1, :], ones_col[:, :], sq,
                    start=(ob == 0), stop=(ob == nt - 1)),
                   reads=["ones", sqres], writes=["PSq5"])
            OP("vector", lambda V, sbi=sbi: V.tensor_copy(
                R0[0:1, sbi * cw:(sbi + 1) * cw], Q4(4)[0:1, :]),
               reads=["PSq4"], writes=["R0"])
            OP("vector", lambda V, sbi=sbi: V.tensor_copy(
                R1[0:1, sbi * cw:(sbi + 1) * cw], Q4(5)[0:1, :]),
               reads=["PSq5"], writes=["R1"])

        ln_finalize(c.s_own)
        for s in range(c.n_oc):
            sl = slice(s * cw, (s + 1) * cw)
            bcast_rows(cw, off=s * cw)
            for t in range(nt):
                ld, ldres, stg, stres, rsd, rres = (
                    (ldb_v, "stld", tf3, "tf3", tf4, "tf4") if t % 2 == 0
                    else (tf1b, "tf1", tf2, "tf2", tf5, "tf5"))
                dma("sync", ld, f_hb[t][:, sl], reads=["fhb"],
                    writes=[ldres])
                OP("vector", lambda V, ld=ld, stg=stg: V.tensor_mul(
                    stg[:, 0:cw], ld, Q4(4)),
                   reads=[ldres, "PSq4"], writes=[stres])
                OP("vector", lambda V, stg=stg: V.tensor_sub(
                    stg[:, 0:cw], stg[:, 0:cw], Q4(5)),
                   reads=[stres, "PSq5"], writes=[stres])
                dma("sync", rsd[:, 0:cw], r2_hb[t][:, sl],
                    reads=["r2hb"], writes=[rres])
                OP("vector", lambda V, stg=stg, rsd=rsd: V.tensor_add(
                    stg[:, 0:cw], stg[:, 0:cw], rsd[:, 0:cw]),
                   reads=[stres, rres], writes=[stres])
                dma("sync", out[t][:, sl], stg[:, 0:cw], reads=[stres],
                    writes=["outd"])

        for eng in ENGINES:
            for q in ("dS", "dG", "dA"):
                for k in range(N_DMA_SEMS):
                    semname = f"{q}{k}"
                    if semname in em.counts:
                        em.wait(eng, (semname, em.counts[semname]))

        with nc.Block() as block:
            em.run(block)

    return nc


def _stage_inputs(c, inputs):
    x = np.asarray(inputs["L0_attn_hidden_states"])
    B = x.shape[0]
    lut = np.asarray(inputs["L0_ffn_lut"]).astype(np.float32)

    wqT = np.ascontiguousarray(np.asarray(inputs["L0_attn_Wq"]).T)
    wkT = np.ascontiguousarray(np.asarray(inputs["L0_attn_Wk"]).T)
    wvT = np.ascontiguousarray(np.asarray(inputs["L0_attn_Wv"]).T)
    woT = np.ascontiguousarray(np.asarray(inputs["L0_attn_Wo"]).T)
    gate = np.ascontiguousarray(
        np.asarray(inputs["L0_attn_gate"]).reshape(1, c.nt).astype(np.float32))

    inv = (1.0 / (10000.0 ** (np.arange(0, c.dm, 2, dtype=np.float64)
                              / c.dm))).astype(np.float64)
    t = np.arange(c.s_full, dtype=np.float64)
    fr = np.outer(inv, t)
    cosk = np.ascontiguousarray(
        np.cos(fr).astype(np.float32).reshape(c.hp, 128, c.s_full))
    sink = np.ascontiguousarray(
        np.sin(fr).astype(np.float32).reshape(c.hp, 128, c.s_full))

    gi = np.asarray(inputs["L0_ffn_gate_idx"]).astype(np.int32)
    ui = np.asarray(inputs["L0_ffn_up_idx"]).astype(np.int32)
    di = np.asarray(inputs["L0_ffn_down_idx"]).astype(np.int32)
    G4 = gi.reshape(c.it, 128, c.nt, 128)
    gidx = np.ascontiguousarray(
        G4.transpose(3, 0, 2, 1).reshape(128, c.ngu, 128))
    U4 = ui.reshape(c.it, 128, c.nt, 128)
    uidx = np.ascontiguousarray(
        U4.transpose(3, 0, 2, 1).reshape(128, c.ngu, 128))
    D4 = di.reshape(c.nt, 128, c.it, 128)
    didx = np.ascontiguousarray(
        D4.transpose(3, 0, 2, 1).reshape(128, c.nd, 128))

    n_cores = 8
    q_per_b = n_cores // B
    in_maps = []
    for core in range(n_cores):
        b = core // q_per_b
        q = core % q_per_b
        xTb = np.ascontiguousarray(x[b].T).reshape(c.nt, 128, c.s_full)
        xown = np.ascontiguousarray(
            x[b, q * c.s_own:(q + 1) * c.s_own, :].T).reshape(
            c.nt, 128, c.s_own)
        cq = np.ascontiguousarray(
            cosk[:, :, q * c.s_own:(q + 1) * c.s_own])
        sq = np.ascontiguousarray(
            sink[:, :, q * c.s_own:(q + 1) * c.s_own])
        in_maps.append({
            "xT": xTb, "xownT": xown, "wqT": wqT, "wkT": wkT, "wvT": wvT,
            "woT": woT, "gate": gate, "cosq": cq, "sinq": sq,
            "cosk": cosk, "sink": sink, "gidx": gidx, "uidx": uidx,
            "didx": didx,
        })
    return in_maps, lut


def kernel(**inputs):
    c = Cfg()
    in_maps, lut = _stage_inputs(c, inputs)
    nc = build_program(c, [float(v) for v in lut])
    res = run_bass_kernel_spmd(nc, in_maps, core_ids=list(range(8)))
    x = np.asarray(inputs["L0_attn_hidden_states"])
    B, S, H = x.shape
    out = np.empty((B, S, H), dtype=np.float32)
    q_per_b = 8 // B
    for core in range(8):
        b = core // q_per_b
        q = core % q_per_b
        o = res.results[core]["outT"].reshape(c.dm, c.s_own)
        out[b, q * c.s_own:(q + 1) * c.s_own, :] = o.T
    return out


# version 16 (final)
# speedup vs baseline: 1.0649x; 1.0308x over previous
import math
from contextlib import ExitStack

import numpy as np
import ml_dtypes

import concourse.bass as bass
import concourse.mybir as mybir
from concourse.bass_utils import run_bass_kernel_spmd

F32 = mybir.dt.float32
BF16 = mybir.dt.bfloat16
I32 = mybir.dt.int32
AF = mybir.ActivationFunctionType
ALU = mybir.AluOpType

ENGINES = ("tensor", "vector", "scalar", "gpsimd", "sync")
DMA_SEM = {"sync": "dS", "gpsimd": "dG", "scalar": "dA"}
ENG_SEM = {"tensor": "tS", "vector": "vS", "scalar": "aS", "gpsimd": "gS",
           "sync": "sS"}


class Emitter:
    def __init__(self, nc):
        self.nc = nc
        self.q = {e: [] for e in ENGINES}
        self.sems = {}
        self.counts = {}
        self.last_wait = {}

    def sem(self, name):
        if name not in self.sems:
            self.sems[name] = self.nc.alloc_semaphore(name)
            self.counts[name] = 0
        return self.sems[name]

    def wait(self, eng, ms):
        if ms is None:
            return
        semname, val = ms
        if val <= 0:
            return
        key = (eng, semname)
        if self.last_wait.get(key, 0) >= val:
            return
        self.last_wait[key] = val
        s = self.sem(semname)
        self.q[eng].append(lambda E, s=s, v=val: E.wait_ge(s, v))

    def emit(self, eng, fn, inc):
        semname, k = inc
        s = self.sem(semname)
        self.counts[semname] += k

        def wrapped(E, fn=fn, s=s, k=k):
            fn(E).then_inc(s, k)

        self.q[eng].append(wrapped)
        return (semname, self.counts[semname])

    def run(self, block):
        for e in ENGINES:
            fns = self.q[e]

            def body(E, fns=fns):
                for f in fns:
                    f(E)

            getattr(block, e)(body)


N_DMA_SEMS = 8


class SB:

    def __init__(self, em):
        self.em = em
        self.state = {}
        self.rr = {}

    def _st(self, r):
        if r not in self.state:
            self.state[r] = {"w": None, "r": []}
        return self.state[r]

    def _wait(self, eng, ms):
        if ms is None:
            return
        if eng == "tensor" and ms[0] == "tS":
            return
        self.em.wait(eng, ms)

    def op(self, eng, fn, reads=(), writes=(), dma=False):
        for r in reads:
            self._wait(eng, self._st(r)["w"])
        for r in writes:
            st = self._st(r)
            self._wait(eng, st["w"])
            for ms in st["r"]:
                self._wait(eng, ms)
        if dma:
            q = DMA_SEM[eng]
            k = self.rr.get(q, 0)
            self.rr[q] = (k + 1) % N_DMA_SEMS
            sem = f"{q}{k}"
            prev = self.em.counts.get(sem, 0)
            if prev:
                self.em.wait(eng, (sem, prev))
            ms = self.em.emit(eng, fn, inc=(sem, 16))
        else:
            ms = self.em.emit(eng, fn, inc=(ENG_SEM[eng], 1))
        for r in writes:
            st = self._st(r)
            st["w"] = ms
            st["r"] = []
        for r in reads:
            self._st(r)["r"].append(ms)
        return ms


class Cfg:
    def __init__(self, dm=2048, s_full=8192, s_own=2048, inter=5504):
        self.dm = dm
        self.nt = dm // 128
        self.hp = self.nt // 2
        self.s_full = s_full
        self.s_own = s_own
        self.inter = inter
        self.it = inter // 128
        self.cw = 512 if s_own % 512 == 0 else 128
        self.n_ch = s_full // self.cw
        self.n_oc = s_own // self.cw
        self.n_kc = s_full // 128
        self.scale = 1.0 / math.sqrt(128.0)
        self.eps = 1e-5
        self.ngu = self.it * self.nt
        self.nd = self.nt * self.it
        self.dec_slab = 8


def pages(name, lo, hi, grain=2048):
    return [f"{name}_p{p}" for p in range(lo // grain,
                                          (hi + grain - 1) // grain)]


def build_program(c, lut_vals):
    nc = bass.Bass("TRN2", target_bir_lowering=False)
    cw = c.cw
    nt, hp, it = c.nt, c.hp, c.it

    xT = nc.dram_tensor("xT", [nt, 128, c.s_full], F32,
                        kind="ExternalInput").ap()
    xownT = nc.dram_tensor("xownT", [nt, 128, c.s_own], F32,
                           kind="ExternalInput").ap()
    wqT = nc.dram_tensor("wqT", [c.dm, c.dm], F32, kind="ExternalInput").ap()
    wkT = nc.dram_tensor("wkT", [c.dm, c.dm], F32, kind="ExternalInput").ap()
    wvT = nc.dram_tensor("wvT", [c.dm, c.dm], F32, kind="ExternalInput").ap()
    woT = nc.dram_tensor("woT", [c.dm, c.dm], F32, kind="ExternalInput").ap()
    gate_in = nc.dram_tensor("gate", [1, nt], F32, kind="ExternalInput").ap()
    cosq = nc.dram_tensor("cosq", [hp, 128, c.s_own], F32,
                          kind="ExternalInput").ap()
    sinq = nc.dram_tensor("sinq", [hp, 128, c.s_own], F32,
                          kind="ExternalInput").ap()
    cosk = nc.dram_tensor("cosk", [hp, 128, c.s_full], F32,
                          kind="ExternalInput").ap()
    sink = nc.dram_tensor("sink", [hp, 128, c.s_full], F32,
                          kind="ExternalInput").ap()
    gidx = nc.dram_tensor("gidx", [128, c.ngu, 128], I32,
                          kind="ExternalInput").ap()
    uidx = nc.dram_tensor("uidx", [128, c.ngu, 128], I32,
                          kind="ExternalInput").ap()
    didx = nc.dram_tensor("didx", [128, c.nd, 128], I32,
                          kind="ExternalInput").ap()
    out = nc.dram_tensor("outT", [nt, 128, c.s_own], F32,
                         kind="ExternalOutput").ap()

    wq_bf = nc.dram_tensor("wq_bf", [128, nt, c.dm], BF16).ap()
    wk_bf = nc.dram_tensor("wk_bf", [128, nt, c.dm], BF16).ap()
    wo_bf = nc.dram_tensor("wo_bf", [128, nt, c.dm], BF16).ap()
    wv_bf = nc.dram_tensor("wv_bf", [128, nt, c.dm], BF16).ap()
    xn_hb = nc.dram_tensor("xn_hb", [nt, 128, c.s_full], BF16).ap()
    kT_h = nc.dram_tensor("kT_h", [nt, 128, c.s_full], BF16).ap()
    qT_h = nc.dram_tensor("qT_h", [nt, 128, c.s_own], BF16).ap()
    v_hb = nc.dram_tensor("v_hb", [c.s_full, c.dm], BF16).ap()
    a_hb = nc.dram_tensor("a_hb", [nt, 128, c.s_own], BF16).ap()
    r2_hb = nc.dram_tensor("r2_hb", [nt, 128, c.s_own], F32).ap()
    f_hb = nc.dram_tensor("f_hb", [nt, 128, c.s_own], BF16).ap()
    wg_hb = nc.dram_tensor("wg_hb", [128, c.ngu, 128], BF16).ap()
    wu_hb = nc.dram_tensor("wu_hb", [128, c.ngu, 128], BF16).ap()
    wd_hb = nc.dram_tensor("wd_hb", [128, c.nd, 128], BF16).ap()

    ident_d = nc.inline_tensor(np.eye(128, dtype=ml_dtypes.bfloat16),
                               "ident").ap()

    stack = ExitStack()

    def sbuf(name, shape, dt):
        return stack.enter_context(nc.sbuf_tensor(name, shape, dt))

    with stack:
        big0_w = max(2 * nt * cw + (cw // 128) * c.dm,
                     3 * c.s_full,
                     hp * c.n_oc * cw,
                     it * cw + 2 * cw)
        big0 = sbuf("big0", [128, big0_w], BF16)
        big1 = sbuf("big1", [128, max(nt * c.s_own, nt * c.dm)], BF16)
        wp = sbuf("wp", [128, 8 * cw], BF16)
        mid_w = max(4 * cw, 2 * c.s_own + 3 * 2 * cw)
        mid = sbuf("mid", [128, mid_w], BF16)
        stat_ld = sbuf("stat_ld", [128, cw], F32)
        stat_sq = sbuf("stat_sq", [128, cw], BF16)
        tf1 = sbuf("tf1", [128, cw], F32)
        tf2 = sbuf("tf2", [128, cw], F32)
        tf3 = sbuf("tf3", [128, cw], F32)
        tf4 = sbuf("tf4", [128, cw], F32)
        wbuf0 = sbuf("wbuf0", [128, c.dm], BF16)
        wbuf1 = sbuf("wbuf1", [128, c.dm], BF16)
        memory = sbuf("memory", [128, nt * 129], F32)
        memb = sbuf("memb", [128, 132], BF16)
        dec_bs = [sbuf("dec_b0", [128, 128 * c.dec_slab], BF16),
                  sbuf("dec_b1", [128, 128 * c.dec_slab], BF16)]
        dec_ts = [sbuf("dec_t0", [128, 128 * c.dec_slab], BF16),
                  sbuf("dec_t1", [128, 128 * c.dec_slab], BF16)]
        dec_as = [sbuf("dec_a0", [128, 128 * c.dec_slab], BF16),
                  sbuf("dec_a1", [128, 128 * c.dec_slab], BF16)]
        R0 = sbuf("R0", [1, c.s_own], F32)
        R1 = sbuf("R1", [1, c.s_own], F32)
        R2 = sbuf("R2", [1, c.s_own], F32)
        R0b = sbuf("R0b", [1, cw], F32)
        R1b = sbuf("R1b", [1, cw], F32)
        R2b = sbuf("R2b", [1, cw], F32)
        tf5 = sbuf("tf5", [128, cw], F32)
        mq2 = sbuf("mq2", [1, nt], F32)
        mk2 = sbuf("mk2", [1, nt], F32)
        g_row = sbuf("g_row", [1, nt], F32)
        og_row = sbuf("og_row", [1, nt], F32)
        gcol = sbuf("gcol", [1, 128], F32)
        ogcol = sbuf("ogcol", [1, 128], F32)
        ones_row = sbuf("ones_row", [1, 128], F32)
        ones_col = sbuf("ones_col", [128, 1], BF16)
        biasC = sbuf("biasC", [128, 1], F32)
        sc1 = sbuf("sc1", [1, 8], F32)
        ident_s = sbuf("ident_s", [128, 128], BF16)

        ps = stack.enter_context(nc.psum_tensor("ps", [128, 4096], F32))
        psb = ps[:, 3584:3584 + cw // 2].bitcast(BF16)

        def Q4(i, w=None):
            w = cw if w is None else w
            return ps[:, i * 512:i * 512 + w]

        def QPAIR(par):
            base = par * 1024
            return ps[:, base:base + 1024].rearrange(
                "p (b x) -> p b x", x=512)[:, :, 0:cw]

        DEN = ps[0:1, 3584:3584 + cw]
        PVPS = ps[:, 3072:3072 + cw]
        MEMPS = ps[:, 3072:3201]

        em = Emitter(nc)
        sb = SB(em)
        OP = sb.op

        def dma(eng, dst, src, reads=(), writes=()):
            return OP(eng, lambda E, d=dst, s=src: E.dma_start(out=d, in_=s),
                      reads=reads, writes=writes, dma=True)

        QA = wp[:, 0:cw]
        QB = wp[:, cw:2 * cw]
        QO = wp[:, 2 * cw:3 * cw]
        QO2 = wp[:, 3 * cw:4 * cw]
        K3A = wp[:, 4 * cw:5 * cw]
        K2A = wp[:, 5 * cw:6 * cw]
        K2B = wp[:, 6 * cw:7 * cw]
        K3B = wp[:, 7 * cw:8 * cw]
        sqb = tf2.bitcast(BF16)[:, 0:cw]
        tf1b = tf1.bitcast(BF16)[:, 0:cw]
        ldb_v = stat_ld.bitcast(BF16)[:, 0:cw]
        midf = mid.bitcast(F32)
        cos_s = [midf[:, 1024 * p:1024 * p + cw] for p in range(3)]
        sin_s = [midf[:, 1024 * p + cw:1024 * p + 2 * cw] for p in range(3)]
        m_cos_s = [pages("mid", 2048 * p, 2048 * p + 1024, 1024)
                   for p in range(3)]
        m_sin_s = [pages("mid", 2048 * p + 1024, 2048 * p + 2048, 1024)
                   for p in range(3)]
        cosb, sinb = cos_s[0], sin_s[0]
        m_cosb, m_sinb = m_cos_s[0], m_sin_s[0]
        SQK = mid[:, 6144:6144 + cw]
        m_SQK = pages("mid", 6144, 6144 + cw, 1024)

        OP("vector", lambda V: V.memset(ones_col[:, :], 1.0), writes=["ones"])
        OP("vector", lambda V: V.memset(ones_row[:, :], 1.0), writes=["onesr"])
        OP("vector", lambda V: V.memset(memory[:, :], 0.0), writes=["memory"])
        OP("vector", lambda V: V.memset(mq2[:, :], 0.0), writes=["mq2"])
        OP("vector", lambda V: V.memset(mk2[:, :], 0.0), writes=["mk2"])
        dma("sync", ident_s[:, :], ident_d[:, :], writes=["ident"])
        dma("sync", g_row[:, :], gate_in[:, :], writes=["grow"])
        OP("scalar", lambda A: A.activation(og_row[:, :], g_row[:, :], AF.Exp,
                                            scale=-1.0),
           reads=["grow"], writes=["ogrow"])
        OP("vector", lambda V: V.tensor_scalar_add(og_row[:, :], og_row[:, :],
                                                   1.0),
           reads=["ogrow"], writes=["ogrow"])
        OP("vector", lambda V: V.reciprocal(g_row[:, :], og_row[:, :]),
           reads=["ogrow"], writes=["grow"])
        OP("vector", lambda V: V.tensor_scalar(og_row[:, :], g_row[:, :],
                                               -1.0, 1.0, ALU.mult, ALU.add),
           reads=["grow"], writes=["ogrow"])
        for (wsrc, wdst, res) in ((wqT, wq_bf, "wqb"), (wkT, wk_bf, "wkb"),
                                  (woT, wo_bf, "wob")):
            s4 = wsrc.rearrange("(hc p) (m f) -> p m hc f", p=128, f=128)
            d4 = wdst.rearrange("p m (hc f) -> p m hc f", f=128)
            for m in range(nt):
                dma("gpsimd", d4[:, m], s4[:, m], writes=[res])
        sv = wvT.rearrange("(hc p) f -> p hc f", p=128)
        for hc in range(nt):
            dma("gpsimd", wv_bf[:, hc], sv[:, hc], writes=["wvb"])

        dec_jobs = []
        for src, dst, ntile in ((gidx, wg_hb, c.ngu), (uidx, wu_hb, c.ngu),
                                (didx, wd_hb, c.nd)):
            for a0 in range(0, ntile, c.dec_slab):
                w = min(c.dec_slab, ntile - a0) * 128
                dec_jobs.append((src[:, a0:a0 + w // 128, :],
                                 dst[:, a0:a0 + w // 128, :], w))

        def _dec_load(n):
            src, dst, w = dec_jobs[n]
            par = n % 2
            src2 = src.rearrange("p t f -> p (t f)")
            dma("gpsimd", dec_bs[par][:, 0:w], src2,
                writes=[f"dec_b{par}"])

        def _dec_jg(n, k):
            src, dst, w = dec_jobs[n]
            par = n % 2
            add_eng = "gpsimd" if n % 4 == 0 else "vector"
            da, dares = dec_as[par], f"dec_a{par}"
            db, dbres = dec_bs[par], f"dec_b{par}"
            for j in range(4 * k, 4 * k + 4):
                if j == 0:
                    OP("vector", lambda V, w=w, da=da, db=db: V.tensor_scalar(
                        da[:, 0:w], db[:, 0:w], 0.0, float(lut_vals[0]),
                        ALU.is_equal, ALU.mult),
                       reads=[dbres], writes=[dares])
                else:
                    tb, tres = dec_ts[j % 2], f"dec_t{j % 2}"
                    OP("vector", lambda V, w=w, j=j, tb=tb, db=db:
                       V.tensor_scalar(
                        tb[:, 0:w], db[:, 0:w], float(j),
                        float(lut_vals[j]), ALU.is_equal, ALU.mult),
                       reads=[dbres], writes=[tres])
                    OP(add_eng, lambda E, w=w, da=da, tb=tb: E.tensor_add(
                        da[:, 0:w], da[:, 0:w], tb[:, 0:w]),
                       reads=[tres, dares], writes=[dares])

        def _dec_store(n):
            src, dst, w = dec_jobs[n]
            par = n % 2
            dst2 = dst.rearrange("p t f -> p (t f)")
            dma("gpsimd", dst2, dec_as[par][:, 0:w],
                reads=[f"dec_a{par}"], writes=["dec_o"])

        dec_seq = []
        N_dec = len(dec_jobs)
        dec_seq.append(lambda: _dec_load(0))
        for n in range(N_dec):
            if n + 1 < N_dec:
                dec_seq.append(lambda n=n: _dec_load(n + 1))
            for k in range(4):
                dec_seq.append(lambda n=n, k=k: _dec_jg(n, k))
            dec_seq.append(lambda n=n: _dec_store(n))
        dec_pos = [0]

        def dec_steps(n):
            while n > 0 and dec_pos[0] < len(dec_seq):
                dec_seq[dec_pos[0]]()
                dec_pos[0] += 1
                n -= 1

        def dec_flush():
            dec_steps(len(dec_seq))

        RS = (((R0, "R0"), (R1, "R1"), (R2, "R2")),
              ((R0b, "R0b"), (R1b, "R1b"), (R2b, "R2b")))

        def ln_finalize(width, par=0, off=0):
            (Rx0, r0), (Rx1, r1), (Rx2, r2) = RS[par]
            sl = slice(off, off + width)
            inv_n = 1.0 / c.dm
            OP("vector", lambda V: V.tensor_scalar_mul(
                Rx0[0:1, sl], Rx0[0:1, sl], inv_n),
               reads=[r0], writes=[r0])
            OP("vector", lambda V: V.tensor_scalar_mul(
                Rx1[0:1, sl], Rx1[0:1, sl], inv_n),
               reads=[r1], writes=[r1])
            OP("vector", lambda V: V.tensor_mul(
                Rx2[0:1, sl], Rx0[0:1, sl], Rx0[0:1, sl]),
               reads=[r0], writes=[r2])
            OP("vector", lambda V: V.tensor_sub(
                Rx1[0:1, sl], Rx1[0:1, sl], Rx2[0:1, sl]),
               reads=[r1, r2], writes=[r1])
            OP("vector", lambda V: V.tensor_scalar_add(
                Rx1[0:1, sl], Rx1[0:1, sl], float(c.eps)),
               reads=[r1], writes=[r1])
            OP("scalar", lambda A: A.activation(
                Rx1[0:1, sl], Rx1[0:1, sl], AF.Ln),
               reads=[r1], writes=[r1])
            OP("scalar", lambda A: A.activation(
                Rx1[0:1, sl], Rx1[0:1, sl], AF.Exp, scale=-0.5),
               reads=[r1], writes=[r1])
            OP("vector", lambda V: V.tensor_mul(
                Rx2[0:1, sl], Rx0[0:1, sl], Rx1[0:1, sl]),
               reads=[r0, r1], writes=[r2])

        def bcast_rows(width, off=0, par=0, banks=(4, 5)):
            (Rx0, r0), (Rx1, r1), (Rx2, r2) = RS[par]
            b0, b1 = banks
            OP("tensor", lambda T: T.matmul(
                Q4(b0, width), ones_row[:, :], Rx1[0:1, off:off + width],
                start=True, stop=True),
               reads=["onesr", r1], writes=[f"PSq{b0}"])
            OP("tensor", lambda T: T.matmul(
                Q4(b1, width), ones_row[:, :], Rx2[0:1, off:off + width],
                start=True, stop=True),
               reads=["onesr", r2], writes=[f"PSq{b1}"])

        def stats_tiles(load_fn, keep, t0, t1):
            ld_bufs = ((stat_ld[:, 0:cw], "stld"), (tf1[:, 0:cw], "tf1"))
            sq_bufs = ((stat_sq[:, 0:cw], "stsq"), (sqb, "tf2"))
            for t in range(t0, t1):
                ld, ldres = ld_bufs[t % 2]
                sq, sqres = sq_bufs[t % 2]
                kd, kres = keep(t)
                dma("sync", ld, load_fn(t), writes=[ldres])
                OP("scalar", lambda A, ld=ld, kd=kd: A.activation(
                    kd, ld, AF.Copy), reads=[ldres], writes=kres)
                OP("scalar", lambda A, ld=ld, sq=sq: A.activation(
                    sq, ld, AF.Square), reads=[ldres], writes=[sqres])
                OP("tensor", lambda T, kd=kd, t=t: T.matmul(
                    Q4(2)[0:1, :], ones_col[:, :], kd,
                    start=(t == 0), stop=(t == nt - 1)),
                   reads=["ones"] + kres, writes=["PSq2"])
                OP("tensor", lambda T, sq=sq, t=t: T.matmul(
                    Q4(3)[0:1, :], ones_col[:, :], sq,
                    start=(t == 0), stop=(t == nt - 1)),
                   reads=["ones", sqres], writes=["PSq3"])

        def stats_rows(par=0, off=0):
            (Rx0, r0), (Rx1, r1), _ = RS[par]
            OP("vector", lambda V: V.tensor_copy(Rx0[0:1, off:off + cw],
                                                 Q4(2)[0:1, :]),
               reads=["PSq2"], writes=[r0])
            OP("vector", lambda V: V.tensor_copy(Rx1[0:1, off:off + cw],
                                                 Q4(3)[0:1, :]),
               reads=["PSq3"], writes=[r1])

        def chunk_stats(load_fn, keep, par=0):
            stats_tiles(load_fn, keep, 0, nt)
            stats_rows(par)

        def hb_stats(src_hb, src_f32, extra_reads=()):
            for s in range(c.n_oc):
                sl = slice(s * cw, (s + 1) * cw)
                for t in range(nt):
                    if src_f32:
                        ld, ldres = ((stat_ld[:, 0:cw], "stld") if t % 2 == 0
                                     else (tf1[:, 0:cw], "tf1"))
                        dma("sync", ld, src_hb[t][:, sl],
                            reads=list(extra_reads), writes=[ldres])
                        mi, mires = ((stat_sq[:, 0:cw], "stsq") if t % 2 == 0
                                     else (sqb, "tf2"))
                        OP("scalar", lambda A, ld=ld, mi=mi: A.activation(
                            mi, ld, AF.Copy), reads=[ldres], writes=[mires])
                        OP("tensor", lambda T, mi=mi, t=t: T.matmul(
                            Q4(4)[0:1, :], ones_col[:, :], mi,
                            start=(t == 0), stop=(t == nt - 1)),
                           reads=["ones", mires], writes=["PSq4"])
                        OP("scalar", lambda A, ld=ld, mi=mi: A.activation(
                            mi, ld, AF.Square), reads=[ldres, mires],
                           writes=[mires])
                        OP("tensor", lambda T, mi=mi, t=t: T.matmul(
                            Q4(5)[0:1, :], ones_col[:, :], mi,
                            start=(t == 0), stop=(t == nt - 1)),
                           reads=["ones", mires], writes=["PSq5"])
                    else:
                        ld, ldres = ((ldb_v, "stld") if t % 2 == 0
                                     else (tf1b, "tf1"))
                        sq, sqres = ((stat_sq[:, 0:cw], "stsq") if t % 2 == 0
                                     else (sqb, "tf2"))
                        dma("sync", ld, src_hb[t][:, sl],
                            reads=list(extra_reads), writes=[ldres])
                        OP("scalar", lambda A, ld=ld, sq=sq: A.activation(
                            sq, ld, AF.Square), reads=[ldres],
                           writes=[sqres])
                        OP("tensor", lambda T, ld=ld, t=t: T.matmul(
                            Q4(4)[0:1, :], ones_col[:, :], ld,
                            start=(t == 0), stop=(t == nt - 1)),
                           reads=["ones", ldres], writes=["PSq4"])
                        OP("tensor", lambda T, sq=sq, t=t: T.matmul(
                            Q4(5)[0:1, :], ones_col[:, :], sq,
                            start=(t == 0), stop=(t == nt - 1)),
                           reads=["ones", sqres], writes=["PSq5"])
                OP("vector", lambda V, sl=sl: V.tensor_copy(
                    R0[0:1, sl], Q4(4)[0:1, :]), reads=["PSq4"],
                   writes=["R0"])
                OP("vector", lambda V, sl=sl: V.tensor_copy(
                    R1[0:1, sl], Q4(5)[0:1, :]), reads=["PSq5"],
                   writes=["R1"])

        def xn_own(t, s):
            lo = t * c.s_own + s * cw
            return big1[:, lo:lo + cw], pages("big1", lo, lo + cw)

        for s in range(c.n_oc):
            sl = slice(s * cw, (s + 1) * cw)
            chunk_stats(lambda t, sl=sl: xownT[t][:, sl],
                        lambda t, s=s: xn_own(t, s), par=s % 2)
            ln_finalize(cw, par=s % 2)
            bcast_rows(cw, par=s % 2)
            for t in range(nt):
                xn, xnres = xn_own(t, s)
                OP("vector", lambda V, xn=xn: V.tensor_mul(
                    tf3[:, 0:cw], xn, Q4(4)),
                   reads=xnres + ["PSq4"], writes=["tf3"])
                OP("vector", lambda V, xn=xn: V.tensor_sub(
                    xn, tf3[:, 0:cw], Q4(5)),
                   reads=["tf3", "PSq5"], writes=xnres)
            dec_steps(6)

        def wload(wbf, m, buf, res, src_res):
            dma("sync", buf[:, :], wbf[:, m, :], reads=[src_res],
                writes=[res])

        def rope_pair(getA, resA, outA, outAres, outB, outBres,
                      getB=None, resB=None, cs=None):
            getB = QB if getB is None else getB
            resB = ["QB"] if resB is None else resB
            cosv, sinv, m_cv, m_sv = ((cosb, sinb, m_cosb, m_sinb)
                                      if cs is None else cs)
            OP("vector", lambda V, aA=getA, cosv=cosv: V.tensor_mul(
                tf1[:, 0:cw], aA, cosv),
               reads=list(resA) + m_cv, writes=["tf1"])
            OP("vector", lambda V, aB=getB, sinv=sinv: V.tensor_mul(
                tf2[:, 0:cw], aB, sinv),
               reads=list(resB) + m_sv, writes=["tf2"])
            OP("vector", lambda V, o=outA: V.tensor_sub(o, tf1[:, 0:cw],
                                                        tf2[:, 0:cw]),
               reads=["tf1", "tf2"], writes=outAres)
            OP("vector", lambda V, aB=getB, cosv=cosv: V.tensor_mul(
                tf1[:, 0:cw], aB, cosv),
               reads=list(resB) + m_cv, writes=["tf1"])
            OP("vector", lambda V, aA=getA, sinv=sinv: V.tensor_mul(
                tf2[:, 0:cw], aA, sinv),
               reads=list(resA) + m_sv, writes=["tf2"])
            OP("vector", lambda V, o=outB: V.tensor_add(o, tf1[:, 0:cw],
                                                        tf2[:, 0:cw]),
               reads=["tf1", "tf2"], writes=outBres)

        def row_norm_max(src, sres, dstrow, dres, mh):
            OP("scalar", lambda A, src=src: A.activation(
                stat_sq[:, 0:cw], src, AF.Square),
               reads=sres, writes=["stsq"])
            OP("tensor", lambda T: T.matmul(
                Q4(2)[0:1, :], ones_col[:, :], stat_sq[:, 0:cw],
                start=True, stop=True),
               reads=["ones", "stsq"], writes=["PSq2"])
            OP("scalar", lambda A: A.activation(R2[0:1, 0:cw],
                                                Q4(2)[0:1, :], AF.Copy),
               reads=["PSq2"], writes=["R2"])
            OP("vector", lambda V: V.reduce_max(
                sc1[0:1, 0:1], R2[0:1, 0:cw], axis=mybir.AxisListType.X),
               reads=["R2"], writes=["sc1"])
            OP("vector", lambda V, mh=mh, dr=dstrow: V.tensor_max(
                dr[0:1, mh:mh + 1], dr[0:1, mh:mh + 1], sc1[0:1, 0:1]),
               reads=["sc1", dres], writes=[dres])

        wload(wq_bf, 0, wbuf0, "wb0", "wqb")
        for tp in range(hp):
            wload(wq_bf, tp + hp, wbuf1, "wb1", "wqb")
            for s in range(c.n_oc):
                for hc in range(nt):
                    rhs, rres = xn_own(hc, s)
                    OP("tensor", lambda T, hc=hc, rhs=rhs: T.matmul(
                        Q4(0), wbuf0[:, hc * 128:(hc + 1) * 128], rhs,
                        start=(hc == 0), stop=(hc == nt - 1)),
                       reads=["wb0"] + rres, writes=["PSq0"])
                lo = (tp * c.n_oc + s) * cw
                st_ = big0[:, lo:lo + cw]
                stres = pages("big0", lo, lo + cw)
                OP("scalar", lambda A, st_=st_: A.activation(st_, Q4(0),
                                                             AF.Copy),
                   reads=["PSq0"], writes=stres)
            if tp + 1 < hp:
                wload(wq_bf, tp + 1, wbuf0, "wb0", "wqb")
            for s in range(c.n_oc):
                sl = slice(s * cw, (s + 1) * cw)
                for hc in range(nt):
                    rhs, rres = xn_own(hc, s)
                    OP("tensor", lambda T, hc=hc, rhs=rhs: T.matmul(
                        Q4(1), wbuf1[:, hc * 128:(hc + 1) * 128], rhs,
                        start=(hc == 0), stop=(hc == nt - 1)),
                       reads=["wb1"] + rres, writes=["PSq1"])
                OP("scalar", lambda A: A.activation(QB, Q4(1), AF.Copy),
                   reads=["PSq1"], writes=["QB"])
                dma("sync", cosb, cosq[tp][:, sl], writes=m_cosb)
                dma("sync", sinb, sinq[tp][:, sl], writes=m_sinb)
                lo = (tp * c.n_oc + s) * cw
                st_ = big0[:, lo:lo + cw]
                stres = pages("big0", lo, lo + cw)
                rope_pair(st_, stres, QO, ["QO"], QO2, ["QO2"])
                dma("sync", qT_h[tp][:, sl], QO, reads=["QO"], writes=["qTh"])
                dma("sync", qT_h[tp + hp][:, sl], QO2, reads=["QO2"],
                    writes=["qTh"])
                row_norm_max(QO, ["QO"], mq2, "mq2", tp)
                row_norm_max(QO2, ["QO2"], mq2, "mq2", tp + hp)
            dec_steps(3)

        def xn_area(t, apar):
            lo = (apar * nt + t) * cw
            return big0[:, lo:lo + cw], pages("big0", lo, lo + cw)

        wv_b1 = big1[:, 0:nt * c.dm]
        m_wv = pages("big1", 0, nt * c.dm)
        dma("sync", wv_b1.rearrange("p (hc f) -> p hc f", f=c.dm), wv_bf,
            reads=["wvb"], writes=m_wv)
        n_sub = cw // 128
        vch_lo = 2 * nt * cw
        VB = (0, 1, 6, 7)

        def vch_sl(sub, d0, d1):
            lo = vch_lo + sub * c.dm + d0
            return (big0[:, lo:lo + (d1 - d0)],
                    pages("big0", lo, lo + (d1 - d0)))

        def c12_stats_part(ch, t0, t1):
            sl_f = slice(ch * cw, (ch + 1) * cw)
            apar = ch % 2
            stats_tiles(lambda t, sl_f=sl_f: xT[t][:, sl_f],
                        lambda t, apar=apar: xn_area(t, apar), t0, t1)

        def c12_norm(ch):
            sl_f = slice(ch * cw, (ch + 1) * cw)
            apar = ch % 2
            stats_rows(par=apar)
            ln_finalize(cw, par=apar)
            bcast_rows(cw, par=apar)
            for t in range(nt):
                xn, xnres = xn_area(t, apar)
                OP("vector", lambda V, xn=xn: V.tensor_mul(
                    tf3[:, 0:cw], xn, Q4(4)),
                   reads=xnres + ["PSq4"], writes=["tf3"])
                OP("vector", lambda V, xn=xn: V.tensor_sub(
                    xn, tf3[:, 0:cw], Q4(5)),
                   reads=["tf3", "PSq5"], writes=xnres)
                dma("sync", xn_hb[t][:, sl_f], xn, reads=xnres,
                    writes=["xnhb"])

        def c12_vmms(ch, dq):
            apar = ch % 2
            for sub in range(n_sub):
                bank = VB[sub]
                for hc in range(nt):
                    rhs = wv_b1[:, hc * c.dm + dq * 512:
                                hc * c.dm + (dq + 1) * 512]
                    xs, xsres = xn_area(hc, apar)
                    lhs = xs[:, sub * 128:(sub + 1) * 128]
                    OP("tensor", lambda T, hc=hc, bank=bank, rhs=rhs,
                       lhs=lhs: T.matmul(
                        Q4(bank, 512), lhs, rhs,
                        start=(hc == 0), stop=(hc == nt - 1)),
                       reads=xsres + m_wv, writes=[f"PSq{bank}"])
            for sub in range(n_sub):
                bank = VB[sub]
                vd, vdres = vch_sl(sub, dq * 512, (dq + 1) * 512)
                OP("scalar", lambda A, vd=vd, bank=bank: A.activation(
                    vd, Q4(bank, 512), AF.Copy),
                   reads=[f"PSq{bank}"], writes=vdres)

        c12_stats_part(0, 0, nt)
        c12_norm(0)
        for ch in range(c.n_ch):
            for dq in range(c.dm // 512):
                if ch + 1 < c.n_ch:
                    tpg = nt // (c.dm // 512)
                    c12_stats_part(ch + 1, dq * tpg, (dq + 1) * tpg)
                c12_vmms(ch, dq)
                dec_steps(5)
            for sub in range(n_sub):
                vrow, vrres = vch_sl(sub, 0, c.dm)
                dma("sync", v_hb[ch * cw + sub * 128:
                                 ch * cw + (sub + 1) * 128, :],
                    vrow, reads=vrres, writes=["vhb"])
            if ch + 1 < c.n_ch:
                c12_norm(ch + 1)
            dec_steps(6)

        wk_b1 = big1[:, 0:nt * c.dm]
        m_wk = pages("big1", 0, nt * c.dm)
        dma("sync", wk_b1.rearrange("p (m f) -> p m f", f=c.dm), wk_bf,
            reads=["wkb"] + m_wv, writes=m_wk)

        kslots = [((QA, "QA"), (QB, "QB")), ((K2A, "K2A"), (K2B, "K2B")),
                  ((K3A, "K3A"), (K3B, "K3B"))]

        def c3_xn_loads(ch):
            apar = ch % 2
            sl_f = slice(ch * cw, (ch + 1) * cw)
            for t in range(nt):
                xs, xsres = xn_area(t, apar)
                dma("sync", xs, xn_hb[t][:, sl_f], reads=["xnhb"],
                    writes=xsres)

        def c3_vch_loads(ch):
            for sub in range(n_sub):
                vrow, vrres = vch_sl(sub, 0, c.dm)
                dma("sync", vrow,
                    v_hb[ch * cw + sub * 128:ch * cw + (sub + 1) * 128, :],
                    reads=["vhb"], writes=vrres)

        def c3_front_pre(u):
            ch, tp = divmod(u, hp)
            par = u % 3
            sl_f = slice(ch * cw, (ch + 1) * cw)
            if tp == min(4, hp - 1) and ch + 1 < c.n_ch:
                c3_xn_loads(ch + 1)
            dma("sync", cos_s[par], cosk[tp][:, sl_f], writes=m_cos_s[par])
            dma("sync", sin_s[par], sink[tp][:, sl_f], writes=m_sin_s[par])

        def c3_kmm(u, half):
            ch, tp = divmod(u, hp)
            apar = ch % 2
            bank = half
            m = tp + half * hp
            wslab = wk_b1[:, m * c.dm:(m + 1) * c.dm]
            for hc in range(nt):
                xs, xsres = xn_area(hc, apar)
                OP("tensor", lambda T, hc=hc, wslab=wslab, bank=bank,
                   xnv=xs: T.matmul(
                    Q4(bank), wslab[:, hc * 128:(hc + 1) * 128],
                    xnv, start=(hc == 0), stop=(hc == nt - 1)),
                   reads=m_wk + xsres, writes=[f"PSq{bank}"])

        def c3_evac(u):
            par = u % 3
            (kA, kAres), (kB, kBres) = kslots[par]
            for (bank, dst, dres) in ((0, kA, kAres), (1, kB, kBres)):
                OP("scalar", lambda A, dst=dst, bank=bank: A.activation(
                    dst, Q4(bank), AF.Copy),
                   reads=[f"PSq{bank}"], writes=[dres])

        SQK2 = ldb_v
        m_SQK2 = ["stld"]

        def c3_back1(u):
            ch, tp = divmod(u, hp)
            par = u % 3
            sl_f = slice(ch * cw, (ch + 1) * cw)
            (kA, kAres), (kB, kBres) = kslots[par]
            rope_pair(kA, [kAres], QO, ["QO"], QO2, ["QO2"],
                      getB=kB, resB=[kBres],
                      cs=(cos_s[par], sin_s[par], m_cos_s[par],
                          m_sin_s[par]))
            for (kout, kres, mh) in ((QO, "QO", tp),
                                     (QO2, "QO2", tp + hp)):
                dma("sync", kT_h[mh][:, sl_f], kout, reads=[kres],
                    writes=["kTh"])
            for sub in range(n_sub):
                OP("tensor", lambda T, sub=sub:
                   T.transpose(psb[:, sub * 128:(sub + 1) * 128],
                               QO[:, sub * 128:(sub + 1) * 128],
                               ident_s[:, :]),
                   reads=["QO", "ident"], writes=["PSq7"])
            OP("scalar", lambda A: A.activation(SQK, psb[:, 0:cw],
                                                AF.Copy),
               reads=["PSq7"], writes=m_SQK)
            for sub in range(n_sub):
                OP("tensor", lambda T, sub=sub:
                   T.transpose(psb[:, sub * 128:(sub + 1) * 128],
                               QO2[:, sub * 128:(sub + 1) * 128],
                               ident_s[:, :]),
                   reads=["QO2", "ident"], writes=["PSq7"])
            OP("scalar", lambda A: A.activation(SQK2, psb[:, 0:cw],
                                                AF.Copy),
               reads=["PSq7"], writes=m_SQK2)
            OP("vector", lambda V: V.tensor_scalar_min(
                tf3[:, 0:cw], SQK, 0.0), reads=m_SQK, writes=["tf3"])
            OP("vector", lambda V: V.tensor_scalar_min(
                tf5[:, 0:cw], SQK2, 0.0), reads=m_SQK2, writes=["tf5"])
            OP("scalar", lambda A: A.activation(
                tf4[:, 0:cw], tf3[:, 0:cw], AF.Exp),
               reads=["tf3"], writes=["tf4"])
            OP("scalar", lambda A: A.activation(
                tf2[:, 0:cw], tf5[:, 0:cw], AF.Exp),
               reads=["tf5"], writes=["tf2"])
            OP("vector", lambda V: V.tensor_scalar_add(
                tf3[:, 0:cw], SQK, 1.0), reads=m_SQK, writes=["tf3"])
            OP("vector", lambda V: V.tensor_max(
                SQK, tf3[:, 0:cw], tf4[:, 0:cw]),
               reads=["tf3", "tf4"], writes=m_SQK)
            OP("vector", lambda V: V.tensor_scalar_add(
                tf5[:, 0:cw], SQK2, 1.0), reads=m_SQK2, writes=["tf5"])
            OP("vector", lambda V: V.tensor_max(
                SQK2, tf5[:, 0:cw], tf2[:, 0:cw]),
               reads=["tf5", "tf2"], writes=m_SQK2)

        def c3_back2(u):
            ch, tp = divmod(u, hp)
            for (kf, kfm, mh) in ((SQK, m_SQK, tp), (SQK2, m_SQK2, tp + hp)):
                for sub in range(n_sub):
                    lhs = kf[:, sub * 128:(sub + 1) * 128]
                    rhs, rres = vch_sl(sub, mh * 128, (mh + 1) * 128)
                    OP("tensor", lambda T, lhs=lhs, rhs=rhs, sub=sub:
                       T.matmul(MEMPS[:, 0:128], lhs, rhs,
                                start=(sub == 0), stop=False),
                       reads=kfm + rres, writes=["PSq6"])
                    OP("tensor", lambda T, lhs=lhs, sub=sub: T.matmul(
                        MEMPS[:, 128:129], lhs, ones_col[:, :],
                        start=False, stop=(sub == n_sub - 1)),
                       reads=kfm + ["ones"], writes=["PSq6"])
                msl = slice(mh * 129, (mh + 1) * 129)
                OP("vector", lambda V, msl=msl: V.tensor_add(
                    memory[:, msl], memory[:, msl], MEMPS[:, 0:129]),
                   reads=["PSq6"], writes=["memory"])
            row_norm_max(QO, ["QO"], mk2, "mk2", tp)
            row_norm_max(QO2, ["QO2"], mk2, "mk2", tp + hp)
            if tp == hp - 1 and ch + 1 < c.n_ch:
                c3_vch_loads(ch + 1)
            if (ch * hp + tp) % 2 == 0:
                dec_steps(1)

        c3_xn_loads(0)
        c3_vch_loads(0)
        n_u = c.n_ch * hp
        for u in range(n_u + 2):
            if 1 <= u <= n_u:
                c3_evac(u - 1)
            if u < n_u:
                c3_front_pre(u)
                c3_kmm(u, 0)
            if u >= 2:
                c3_back1(u - 2)
            if u < n_u:
                c3_kmm(u, 1)
            if u >= 2:
                c3_back2(u - 2)

        OP("vector", lambda V: V.reduce_max(sc1[0:1, 0:1], mq2[0:1, :],
                                            axis=mybir.AxisListType.X),
           reads=["mq2"], writes=["sc1"])
        OP("vector", lambda V: V.reduce_max(sc1[0:1, 1:2], mk2[0:1, :],
                                            axis=mybir.AxisListType.X),
           reads=["mk2"], writes=["sc1"])
        OP("vector", lambda V: V.tensor_mul(sc1[0:1, 2:3], sc1[0:1, 0:1],
                                            sc1[0:1, 1:2]),
           reads=["sc1"], writes=["sc1"])
        OP("scalar", lambda A: A.activation(sc1[0:1, 3:4], sc1[0:1, 2:3],
                                            AF.Ln), reads=["sc1"],
           writes=["sc1"])
        OP("scalar", lambda A: A.activation(sc1[0:1, 3:4], sc1[0:1, 3:4],
                                            AF.Exp, scale=0.5),
           reads=["sc1"], writes=["sc1"])
        OP("vector", lambda V: V.tensor_scalar_mul(sc1[0:1, 3:4],
                                                   sc1[0:1, 3:4],
                                                   -float(c.scale)),
           reads=["sc1"], writes=["sc1"])
        OP("tensor", lambda T: T.matmul(Q4(4, 1)[:, 0:1], ones_row[:, :],
                                        sc1[0:1, 3:4], start=True, stop=True),
           reads=["onesr", "sc1"], writes=["PSq4"])
        OP("vector", lambda V: V.tensor_copy(biasC[:, :], Q4(4, 1)[:, 0:1]),
           reads=["PSq4"], writes=["biasC"])

        qbuf = mid[:, 0:c.s_own]
        qf = mid[:, c.s_own:2 * c.s_own]
        pbs = [mid[:, 2 * c.s_own + i * 2 * cw:
                   2 * c.s_own + (i + 1) * 2 * cw] for i in range(3)]
        m_qbuf = pages("mid", 0, c.s_own, 1024)
        m_qf = pages("mid", c.s_own, 2 * c.s_own, 1024)
        m_pbs = [pages("mid", 2 * c.s_own + i * 2 * cw,
                       2 * c.s_own + (i + 1) * 2 * cw, 1024)
                 for i in range(3)]

        kbuf = big0[:, 0:c.s_full]
        vb0 = big0[:, c.s_full:2 * c.s_full]
        vb1 = big0[:, 2 * c.s_full:3 * c.s_full]
        m_kbuf = pages("big0", 0, c.s_full)
        m_vb = [pages("big0", c.s_full, 2 * c.s_full),
                pages("big0", 2 * c.s_full, 3 * c.s_full)]

        v_re = v_hb.rearrange("(cb p) d -> p cb d", p=128)
        n_cb = c.s_full // 128

        def load_vhead(h, vb, vres):
            npiece = 8 if n_cb % 8 == 0 else 1
            step = n_cb // npiece
            for pc in range(npiece):
                src = v_re[:, pc * step:(pc + 1) * step,
                           h * 128:(h + 1) * 128]
                dst = vb[:, pc * step * 128:(pc + 1) * step * 128]
                dst3 = dst.rearrange("p (cb d) -> p cb d", d=128)
                dma("sync", dst3, src, reads=["vhb"], writes=vres)

        load_vhead(0, vb0, m_vb[0])
        for h in range(nt):
            vcur, m_vcur = (vb0, m_vb[0]) if h % 2 == 0 else (vb1, m_vb[1])
            if h + 1 < nt:
                vnxt, m_vnxt = (vb1, m_vb[1]) if h % 2 == 0 else \
                    (vb0, m_vb[0])
                load_vhead(h + 1, vnxt, m_vnxt)
            for pc in range(c.s_full // 2048):
                dma("sync", kbuf[:, pc * 2048:(pc + 1) * 2048],
                    kT_h[h][:, pc * 2048:(pc + 1) * 2048], reads=["kTh"],
                    writes=pages("big0", pc * 2048, (pc + 1) * 2048))
            for pc in range(c.s_own // 1024):
                dma("sync", qbuf[:, pc * 1024:(pc + 1) * 1024],
                    qT_h[h][:, pc * 1024:(pc + 1) * 1024], reads=["qTh"],
                    writes=pages("mid", pc * 1024, (pc + 1) * 1024, 1024))
            for s in range(c.n_oc):
                qsl = qbuf[:, s * cw:(s + 1) * cw]
                m_qs = pages("mid", s * cw, (s + 1) * cw, 1024)
                OP("vector", lambda V, qsl=qsl: V.tensor_scalar_min(
                    tf1[:, 0:cw], qsl, 0.0), reads=m_qs, writes=["tf1"])
                OP("scalar", lambda A: A.activation(tf2[:, 0:cw],
                                                    tf1[:, 0:cw], AF.Exp),
                   reads=["tf1"], writes=["tf2"])
                OP("vector", lambda V, qsl=qsl: V.tensor_scalar_add(
                    tf1[:, 0:cw], qsl, 1.0), reads=m_qs, writes=["tf1"])
                OP("vector", lambda V, s=s: V.tensor_max(
                    qf[:, s * cw:(s + 1) * cw], tf1[:, 0:cw], tf2[:, 0:cw]),
                   reads=["tf1", "tf2"],
                   writes=pages("mid", c.s_own + s * cw,
                                c.s_own + (s + 1) * cw, 1024))
            OP("vector", lambda V, h=h: V.tensor_copy(
                memb[:, 0:129], memory[:, h * 129:(h + 1) * 129]),
               reads=["memory"], writes=["memb"])
            OP("tensor", lambda T, h=h: T.matmul(
                Q4(4, 128)[0:1, :], g_row[0:1, h:h + 1], ones_row[:, :],
                start=True, stop=True),
               reads=["grow", "onesr"], writes=["PSq4"])
            OP("vector", lambda V: V.tensor_copy(gcol[0:1, :],
                                                 Q4(4, 128)[0:1, :]),
               reads=["PSq4"], writes=["gcol"])
            OP("tensor", lambda T, h=h: T.matmul(
                Q4(4, 128)[0:1, :], og_row[0:1, h:h + 1], ones_row[:, :],
                start=True, stop=True),
               reads=["ogrow", "onesr"], writes=["PSq4"])
            OP("vector", lambda V: V.tensor_copy(ogcol[0:1, :],
                                                 Q4(4, 128)[0:1, :]),
               reads=["PSq4"], writes=["ogcol"])
            def mk_qb(qb, vcur, m_vcur):
                qsl = slice(qb * cw, (qb + 1) * cw)
                m_q = pages("mid", qb * cw, (qb + 1) * cw, 1024)

                def emit_sc(g):
                    par = g % 3
                    for half in range(2):
                        kc = 2 * g + half
                        bank = 2 * par + half
                        m_k = pages("big0", kc * 128, (kc + 1) * 128)
                        OP("tensor", lambda T, kc=kc, bank=bank:
                           T.matmul(Q4(bank),
                                    kbuf[:, kc * 128:(kc + 1) * 128],
                                    qbuf[:, qsl], start=True, stop=True),
                           reads=m_k + m_q, writes=[f"PSq{bank}"])

                def emit_exp(g):
                    par = g % 3
                    pbx, m_pbx = pbs[par], m_pbs[par]
                    pbx3 = pbx.rearrange("p (b x) -> p b x", x=cw)
                    OP("scalar", lambda A, par=par, pbx3=pbx3: A.activation(
                        pbx3, QPAIR(par), AF.Exp, bias=biasC[:, 0:1],
                        scale=c.scale),
                       reads=[f"PSq{2 * par}", f"PSq{2 * par + 1}", "biasC"],
                       writes=m_pbx)

                def emit_pv(g):
                    par = g % 3
                    pbx, m_pbx = pbs[par], m_pbs[par]
                    for half in range(2):
                        kc = 2 * g + half
                        OP("tensor", lambda T, kc=kc, half=half, pbx=pbx:
                           T.matmul(
                            PVPS, vcur[:, kc * 128:(kc + 1) * 128],
                            pbx[:, half * cw:(half + 1) * cw],
                            start=(kc == 0), stop=(kc == c.n_kc - 1)),
                           reads=m_vcur + m_pbx, writes=["PSq6"])
                        OP("tensor", lambda T, kc=kc, half=half, pbx=pbx:
                           T.matmul(DEN, ones_col[:, :],
                                    pbx[:, half * cw:(half + 1) * cw],
                                    start=(kc == 0),
                                    stop=(kc == c.n_kc - 1)),
                           reads=["ones"] + m_pbx, writes=["PSq7"])

                def emit_lin():
                    m_qfq = pages("mid", c.s_own + qb * cw,
                                  c.s_own + (qb + 1) * cw, 1024)
                    OP("tensor", lambda T: T.matmul(
                        PVPS, memb[:, 0:128], qf[:, qsl],
                        start=True, stop=True),
                       reads=["memb"] + m_qfq, writes=["PSq6"])
                    OP("scalar", lambda A: A.activation(tf3[:, 0:cw], PVPS,
                                                        AF.Copy),
                       reads=["PSq6"], writes=["tf3"])
                    OP("tensor", lambda T: T.matmul(
                        DEN, memb[:, 128:129], qf[:, qsl],
                        start=True, stop=True),
                       reads=["memb"] + m_qfq, writes=["PSq7"])
                    OP("vector", lambda V: V.tensor_copy(
                        R1[0:1, qsl], DEN), reads=["PSq7"], writes=["R1"])
                    OP("vector", lambda V: V.reciprocal(
                        R1[0:1, qsl], R1[0:1, qsl]), reads=["R1"],
                       writes=["R1"])

                def emit_epi():
                    OP("vector", lambda V: V.tensor_copy(
                        R0[0:1, qsl], DEN), reads=["PSq7"], writes=["R0"])
                    OP("scalar", lambda A: A.activation(
                        tf4[:, 0:cw], PVPS, AF.Copy),
                       reads=["PSq6"], writes=["tf4"])
                    OP("vector", lambda V: V.reciprocal(
                        R0[0:1, qsl], R0[0:1, qsl]), reads=["R0"],
                       writes=["R0"])
                    OP("tensor", lambda T: T.matmul(
                        PVPS, gcol[0:1, :], R1[0:1, qsl],
                        start=True, stop=True),
                       reads=["gcol", "R1"], writes=["PSq6"])
                    OP("tensor", lambda T: T.matmul(
                        Q4(7), ogcol[0:1, :], R0[0:1, qsl],
                        start=True, stop=True),
                       reads=["ogcol", "R0"], writes=["PSq7"])
                    lo = h * c.s_own + qb * cw
                    cmb = big1[:, lo:lo + cw]
                    cres = pages("big1", lo, lo + cw)
                    OP("vector", lambda V: V.tensor_mul(
                        tf3[:, 0:cw], tf3[:, 0:cw], PVPS),
                       reads=["tf3", "PSq6"], writes=["tf3"])
                    OP("vector", lambda V: V.tensor_mul(
                        tf4[:, 0:cw], tf4[:, 0:cw], Q4(7)),
                       reads=["tf4", "PSq7"], writes=["tf4"])
                    OP("vector", lambda V, cmb=cmb, cres=cres: V.tensor_add(
                        cmb, tf4[:, 0:cw], tf3[:, 0:cw]),
                       reads=["tf3", "tf4"], writes=cres)

                return emit_sc, emit_exp, emit_pv, emit_lin, emit_epi

            ng = c.n_kc // 2
            prev_epi = None
            for qb in range(c.n_oc):
                e_sc, e_exp, e_pv, e_lin, e_epi = mk_qb(qb, vcur, m_vcur)
                e_sc(0)
                e_sc(1)
                e_exp(0)
                if prev_epi is not None:
                    prev_epi()
                e_lin()
                for g in range(2, ng + 2):
                    if g < ng:
                        e_sc(g)
                        if g in (2, 10, 18):
                            dec_steps(4)
                        elif g == 25:
                            dec_steps(2)
                    if g - 1 < ng:
                        e_exp(g - 1)
                    e_pv(g - 2)
                prev_epi = e_epi
            prev_epi()
            dec_steps(4)

        def park_a(s, ob):
            lo = (s % 2) * nt * cw + ob * cw
            return big0[:, lo:lo + cw], pages("big0", lo, lo + cw)

        def park_r2(t):
            lo = 2 * nt * cw + t * cw
            return big0[:, lo:lo + cw], pages("big0", lo, lo + cw)

        def p4_front(s):
            for ob in range(nt):
                g = s * nt + ob
                wb, wres = (wbuf0, "wb0") if g % 2 == 0 else (wbuf1, "wb1")
                if g == 0:
                    wload(wo_bf, 0, wbuf0, "wb0", "wob")
                if g + 1 < c.n_oc * nt:
                    nwb, nwres = ((wbuf1, "wb1") if g % 2 == 0 else
                                  (wbuf0, "wb0"))
                    wload(wo_bf, (g + 1) % nt, nwb, nwres, "wob")
                bank = ob % 2
                for ic in range(nt):
                    lo = ic * c.s_own + s * cw
                    cap = big1[:, lo:lo + cw]
                    cres = pages("big1", lo, lo + cw)
                    OP("tensor", lambda T, ic=ic, bank=bank, wb=wb, cap=cap:
                       T.matmul(Q4(bank), wb[:, ic * 128:(ic + 1) * 128],
                                cap, start=(ic == 0), stop=(ic == nt - 1)),
                       reads=[wres] + cres, writes=[f"PSq{bank}"])
                pk, pkres = park_a(s, ob)
                OP("scalar", lambda A, pk=pk, bank=bank: A.activation(
                    pk, Q4(bank), AF.Copy),
                   reads=[f"PSq{bank}"], writes=pkres)
                sq, sqres = ((stat_sq[:, 0:cw], "stsq") if ob % 2 == 0
                             else (sqb, "tf2"))
                OP("scalar", lambda A, pk=pk, sq=sq: A.activation(
                    sq, pk, AF.Square), reads=pkres, writes=[sqres])
                OP("tensor", lambda T, pk=pk, ob=ob: T.matmul(
                    Q4(2)[0:1, :], ones_col[:, :], pk,
                    start=(ob == 0), stop=(ob == nt - 1)),
                   reads=["ones"] + pkres, writes=["PSq2"])
                OP("tensor", lambda T, sq=sq, ob=ob: T.matmul(
                    Q4(3)[0:1, :], ones_col[:, :], sq,
                    start=(ob == 0), stop=(ob == nt - 1)),
                   reads=["ones", sqres], writes=["PSq3"])
                dec_steps(1)

        def p4_mid1(s):
            stats_rows(par=0, off=s * cw)
            ln_finalize(cw, par=0, off=s * cw)
            bcast_rows(cw, off=s * cw, banks=(6, 7))

        def p4_back1(s):
            sl = slice(s * cw, (s + 1) * cw)
            dma("sync", tf4[:, 0:cw], xownT[0][:, sl], writes=["tf4"])
            for t in range(nt):
                stg, stres, rsd, rres = (
                    (tf3, "tf3", tf4, "tf4") if t % 2 == 0
                    else (stat_ld, "stld", tf5, "tf5"))
                pk, pkres = park_a(s, t)
                if t + 1 < nt:
                    rsdn, rresn = ((tf4, "tf4") if (t + 1) % 2 == 0
                                   else (tf5, "tf5"))
                    dma("sync", rsdn[:, 0:cw], xownT[t + 1][:, sl],
                        writes=[rresn])
                OP("vector", lambda V, pk=pk, stg=stg: V.tensor_mul(
                    stg[:, 0:cw], pk, Q4(6)),
                   reads=pkres + ["PSq6"], writes=[stres])
                OP("vector", lambda V, stg=stg: V.tensor_sub(
                    stg[:, 0:cw], stg[:, 0:cw], Q4(7)),
                   reads=[stres, "PSq7"], writes=[stres])
                OP("vector", lambda V, stg=stg, rsd=rsd: V.tensor_add(
                    stg[:, 0:cw], stg[:, 0:cw], rsd[:, 0:cw]),
                   reads=[stres, rres], writes=[stres])
                dma("sync", r2_hb[t][:, sl], stg[:, 0:cw], reads=[stres],
                    writes=["r2hb"])
                pr, prres = park_r2(t)
                OP("scalar", lambda A, stg=stg, pr=pr: A.activation(
                    pr, stg[:, 0:cw], AF.Copy),
                   reads=[stres], writes=prres)
                sq, sqres = ((stat_sq[:, 0:cw], "stsq") if t % 2 == 0
                             else (sqb, "tf2"))
                OP("scalar", lambda A, stg=stg, sq=sq: A.activation(
                    sq, stg[:, 0:cw], AF.Square),
                   reads=[stres], writes=[sqres])
                OP("tensor", lambda T, pr=pr, t=t: T.matmul(
                    Q4(4)[0:1, :], ones_col[:, :], pr,
                    start=(t == 0), stop=(t == nt - 1)),
                   reads=["ones"] + prres, writes=["PSq4"])
                OP("tensor", lambda T, sq=sq, t=t: T.matmul(
                    Q4(5)[0:1, :], ones_col[:, :], sq[:, 0:cw],
                    start=(t == 0), stop=(t == nt - 1)),
                   reads=["ones", sqres], writes=["PSq5"])
                dec_steps(1)
            OP("vector", lambda V, s=s: V.tensor_copy(
                R0[0:1, s * cw:(s + 1) * cw], Q4(4)[0:1, :]),
               reads=["PSq4"], writes=["R0"])
            OP("vector", lambda V, s=s: V.tensor_copy(
                R1[0:1, s * cw:(s + 1) * cw], Q4(5)[0:1, :]),
               reads=["PSq5"], writes=["R1"])

        def p4_mid2(s):
            ln_finalize(cw, par=0, off=s * cw)
            bcast_rows(cw, off=s * cw, banks=(6, 7))

        def p4_back2(s):
            for t in range(nt):
                stg, stres = ((tf3, "tf3") if t % 2 == 0 else (tf2, "tf2"))
                pr, prres = park_r2(t)
                OP("vector", lambda V, pr=pr, stg=stg: V.tensor_mul(
                    stg[:, 0:cw], pr, Q4(6)),
                   reads=prres + ["PSq6"], writes=[stres])
                lo = t * c.s_own + s * cw
                xn2 = big1[:, lo:lo + cw]
                x2res = pages("big1", lo, lo + cw)
                OP("vector", lambda V, xn2=xn2, stg=stg: V.tensor_sub(
                    xn2, stg[:, 0:cw], Q4(7)),
                   reads=[stres, "PSq7"], writes=x2res)
            dec_steps(2)

        p4_front(0)
        for s in range(c.n_oc):
            if s >= 1:
                p4_mid2(s - 1)
                p4_back2(s - 1)
            p4_mid1(s)
            if s + 1 < c.n_oc:
                p4_front(s + 1)
            p4_back1(s)
        p4_mid2(c.n_oc - 1)
        p4_back2(c.n_oc - 1)

        def xn2_t(t, s):
            lo = t * c.s_own + s * cw
            return big1[:, lo:lo + cw], pages("big1", lo, lo + cw)

        hbufs = [(big0[:, ic * cw:(ic + 1) * cw],
                  pages("big0", ic * cw, (ic + 1) * cw)) for ic in range(it)]
        fstg_lo = it * cw
        fstg = big0[:, fstg_lo:fstg_lo + cw]
        m_fstg = pages("big0", fstg_lo, fstg_lo + cw)
        sgb = big0[:, fstg_lo + cw:fstg_lo + 2 * cw]
        m_sgb = pages("big0", fstg_lo + cw, fstg_lo + 2 * cw)

        for sbi in range(c.n_oc):
            sl = slice(sbi * cw, (sbi + 1) * cw)
            for ic in range(it):
                wg_s = wg_hb[:, ic * nt:(ic + 1) * nt, :].rearrange(
                    "p t f -> p (t f)")
                wu_s = wu_hb[:, ic * nt:(ic + 1) * nt, :].rearrange(
                    "p t f -> p (t f)")
                dma("sync", wbuf0[:, :], wg_s, reads=["dec_o"],
                    writes=["wb0"])
                dma("sync", wbuf1[:, :], wu_s, reads=["dec_o"],
                    writes=["wb1"])
                for hc in range(nt):
                    x2, x2res = xn2_t(hc, sbi)
                    OP("tensor", lambda T, hc=hc, x2=x2: T.matmul(
                        Q4(0), wbuf0[:, hc * 128:(hc + 1) * 128], x2,
                        start=(hc == 0), stop=(hc == nt - 1)),
                       reads=["wb0"] + x2res, writes=["PSq0"])
                for hc in range(nt):
                    x2, x2res = xn2_t(hc, sbi)
                    OP("tensor", lambda T, hc=hc, x2=x2: T.matmul(
                        Q4(1), wbuf1[:, hc * 128:(hc + 1) * 128], x2,
                        start=(hc == 0), stop=(hc == nt - 1)),
                       reads=["wb1"] + x2res, writes=["PSq1"])
                OP("scalar", lambda A: A.activation(sgb, Q4(0), AF.Silu),
                   reads=["PSq0"], writes=m_sgb)
                hb, hres = hbufs[ic]
                OP("vector", lambda V, hb=hb: V.tensor_mul(hb, sgb, Q4(1)),
                   reads=m_sgb + ["PSq1"], writes=hres)
                if sbi == 0:
                    dec_steps(4)
            if sbi == 0:
                dec_flush()
            for ob in range(nt):
                bank = 2 + ob % 2
                for icg in range(0, it, 16):
                    ilen = min(16, it - icg)
                    wd_s = wd_hb[:, ob * it + icg:ob * it + icg + ilen, :]\
                        .rearrange("p t f -> p (t f)")
                    wb, wres = ((wbuf0, "wb0") if (icg // 16) % 2 == 0 else
                                (wbuf1, "wb1"))
                    dma("sync", wb[:, 0:ilen * 128], wd_s, reads=["dec_o"],
                        writes=[wres])
                    for i2 in range(ilen):
                        ic = icg + i2
                        hb, hres = hbufs[ic]
                        OP("tensor", lambda T, i2=i2, ic=ic, hb=hb, wb=wb,
                           bank=bank: T.matmul(
                            Q4(bank), wb[:, i2 * 128:(i2 + 1) * 128], hb,
                            start=(ic == 0), stop=(ic == it - 1)),
                           reads=[wres] + hres, writes=[f"PSq{bank}"])
                OP("scalar", lambda A, bank=bank: A.activation(
                    fstg, Q4(bank), AF.Copy),
                   reads=[f"PSq{bank}"], writes=m_fstg)
                dma("sync", f_hb[ob][:, sl], fstg, reads=m_fstg,
                    writes=["fhb"])
                sq, sqres = ((stat_sq[:, 0:cw], "stsq") if ob % 2 == 0
                             else (sqb, "tf2"))
                OP("scalar", lambda A, sq=sq: A.activation(
                    sq, fstg, AF.Square), reads=m_fstg, writes=[sqres])
                OP("tensor", lambda T, ob=ob: T.matmul(
                    Q4(4)[0:1, :], ones_col[:, :], fstg,
                    start=(ob == 0), stop=(ob == nt - 1)),
                   reads=["ones"] + m_fstg, writes=["PSq4"])
                OP("tensor", lambda T, sq=sq, ob=ob: T.matmul(
                    Q4(5)[0:1, :], ones_col[:, :], sq,
                    start=(ob == 0), stop=(ob == nt - 1)),
                   reads=["ones", sqres], writes=["PSq5"])
            OP("vector", lambda V, sbi=sbi: V.tensor_copy(
                R0[0:1, sbi * cw:(sbi + 1) * cw], Q4(4)[0:1, :]),
               reads=["PSq4"], writes=["R0"])
            OP("vector", lambda V, sbi=sbi: V.tensor_copy(
                R1[0:1, sbi * cw:(sbi + 1) * cw], Q4(5)[0:1, :]),
               reads=["PSq5"], writes=["R1"])

        ln_finalize(c.s_own)
        def p6_bufs(t):
            return ((ldb_v, "stld", tf3, "tf3", tf4, "tf4") if t % 2 == 0
                    else (tf1b, "tf1", tf2, "tf2", tf5, "tf5"))

        for s in range(c.n_oc):
            sl = slice(s * cw, (s + 1) * cw)
            bcast_rows(cw, off=s * cw)
            ld0, ldres0, _, _, rsd0, rres0 = p6_bufs(0)
            dma("sync", ld0, f_hb[0][:, sl], reads=["fhb"], writes=[ldres0])
            dma("sync", rsd0[:, 0:cw], r2_hb[0][:, sl],
                reads=["r2hb"], writes=[rres0])
            for t in range(nt):
                ld, ldres, stg, stres, rsd, rres = p6_bufs(t)
                if t + 1 < nt:
                    ldn, ldresn, _, _, rsdn, rresn = p6_bufs(t + 1)
                    dma("sync", ldn, f_hb[t + 1][:, sl], reads=["fhb"],
                        writes=[ldresn])
                    dma("sync", rsdn[:, 0:cw], r2_hb[t + 1][:, sl],
                        reads=["r2hb"], writes=[rresn])
                OP("vector", lambda V, ld=ld, stg=stg: V.tensor_mul(
                    stg[:, 0:cw], ld, Q4(4)),
                   reads=[ldres, "PSq4"], writes=[stres])
                OP("vector", lambda V, stg=stg: V.tensor_sub(
                    stg[:, 0:cw], stg[:, 0:cw], Q4(5)),
                   reads=[stres, "PSq5"], writes=[stres])
                OP("vector", lambda V, stg=stg, rsd=rsd: V.tensor_add(
                    stg[:, 0:cw], stg[:, 0:cw], rsd[:, 0:cw]),
                   reads=[stres, rres], writes=[stres])
                dma("sync", out[t][:, sl], stg[:, 0:cw], reads=[stres],
                    writes=["outd"])

        for eng in ENGINES:
            for q in ("dS", "dG", "dA"):
                for k in range(N_DMA_SEMS):
                    semname = f"{q}{k}"
                    if semname in em.counts:
                        em.wait(eng, (semname, em.counts[semname]))

        with nc.Block() as block:
            em.run(block)

    return nc


def _stage_inputs(c, inputs):
    x = np.asarray(inputs["L0_attn_hidden_states"])
    B = x.shape[0]
    lut = np.asarray(inputs["L0_ffn_lut"]).astype(np.float32)

    wqT = np.ascontiguousarray(np.asarray(inputs["L0_attn_Wq"]).T)
    wkT = np.ascontiguousarray(np.asarray(inputs["L0_attn_Wk"]).T)
    wvT = np.ascontiguousarray(np.asarray(inputs["L0_attn_Wv"]).T)
    woT = np.ascontiguousarray(np.asarray(inputs["L0_attn_Wo"]).T)
    gate = np.ascontiguousarray(
        np.asarray(inputs["L0_attn_gate"]).reshape(1, c.nt).astype(np.float32))

    inv = (1.0 / (10000.0 ** (np.arange(0, c.dm, 2, dtype=np.float64)
                              / c.dm))).astype(np.float64)
    t = np.arange(c.s_full, dtype=np.float64)
    fr = np.outer(inv, t)
    cosk = np.ascontiguousarray(
        np.cos(fr).astype(np.float32).reshape(c.hp, 128, c.s_full))
    sink = np.ascontiguousarray(
        np.sin(fr).astype(np.float32).reshape(c.hp, 128, c.s_full))

    gi = np.asarray(inputs["L0_ffn_gate_idx"]).astype(np.int32)
    ui = np.asarray(inputs["L0_ffn_up_idx"]).astype(np.int32)
    di = np.asarray(inputs["L0_ffn_down_idx"]).astype(np.int32)
    G4 = gi.reshape(c.it, 128, c.nt, 128)
    gidx = np.ascontiguousarray(
        G4.transpose(3, 0, 2, 1).reshape(128, c.ngu, 128))
    U4 = ui.reshape(c.it, 128, c.nt, 128)
    uidx = np.ascontiguousarray(
        U4.transpose(3, 0, 2, 1).reshape(128, c.ngu, 128))
    D4 = di.reshape(c.nt, 128, c.it, 128)
    didx = np.ascontiguousarray(
        D4.transpose(3, 0, 2, 1).reshape(128, c.nd, 128))

    n_cores = 8
    q_per_b = n_cores // B
    in_maps = []
    for core in range(n_cores):
        b = core // q_per_b
        q = core % q_per_b
        xTb = np.ascontiguousarray(x[b].T).reshape(c.nt, 128, c.s_full)
        xown = np.ascontiguousarray(
            x[b, q * c.s_own:(q + 1) * c.s_own, :].T).reshape(
            c.nt, 128, c.s_own)
        cq = np.ascontiguousarray(
            cosk[:, :, q * c.s_own:(q + 1) * c.s_own])
        sq = np.ascontiguousarray(
            sink[:, :, q * c.s_own:(q + 1) * c.s_own])
        in_maps.append({
            "xT": xTb, "xownT": xown, "wqT": wqT, "wkT": wkT, "wvT": wvT,
            "woT": woT, "gate": gate, "cosq": cq, "sinq": sq,
            "cosk": cosk, "sink": sink, "gidx": gidx, "uidx": uidx,
            "didx": didx,
        })
    return in_maps, lut


def kernel(**inputs):
    c = Cfg()
    in_maps, lut = _stage_inputs(c, inputs)
    nc = build_program(c, [float(v) for v in lut])
    res = run_bass_kernel_spmd(nc, in_maps, core_ids=list(range(8)))
    x = np.asarray(inputs["L0_attn_hidden_states"])
    B, S, H = x.shape
    out = np.empty((B, S, H), dtype=np.float32)
    q_per_b = 8 // B
    for core in range(8):
        b = core // q_per_b
        q = core % q_per_b
        o = res.results[core]["outT"].reshape(c.dm, c.s_own)
        out[b, q * c.s_own:(q + 1) * c.s_own, :] = o.T
    return out
